# version 22
# baseline (speedup 1.0000x reference)
import os
import numpy as np
from contextlib import ExitStack

import concourse.bass as bass
import concourse.mybir as mybir
from concourse.bass_utils import run_bass_kernel_spmd

F32 = mybir.dt.float32
BF16 = mybir.dt.bfloat16
AF = mybir.ActivationFunctionType
ALU = mybir.AluOpType
AX = mybir.AxisListType

T = 4096
C = 1024
TH = 2048
TC = 512
NE = 16
NCH = 8

ENGS = ("tensor", "vector", "scalar", "gpsimd", "sync")


class Builder:
    def __init__(self, nc, stack):
        self.nc = nc
        self.stack = stack
        self.ops = {e: [] for e in ENGS}
        self.cnt = {}
        self.name_of = {}

    def sem(self, name):
        h = self.stack.enter_context(self.nc.semaphore(name))
        self.cnt[name] = 0
        self.name_of[id(h)] = name
        return h

    def n(self, sem):
        return self.cnt[self.name_of[id(sem)]]

    def emit(self, eng, fn, waits=(), incs=()):
        real = [(s, k) for (s, k) in incs if not (isinstance(s, tuple) and s[0] == "REL")]
        rels = [s[1] for (s, k) in incs if isinstance(s, tuple) and s[0] == "REL"]
        assert len(real) <= 1, "at most one sync update per instruction"
        for s, k in real:
            self.cnt[self.name_of[id(s)]] += k
        if rels:
            assert real, "bank release needs a real inc on the same instruction"
            rs, _ = real[0]
            tok = (rs, self.n(rs))
            for bi in rels:
                self.bank_tok[bi] = tok
        waits = tuple(w for w in waits if w is not None)
        real = tuple(real)

        def run(e):
            for s, v in waits:
                if v > 0:
                    e.wait_ge(s, v)
            ins = fn(e)
            for s, k in real:
                ins = ins.then_inc(s, k)
        self.ops[eng].append(run)

    def finish(self, block):
        for eng in ENGS:
            fns = list(self.ops[eng])
            def make(fns):
                def body(e):
                    for f in fns:
                        f(e)
                return body
            getattr(block, eng)(make(fns))


def build_program():
    stage = os.environ.get("KSTAGE", "full")
    ser = os.environ.get("KSER", "")
    nc = bass.Bass("TRN2", target_bir_lowering=False)

    xT_d = nc.dram_tensor("xT", [C, T], F32, kind="ExternalInput").ap()
    wat_d = nc.dram_tensor("wat", [C, 3 * C], F32, kind="ExternalInput").ap()
    batqk_d = nc.dram_tensor("batqk", [128, 16], F32, kind="ExternalInput").ap()
    bav_d = nc.dram_tensor("bav", [1, C], F32, kind="ExternalInput").ap()
    wproj_d = nc.dram_tensor("wproj", [C, C], F32, kind="ExternalInput").ap()
    bproj_d = nc.dram_tensor("bproj", [128, 8], F32, kind="ExternalInput").ap()
    gwT_d = nc.dram_tensor("gwT", [C, NE], F32, kind="ExternalInput").ap()
    weg_d = nc.dram_tensor("weg", [NE, C, 512], F32, kind="ExternalInput").ap()
    weu_d = nc.dram_tensor("weu", [NE, C, 512], F32, kind="ExternalInput").ap()
    wed_d = nc.dram_tensor("wed", [NE, 512, C], F32, kind="ExternalInput").ap()
    wsg_d = nc.dram_tensor("wsg", [C, C], F32, kind="ExternalInput").ap()
    wsu_d = nc.dram_tensor("wsu", [C, C], F32, kind="ExternalInput").ap()
    wsd_d = nc.dram_tensor("wsd", [C, C], F32, kind="ExternalInput").ap()
    cmask_d = nc.dram_tensor("cmask", [4 * 128, 512], F32, kind="ExternalInput").ap()
    idn_d = nc.dram_tensor("idn", [128, 128], F32, kind="ExternalInput").ap()
    out_d = nc.dram_tensor("out", [T, C], F32, kind="ExternalOutput").ap()

    dump = os.environ.get("KDUMP", "0") == "1"
    dkind = dict(kind="ExternalOutput") if dump else {}
    park_d = nc.dram_tensor("park", [C, T], F32, **dkind).ap()
    qsp_d = nc.dram_tensor("qsp", [C, T], BF16, **dkind).ap()
    vsp_d = nc.dram_tensor("vsp", [4, T, 256], BF16, **dkind).ap()
    combD_d = nc.dram_tensor("combD", [NE, T], F32).ap()

    stack = ExitStack()
    with stack:
        b = Builder(nc, stack)
        sb = lambda name, shape, dt=F32: stack.enter_context(nc.sbuf_tensor(name, shape, dt))

        wbufA = sb("wbufA", [128, 12288], BF16)
        wbufB = sb("wbufB", [128, 12288], BF16)
        idn_f = sb("idn_f", [128, 128], F32)
        idn_b = sb("idn_b", [128, 128], BF16)
        ones_b = sb("ones_b", [128, 1], BF16)
        ones128 = sb("ones128", [128, 128], BF16)
        epsb = sb("epsb", [128, 1], F32)
        batqk_sb = sb("batqk_sb", [128, 16], F32)
        bproj_sb = sb("bproj_sb", [128, 8], F32)
        gwT_sb = sb("gwT_sb", [128, NCH * NE], BF16)

        PA = stack.enter_context(nc.psum_tensor("PA", [128, 2048], F32))
        PB = stack.enter_context(nc.psum_tensor("PB", [128, 2048], F32))
        PFULL = [PA, PB]
        P = [PA[:, i * 512:(i + 1) * 512] for i in range(4)] + \
            [PB[:, i * 512:(i + 1) * 512] for i in range(4)]

        b.bank_tok = [None] * 8
        pe = b.sem("pe")
        ve = b.sem("ve")
        ac = b.sem("ac")
        po = b.sem("po")
        dwA = b.sem("dwA")
        dwB = b.sem("dwB")
        dini = b.sem("dini")
        dsy = b.sem("dsy")
        dsc = b.sem("dsc")
        dbc = [b.sem(f"dbc{i}") for i in range(4)]
        dln = [b.sem(f"dln{i}") for i in range(2)]
        dxc = [b.sem(f"dxc{i}") for i in range(2)]
        dqt = [b.sem(f"dqt{i}") for i in range(2)]
        dvc = [b.sem(f"dvc{i}") for i in range(6)]
        dxg = [b.sem(f"dxg{i}") for i in range(2)]
        dxl = [b.sem(f"dxl{i}") for i in range(4)]
        dvs = [b.sem(f"dvs{i}") for i in range(2)]
        dqs = [b.sem(f"dqs{i}") for i in range(2)]
        dxw = b.sem("dxw")

        def bank_w(i):
            return b.bank_tok[i]

        def bank_rel(i):
            return (("REL", i), 0)

        def allsync():
            return [(pe, b.n(pe)), (ve, b.n(ve)), (ac, b.n(ac)), (po, b.n(po))]

        FBUF = {}
        NAMEC = [0]

        b.emit("sync", lambda e: e.dma_start(out=idn_f[:], in_=idn_d[:]), incs=[(dsy, 16)])
        b.emit("sync", lambda e: e.dma_start(out=batqk_sb[:], in_=batqk_d[:]), incs=[(dsy, 16)])
        b.emit("sync", lambda e: e.dma_start(out=bproj_sb[:], in_=bproj_d[:]), incs=[(dsy, 16)])
        b.emit("gpsimd", lambda e: e.dma_start(out=idn_b[:], in_=idn_d[:]), incs=[(dini, 16)])
        b.emit("gpsimd", lambda e: e.dma_start(
            out=gwT_sb[:].rearrange("p (n f) -> p n f", n=NCH),
            in_=gwT_d[:].rearrange("(n p) f -> p n f", p=128)), incs=[(dini, 16)])
        b.emit("vector", lambda e: e.memset(ones_b[:], 1.0), incs=[(ve, 1)])
        b.emit("vector", lambda e: e.memset(ones128[:], 1.0), incs=[(ve, 1)])
        b.emit("vector", lambda e: e.memset(epsb[:], 1e-5), incs=[(ve, 1)])
        init_ready = [(dsy, b.n(dsy)), (dini, b.n(dini)), (ve, b.n(ve))]

        LNB = (6, 7)

        def ln_sums(x_view, slot, x_ready):
            xbf = FBUF["xbfc"]; xsq = FBUF["xsqc"]
            pe_now = b.n(pe)
            for c in range(NCH):
                xb = xbf[:, c * TC:(c + 1) * TC]
                xq = xsq[:, c * TC:(c + 1) * TC]
                b.emit("scalar", lambda e, c=c, xb=xb: e.copy(xb, x_view(c)),
                       waits=(list(x_ready) + [(pe, pe_now)]) if c == 0 else [], incs=[(ac, 1)])
                b.emit("scalar", lambda e, c=c, xq=xq: e.activation(xq, x_view(c), AF.Square),
                       incs=[(ac, 1)])
            ac_casts = b.n(ac)
            wl = [(ve, b.n(ve)), (ac, ac_casts), bank_w(LNB[0]), bank_w(LNB[1])]
            for c in range(NCH):
                xb = xbf[:, c * TC:(c + 1) * TC]
                xq = xsq[:, c * TC:(c + 1) * TC]
                st, sp = (c == 0), (c == NCH - 1)
                b.emit("tensor", lambda e, st=st, sp=sp, xb=xb: e.matmul(
                    P[LNB[0]][:], ones128[:], xb, start=st, stop=sp),
                    waits=wl if c == 0 else [])
                b.emit("tensor", lambda e, st=st, sp=sp, xq=xq: e.matmul(
                    P[LNB[1]][:], ones128[:], xq, start=st, stop=sp),
                    incs=[(pe, 1)] if sp else [])
            return b.n(pe), ac_casts

        def ln_rows(slot, pe_done):
            bcmr = FBUF["bcmr"]; bcrs = FBUF["bcrs"]
            bm = bcmr[:, slot * TC:(slot + 1) * TC]
            br = bcrs[:, slot * TC:(slot + 1) * TC]
            b.emit("vector", lambda e, bm=bm: e.tensor_scalar(bm, P[LNB[0]][:], 1.0 / C, None, ALU.mult),
                   waits=[(pe, pe_done)], incs=[(ve, 1), bank_rel(LNB[0])])
            g = b.n(ve)
            b.emit("vector", lambda e, bm=bm, br=br: e.tensor_tensor(br, bm, bm, ALU.mult),
                   waits=[(ve, g)], incs=[(ve, 1)])
            g = b.n(ve)
            b.emit("vector", lambda e, br=br: e.scalar_tensor_tensor(
                br, P[LNB[1]][:], 1.0 / C, br, ALU.mult, ALU.subtract),
                waits=[(ve, g)], incs=[(ve, 1), bank_rel(LNB[1])])
            ve_var = b.n(ve)
            b.emit("scalar", lambda e, br=br: e.activation(br, br, AF.Ln, bias=epsb[:, 0:1]),
                   waits=[(ve, ve_var)], incs=[(ac, 1)])
            g = b.n(ac)
            b.emit("scalar", lambda e, br=br: e.activation(br, br, AF.Exp, scale=-0.5),
                   waits=[(ac, g)], incs=[(ac, 1)])
            ac_rstd = b.n(ac)
            b.emit("vector", lambda e, bm=bm, br=br: e.scalar_tensor_tensor(
                bm, bm, -1.0, br, ALU.mult, ALU.mult),
                waits=[(ac, ac_rstd)], incs=[(ve, 1)])

        def ln_stats(x_view, gbase, slot, x_ready):
            ln_rows(slot, ln_sums(x_view, slot, x_ready)[0])

        def ln_apply(x_view, xo_view, slot, x_ready):
            tmpb = FBUF["tmpb"]; tmpb2 = FBUF["tmpb2"]
            bm = FBUF["bcmr"][:, slot * TC:(slot + 1) * TC]
            br = FBUF["bcrs"][:, slot * TC:(slot + 1) * TC]
            ready_v = b.n(ve)
            for c in range(NCH):
                eng = "vector" if c < 4 else "gpsimd"
                tb = tmpb[:, (c % 2) * TC:(c % 2 + 1) * TC] if c < 4 else tmpb2[:, 0:TC]
                sm = ve if c < 4 else po
                wl = [(ve, ready_v)] + list(x_ready) + [(pe, b.n(pe))] if c in (0, 4) else []
                if c >= 4:
                    wl = wl + [(po, b.n(po))]
                b.emit(eng, lambda e, c=c, tb=tb: e.tensor_tensor(tb, x_view(c), br, ALU.mult),
                       waits=wl, incs=[(sm, 1)])
                b.emit(eng, lambda e, c=c, tb=tb: e.tensor_tensor(xo_view(c), tb, bm, ALU.add),
                       waits=[(po, b.n(po))] if c >= 4 else [], incs=[(sm, 1)])
            return b.n(ve), b.n(po)

        def gating_topk(xln_v, half_base):
            scsig = FBUF["scsig"]; tks = FBUF["tks"]; tkm = FBUF["tkm"]
            tkmx = FBUF["tkmx"]; tkden = FBUF["tkden"]
            comb_tm = FBUF["comb_tm"]; comb_fm = FBUF["comb_fm"]
            for j in range(16):
                tcq, tt = j // 4, j % 4
                bk = j % 2
                wl = [bank_w(bk)]
                for c in range(NCH):
                    st, sp = (c == 0), (c == NCH - 1)
                    b.emit("tensor", lambda e, c=c, bk=bk, st=st, sp=sp, tcq=tcq, tt=tt: e.matmul(
                        P[bk][0:128, 0:16],
                        xln_v(c, tcq)[:, tt * 128:(tt + 1) * 128],
                        gwT_sb[:, c * NE:(c + 1) * NE], start=st, stop=sp),
                        waits=wl if c == 0 else [], incs=[(pe, 1)] if sp else [])
                g = b.n(pe)
                b.emit("scalar", lambda e, j=j, bk=bk: e.activation(
                    scsig[:, j * NE:(j + 1) * NE], P[bk][0:128, 0:16], AF.Sigmoid),
                    waits=[(pe, g)], incs=[(ac, 1), bank_rel(bk)])
            ac_sig = b.n(ac)
            v3 = lambda buf: buf.rearrange("p (j e) -> p j e", e=NE)
            mx3 = tkmx.rearrange("p (j e) -> p j e", e=1)
            mx3b = mx3.broadcast_to([128, 16, NE])
            dn3 = tkden.rearrange("p (j e) -> p j e", e=1)
            dn3b = dn3.broadcast_to([128, 16, NE])
            b.emit("vector", lambda e: e.tensor_copy(tks, scsig),
                   waits=[(ac, ac_sig)], incs=[(ve, 1)])
            for _ in range(3):
                b.emit("vector", lambda e: e.tensor_reduce(mx3, v3(tks), AX.X, ALU.max),
                       incs=[(ve, 1)])
                g = b.n(ve)
                b.emit("vector", lambda e: e.tensor_tensor(v3(tkm), v3(tks), mx3b, ALU.is_ge),
                       waits=[(ve, g)], incs=[(ve, 1)])
                b.emit("vector", lambda e: e.scalar_tensor_tensor(
                    tks, tkm, -1e30, tks, ALU.mult, ALU.add), incs=[(ve, 1)])
            b.emit("vector", lambda e: e.tensor_reduce(mx3, v3(tks), AX.X, ALU.max),
                   incs=[(ve, 1)])
            g = b.n(ve)
            b.emit("vector", lambda e: e.tensor_tensor(v3(tkm), v3(scsig), mx3b, ALU.is_ge),
                   waits=[(ve, g)], incs=[(ve, 1)])
            b.emit("vector", lambda e: e.tensor_tensor(comb_tm, scsig, tkm, ALU.mult),
                   incs=[(ve, 1)])
            b.emit("vector", lambda e: e.tensor_reduce(dn3, v3(comb_tm), AX.X, ALU.add),
                   incs=[(ve, 1)])
            g = b.n(ve)
            b.emit("vector", lambda e: e.tensor_scalar(tkden, tkden, 1e-20, None, ALU.add),
                   waits=[(ve, g)], incs=[(ve, 1)])
            g = b.n(ve)
            b.emit("vector", lambda e: e.reciprocal(tkden, tkden),
                   waits=[(ve, g)], incs=[(ve, 1)])
            g = b.n(ve)
            b.emit("vector", lambda e: e.tensor_tensor(v3(comb_tm), v3(comb_tm), dn3b, ALU.mult),
                   waits=[(ve, g)], incs=[(ve, 1)])
            b.emit("vector", lambda e: e.tensor_scalar(comb_tm, comb_tm, 0.5, None, ALU.mult),
                   incs=[(ve, 1)])
            ve_comb = b.n(ve)
            for j in range(16):
                bk = j % 2
                wl = [(ve, ve_comb), bank_w(bk)]
                if j == 0:
                    wl.append((dsc, b.n(dsc)))
                b.emit("tensor", lambda e, j=j, bk=bk: e.transpose(
                    P[bk][0:16, 0:128], comb_tm[:, j * NE:(j + 1) * NE], idn_f[:]),
                    waits=wl, incs=[(pe, 1)])
                g = b.n(pe)
                b.emit("vector", lambda e, j=j, bk=bk: e.tensor_copy(
                    comb_fm[:, j * 128:(j + 1) * 128], P[bk][0:16, 0:128]),
                    waits=[(pe, g)], incs=[(ve, 1), bank_rel(bk)])
            g = b.n(ve)
            b.emit("sync", lambda e, hb=half_base: e.dma_start(
                out=combD_d[:, hb:hb + TH], in_=comb_fm),
                waits=[(ve, g)], incs=[(dsc, 16)])
            return b.n(dsc)

        def expert_block(xln_v, x_v, cfg):
            n_i = cfg["n_i"]
            hbuf = FBUF["hbuf"]; sgb = FBUF["sgb"]; bccb = FBUF["bccb"]
            tmpb = FBUF["tmpb"]
            po_h_l = {}

            def emit_gu(tc):
                hslot = tc % 2
                if cfg["hmod"] == "comb":
                    ring = cfg["bc_state"]["n"] % 2
                    cfg["bc_state"]["n"] += 1
                    b.emit("sync", lambda e, ring=ring, tc=tc: e.dma_start(
                        out=bccb[:, ring * TC:(ring + 1) * TC],
                        in_=combD_d[cfg["e_idx"]:cfg["e_idx"] + 1,
                                    cfg["hb"] + tc * TC:cfg["hb"] + (tc + 1) * TC].broadcast_to([128, TC])),
                        waits=[(dsc, cfg["comb_at"]), (po, b.n(po))], incs=[(dbc[ring], 16)])
                    bct = bccb[:, ring * TC:(ring + 1) * TC]
                    bc_ready = (dbc[ring], b.n(dbc[ring]))
                for i in range(n_i):
                    bkG, bkU = (0, 1) if i % 2 == 0 else (2, 3)
                    wl = [bank_w(bkG), cfg["wready"]]
                    for c in range(NCH):
                        st, sp = (c == 0), (c == NCH - 1)
                        b.emit("tensor", lambda e, c=c, i=i, tc=tc, bkG=bkG, st=st, sp=sp: e.matmul(
                            P[bkG][:], cfg["wg"](c, i), xln_v(c, tc), start=st, stop=sp),
                            waits=wl if c == 0 else [], incs=[(pe, 1)] if sp else [])
                    pe_g = b.n(pe)
                    wl2 = [bank_w(bkU)]
                    if "wready2" in cfg and tc == 0 and i == 0:
                        wl2.append(cfg["wready2"])
                    for c in range(NCH):
                        st, sp = (c == 0), (c == NCH - 1)
                        b.emit("tensor", lambda e, c=c, i=i, tc=tc, bkU=bkU, st=st, sp=sp: e.matmul(
                            P[bkU][:], cfg["wu"](c, i), xln_v(c, tc), start=st, stop=sp),
                            waits=wl2 if c == 0 else [], incs=[(pe, 1)] if sp else [])
                    pe_u = b.n(pe)
                    sgt = sgb[:, (i % 2) * TC:(i % 2 + 1) * TC]
                    tmt = tmpb[:, (i % 2) * TC:(i % 2 + 1) * TC]
                    hvt = hbuf[:, (hslot * 8 + i) * TC:(hslot * 8 + i + 1) * TC]
                    b.emit("scalar", lambda e, sgt=sgt, bkG=bkG: e.activation(sgt, P[bkG][:], AF.Silu),
                           waits=[(pe, pe_g), (ve, b.n(ve))], incs=[(ac, 1), bank_rel(bkG)])
                    ac_sg = b.n(ac)
                    b.emit("vector", lambda e, sgt=sgt, tmt=tmt, bkU=bkU: e.tensor_tensor(
                        tmt, sgt, P[bkU][:], ALU.mult),
                        waits=[(pe, pe_u), (ac, ac_sg), (po, b.n(po))],
                        incs=[(ve, 1), bank_rel(bkU)])
                    ve_tm = b.n(ve)
                    if cfg["hmod"] == "comb":
                        b.emit("gpsimd", lambda e, tmt=tmt, hvt=hvt, bct=bct: e.tensor_tensor(
                            hvt, tmt, bct, ALU.mult),
                            waits=[(ve, ve_tm), bc_ready], incs=[(po, 1)])
                    else:
                        b.emit("gpsimd", lambda e, tmt=tmt, hvt=hvt: e.tensor_scalar(
                            hvt, tmt, 0.5, None, ALU.mult),
                            waits=[(ve, ve_tm)], incs=[(po, 1)])
                po_h_l[tc] = b.n(po)

            def emit_down(tc):
                hslot = tc % 2
                for c in range(NCH):
                    bkD = 4 + (c % 2)
                    wl = [bank_w(bkD)]
                    if c == 0:
                        wl.append((po, po_h_l[tc]))
                    for i in range(n_i):
                        st, sp = (i == 0), (i == n_i - 1)
                        hvt = hbuf[:, (hslot * 8 + i) * TC:(hslot * 8 + i + 1) * TC]
                        b.emit("tensor", lambda e, i=i, c=c, bkD=bkD, st=st, sp=sp, hvt=hvt: e.matmul(
                            P[bkD][:], cfg["wd"](i, c), hvt, start=st, stop=sp),
                            waits=wl if i == 0 else [], incs=[(pe, 1)] if sp else [])
                    pe_d = b.n(pe)
                    b.emit("vector", lambda e, c=c, tc=tc, bkD=bkD: e.tensor_tensor(
                        x_v(c, tc), x_v(c, tc), P[bkD][:], ALU.add),
                        waits=[(pe, pe_d)], incs=[(ve, 1), bank_rel(bkD)])
                if "on_tc_done" in cfg:
                    cfg["on_tc_done"](tc, b.n(ve))

            for tc in range(4):
                emit_gu(tc)
                if tc >= 1:
                    emit_down(tc - 1)
            emit_down(3)

        def emit_expert_loads(e_idx, wbuf, wsem, pe_gate):
            b.emit("gpsimd", lambda e, ei=e_idx, wbuf=wbuf: e.dma_start(
                out=wbuf[:, 0:4096].rearrange("p (n f) -> p n f", n=8),
                in_=weg_d[ei].rearrange("(n p) f -> p n f", p=128)),
                waits=[(pe, pe_gate)], incs=[(wsem, 16)])
            b.emit("gpsimd", lambda e, ei=e_idx, wbuf=wbuf: e.dma_start(
                out=wbuf[:, 4096:8192].rearrange("p (n f) -> p n f", n=8),
                in_=weu_d[ei].rearrange("(n p) f -> p n f", p=128)), incs=[(wsem, 16)])
            b.emit("gpsimd", lambda e, ei=e_idx, wbuf=wbuf: e.dma_start(
                out=wbuf[:, 8192:12288].rearrange("p (n f) -> p n f", n=4),
                in_=wed_d[ei].rearrange("(n p) f -> p n f", p=128)), incs=[(wsem, 16)])
            return b.n(wsem)

        def emit_sink(sink, hb, x_v, outst, ve_done, x_ready=()):
            if sink == "park":
                for c in range(NCH):
                    b.emit("sync", lambda e, c=c, hb=hb: e.dma_start(
                        out=park_d[c * 128:(c + 1) * 128, hb:hb + TH],
                        in_=x_sb_ref[0][:, c * TH:(c + 1) * TH]),
                        waits=[(ve, ve_done)] if c == 0 else [], incs=[(dsy, 16)])
            else:
                for j in range(16):
                    tcq, tt = j // 4, j % 4
                    dsy_prev = b.n(dsy)
                    for c in range(NCH):
                        bk = c % 2
                        wl = [(ve, ve_done), bank_w(bk)] + list(x_ready)
                        if c == 0 and j >= 1:
                            wl.append((dsy, dsy_prev))
                        b.emit("tensor", lambda e, c=c, tcq=tcq, tt=tt, bk=bk: e.transpose(
                            P[bk][:, 0:128],
                            x_v(c, tcq)[:, tt * 128:(tt + 1) * 128], idn_f[:]),
                            waits=wl, incs=[(pe, 1)])
                        g = b.n(pe)
                        b.emit("vector", lambda e, c=c, bk=bk, outst=outst: e.tensor_copy(
                            outst[:, c * 128:(c + 1) * 128], P[bk][:, 0:128]),
                            waits=[(pe, g)], incs=[(ve, 1), bank_rel(bk)])
                    g = b.n(ve)
                    tglob = hb + j * 128
                    b.emit("sync", lambda e, tglob=tglob, outst=outst: e.dma_start(
                        out=out_d[tglob:tglob + 128, :],
                        in_=outst.rearrange("p (c f) -> p c f", c=8)),
                        waits=[(ve, g)], incs=[(dsy, 16)])

        x_sb_ref = [None]

        def ffn_call(src, sink, compute=True):
            with ExitStack() as ps:
                NAMEC[0] += 1
                _uf = str(NAMEC[0])
                x_sb = ps.enter_context(nc.sbuf_tensor("x_ffn" + _uf, [128, NCH * TH], F32))
                xln_sb = ps.enter_context(nc.sbuf_tensor("xln_ffn" + _uf, [128, NCH * TH], BF16))
                hbuf = ps.enter_context(nc.sbuf_tensor("hbuf" + _uf, [128, 2 * 8 * TC], BF16))
                sgb = ps.enter_context(nc.sbuf_tensor("sgb" + _uf, [128, 2 * TC], F32))
                bccb = ps.enter_context(nc.sbuf_tensor("bccb" + _uf, [128, 2 * TC], F32))
                _gsc = hbuf[:, 4096:8192].bitcast(F32)
                scsig = _gsc[:, 0:256]
                tks = _gsc[:, 256:512]
                tkm = _gsc[:, 512:768]
                comb_tm = _gsc[:, 768:1024]
                tkmx = _gsc[:, 1024:1040]
                tkden = _gsc[:, 1040:1056]
                comb_fm = hbuf[:, 0:4096].bitcast(F32)[0:16, :]
                FBUF.update(dict(hbuf=hbuf, sgb=sgb, bccb=bccb, scsig=scsig, tks=tks,
                                 tkm=tkm, tkmx=tkmx, tkden=tkden, comb_tm=comb_tm,
                                 comb_fm=comb_fm))
                NAMEC[0] += 1
                _u = str(NAMEC[0])
                lnbufs = dict(
                    bcmr=ps.enter_context(nc.sbuf_tensor("bcmr" + _u, [128, 2 * TC], F32)),
                    bcrs=ps.enter_context(nc.sbuf_tensor("bcrs" + _u, [128, 2 * TC], F32)),
                    xbfc=ps.enter_context(nc.sbuf_tensor("xbfc" + _u, [128, 8 * TC], BF16)),
                    xsqc=ps.enter_context(nc.sbuf_tensor("xsqc" + _u, [128, 8 * TC], BF16)),
                    tmpb=ps.enter_context(nc.sbuf_tensor("tmpb" + _u, [128, 2 * TC], F32)),
                    tmpb2=ps.enter_context(nc.sbuf_tensor("tmpb2" + _u, [128, TC], F32)),
                )
                FBUF.update(lnbufs)
                outst = hbuf[:, 0:2048].bitcast(F32)
                x_sb_ref[0] = x_sb
                x_v = lambda c, tc: x_sb[:, c * TH + tc * TC: c * TH + (tc + 1) * TC]
                xln_v = lambda c, tc: xln_sb[:, c * TH + tc * TC: c * TH + (tc + 1) * TC]
                src_d = xT_d if src == "input" else park_d

                for half in range(2):
                    hb = half * TH
                    x_ready_tc = {}
                    for tc in range(4):
                        for c in range(NCH):
                            wl = allsync() + [(dsy, b.n(dsy)), (dxw, b.n(dxw))] + init_ready \
                                if (tc == 0 and c == 0) else []
                            b.emit("sync", lambda e, c=c, tc=tc, hb=hb: e.dma_start(
                                out=x_sb[:, c * TH + tc * TC: c * TH + (tc + 1) * TC],
                                in_=src_d[c * 128:(c + 1) * 128, hb + tc * TC: hb + (tc + 1) * TC]),
                                waits=wl, incs=[(dxl[tc], 16)])
                        x_ready_tc[tc] = [(dxl[tc], b.n(dxl[tc]))]
                    x_ready = [x_ready_tc[t][0] for t in range(4)]
                    if not compute:
                        emit_sink(sink, hb, x_v, outst, b.n(ve), x_ready)
                        continue
                    wA_at = emit_expert_loads(0, wbufA, dwA, b.n(pe))
                    wB_at = emit_expert_loads(1, wbufB, dwB, b.n(pe))
                    w_at = {0: wA_at, 1: wB_at}
                    for tc in range(4):
                        slot = tc % 2
                        ln_stats(lambda c, tc=tc: x_v(c, tc), hb + tc * TC, slot, x_ready)
                        ln_apply(lambda c, tc=tc: x_v(c, tc),
                                 lambda c, tc=tc: xln_v(c, tc), slot, x_ready)
                    b.emit("tensor", lambda e: e.matmul(
                        P[LNB[0]][0:1, 0:1], ones_b[:], ones_b[:], start=True, stop=True),
                        waits=[(ve, b.n(ve)), (po, b.n(po)), bank_w(LNB[0])], incs=[(pe, 1)])
                    g = b.n(pe)
                    b.emit("vector", lambda e, _tk=tkden: e.tensor_copy(_tk[:, 0:1], P[LNB[0]][0:128, 0:1]),
                           waits=[(pe, g)], incs=[(ve, 1), bank_rel(LNB[0])])
                    comb_at = gating_topk(xln_v, hb)
                    bc_state = {"n": 0}
                    for ei in range(NE):
                        wbuf = wbufA if ei % 2 == 0 else wbufB
                        wsem = dwA if ei % 2 == 0 else dwB
                        expert_block(xln_v, x_v, dict(
                            n_i=4, hmod="comb", e_idx=ei, hb=hb, comb_at=comb_at,
                            bc_state=bc_state, wready=(wsem, w_at[ei]),
                            wg=lambda cc, i, wbuf=wbuf: wbuf[:, cc * 512 + i * 128: cc * 512 + (i + 1) * 128],
                            wu=lambda cc, i, wbuf=wbuf: wbuf[:, 4096 + cc * 512 + i * 128: 4096 + cc * 512 + (i + 1) * 128],
                            wd=lambda i, cc, wbuf=wbuf: wbuf[:, 8192 + i * 1024 + cc * 128: 8192 + i * 1024 + (cc + 1) * 128],
                        ))
                        pe_done = b.n(pe)
                        if ei + 2 < NE:
                            tgt = ei + 2
                            w_at[tgt] = emit_expert_loads(
                                tgt, wbufA if tgt % 2 == 0 else wbufB,
                                dwA if tgt % 2 == 0 else dwB, pe_done)
                        elif ei == NE - 2:
                            b.emit("gpsimd", lambda e: e.dma_start(
                                out=wbufA[:, 0:8192].rearrange("p (n f) -> p n f", n=8),
                                in_=wsg_d[:].rearrange("(n p) f -> p n f", p=128)),
                                waits=[(pe, pe_done)], incs=[(dwA, 16)])
                            b.emit("gpsimd", lambda e: e.dma_start(
                                out=wbufA[:, 8192:12288].rearrange("p (n f) -> p n f", n=4),
                                in_=wsd_d[0:512].rearrange("(n p) f -> p n f", p=128)),
                                incs=[(dwA, 16)])
                            shA_at = b.n(dwA)
                        elif ei == NE - 1:
                            b.emit("gpsimd", lambda e: e.dma_start(
                                out=wbufB[:, 0:8192].rearrange("p (n f) -> p n f", n=8),
                                in_=wsu_d[:].rearrange("(n p) f -> p n f", p=128)),
                                waits=[(pe, pe_done)], incs=[(dwB, 16)])
                            b.emit("gpsimd", lambda e: e.dma_start(
                                out=wbufB[:, 8192:12288].rearrange("p (n f) -> p n f", n=4),
                                in_=wsd_d[512:1024].rearrange("(n p) f -> p n f", p=128)),
                                incs=[(dwB, 16)])
                            shB_at = b.n(dwB)

                    def wsd_sl(i, cc):
                        if i < 4:
                            return wbufA[:, 8192 + i * 1024 + cc * 128: 8192 + i * 1024 + (cc + 1) * 128]
                        return wbufB[:, 8192 + (i - 4) * 1024 + cc * 128: 8192 + (i - 4) * 1024 + (cc + 1) * 128]
                    expert_block(xln_v, x_v, dict(
                        n_i=8, hmod="half", wready=(dwA, shA_at), wready2=(dwB, shB_at),
                        wg=lambda cc, i: wbufA[:, cc * 1024 + i * 128: cc * 1024 + (i + 1) * 128],
                        wu=lambda cc, i: wbufB[:, cc * 1024 + i * 128: cc * 1024 + (i + 1) * 128],
                        wd=wsd_sl,
                    ))
                    ve_done = b.n(ve)
                    emit_sink(sink, hb, x_v, outst, ve_done)


        def qkv_phase(K_sb):
            with ExitStack() as ps:
                xpc = ps.enter_context(nc.sbuf_tensor("xpc", [128, 2 * 8 * TC], F32))
                xlnc = ps.enter_context(nc.sbuf_tensor("xlnc", [128, 2 * 8 * TC], BF16))
                qst = ps.enter_context(nc.sbuf_tensor("qst", [128, 2 * TC], BF16))
                vst = ps.enter_context(nc.sbuf_tensor("vst", [128, 2 * 1024], BF16))
                bvbc = ps.enter_context(nc.sbuf_tensor("bvbc", [128, 1024], F32))
                NAMEC[0] += 1
                _u = str(NAMEC[0])
                lnbufs = dict(
                    bcmr=ps.enter_context(nc.sbuf_tensor("bcmr" + _u, [128, 2 * TC], F32)),
                    bcrs=ps.enter_context(nc.sbuf_tensor("bcrs" + _u, [128, 2 * TC], F32)),
                    xbfc=ps.enter_context(nc.sbuf_tensor("xbfc" + _u, [128, 8 * TC], BF16)),
                    xsqc=ps.enter_context(nc.sbuf_tensor("xsqc" + _u, [128, 8 * TC], BF16)),
                    tmpb=ps.enter_context(nc.sbuf_tensor("tmpb" + _u, [128, 2 * TC], F32)),
                    tmpb2=ps.enter_context(nc.sbuf_tensor("tmpb2" + _u, [128, TC], F32)),
                )
                FBUF.update(lnbufs)

                pe_gate = b.n(pe)
                b.emit("gpsimd", lambda e: e.dma_start(
                    out=wbufA[:].rearrange("p (n f) -> p n f", n=4),
                    in_=wat_d[0:512].rearrange("(n p) f -> p n f", p=128)),
                    waits=[(pe, pe_gate)], incs=[(dwA, 16)])
                b.emit("gpsimd", lambda e: e.dma_start(
                    out=wbufB[:].rearrange("p (n f) -> p n f", n=4),
                    in_=wat_d[512:1024].rearrange("(n p) f -> p n f", p=128)),
                    incs=[(dwB, 16)])
                wA_at, wB_at = b.n(dwA), b.n(dwB)
                b.emit("scalar", lambda e: e.dma_start(
                    out=bvbc[:], in_=bav_d[0:1, :].broadcast_to([128, C])), incs=[(dsc, 16)])
                bv_at = b.n(dsc)

                def wat_sl(c, m):
                    buf = wbufA if c < 4 else wbufB
                    cc = c % 4
                    return buf[:, cc * 3072 + m * 128: cc * 3072 + (m + 1) * 128]

                def wat_v(c, blk):
                    buf = wbufA if c < 4 else wbufB
                    cc = c % 4
                    return buf[:, cc * 3072 + 2048 + blk * 512: cc * 3072 + 2048 + (blk + 1) * 512]

                def qkv_load(tcg, wl):
                    ring = tcg % 2
                    gb = tcg * TC
                    b.emit("gpsimd", lambda e, gb=gb, ring=ring: e.dma_start(
                        out=xpc[:, ring * 8 * TC:(ring + 1) * 8 * TC].rearrange("p (c f) -> p c f", c=8),
                        in_=park_d[:, gb:gb + TC].rearrange("(c p) f -> p c f", p=128)),
                        waits=wl, incs=[(dxc[ring], 16)])
                    return [(dxc[ring], b.n(dxc[ring]))]

                def qkv_body(tcg, x_ready):
                    ring = tcg % 2
                    gb = tcg * TC
                    xv = lambda c, ring=ring: xpc[:, (ring * 8 + c) * TC:(ring * 8 + c + 1) * TC]
                    xlv = lambda c, ring=ring: xlnc[:, (ring * 8 + c) * TC:(ring * 8 + c + 1) * TC]
                    apply_gates[tcg] = ln_apply(xv, xlv, tcg % 2, x_ready)
                    po_ln, ve_ln = b.n(po), b.n(ve)
                    for m in range(16):
                        bk = m % 2
                        wl = [bank_w(bk), (dwA, wA_at), (dwB, wB_at)]
                        if m == 0:
                            wl += [(ve, ve_ln), (po, po_ln)]
                        for c in range(NCH):
                            st, sp = (c == 0), (c == NCH - 1)
                            b.emit("tensor", lambda e, c=c, m=m, bk=bk, st=st, sp=sp, xlv=xlv: e.matmul(
                                P[bk][:], wat_sl(c, m), xlv(c), start=st, stop=sp),
                                waits=wl if c == 0 else [], incs=[(pe, 1)] if sp else [])
                        pe_m = b.n(pe)
                        if m < 8:
                            qslot = m % 2
                            b.emit("scalar", lambda e, m=m, bk=bk, qslot=qslot: e.add(
                                qst[:, qslot * TC:(qslot + 1) * TC], P[bk][:],
                                batqk_sb[:, m:m + 1]),
                                waits=[(pe, pe_m), (dqs[qslot], b.n(dqs[qslot]))],
                                incs=[(ac, 1), bank_rel(bk)])
                            g = b.n(ac)
                            b.emit("sync", lambda e, m=m, gb=gb, qslot=qslot: e.dma_start(
                                out=qsp_d[m * 128:(m + 1) * 128, gb:gb + TC],
                                in_=qst[:, qslot * TC:(qslot + 1) * TC]),
                                waits=[(ac, g)], incs=[(dqs[qslot], 16)])
                        else:
                            mk = m - 8
                            b.emit("scalar", lambda e, mk=mk, bk=bk, gb=gb: e.add(
                                K_sb[:, mk * T + gb: mk * T + gb + TC], P[bk][:],
                                batqk_sb[:, 8 + mk:9 + mk]),
                                waits=[(pe, pe_m)], incs=[(ac, 1), bank_rel(bk)])
                    for tt in range(4):
                        vslot = tt % 2
                        for blk in range(2):
                            bk = 4 + blk
                            wl = [bank_w(bk)]
                            for c in range(NCH):
                                st, sp = (c == 0), (c == NCH - 1)
                                b.emit("tensor", lambda e, c=c, tt=tt, blk=blk, bk=bk, st=st, sp=sp, xlv=xlv: e.matmul(
                                    P[bk][:], xlv(c)[:, tt * 128:(tt + 1) * 128], wat_v(c, blk),
                                    start=st, stop=sp),
                                    waits=wl if c == 0 else [], incs=[(pe, 1)] if sp else [])
                            pe_v = b.n(pe)
                            b.emit("vector", lambda e, blk=blk, bk=bk, vslot=vslot: e.tensor_tensor(
                                vst[:, vslot * 1024 + blk * 512: vslot * 1024 + (blk + 1) * 512],
                                P[bk][:], bvbc[:, blk * 512:(blk + 1) * 512], ALU.add),
                                waits=[(pe, pe_v), (dsc, bv_at), (dvs[vslot], b.n(dvs[vslot]))],
                                incs=[(ve, 1), bank_rel(bk)])
                        g = b.n(ve)
                        trow = gb + tt * 128
                        for pp in range(4):
                            b.emit("sync", lambda e, pp=pp, trow=trow, vslot=vslot: e.dma_start(
                                out=vsp_d[pp, trow:trow + 128, :],
                                in_=vst[:, vslot * 1024 + pp * 256: vslot * 1024 + (pp + 1) * 256]),
                                waits=[(ve, g)] if pp == 0 else [], incs=[(dvs[vslot], 16)])

                xrdy = {}
                apply_gates = {}
                cast_gates = {}
                xrdy[0] = qkv_load(0, [(dsy, b.n(dsy)), (pe, b.n(pe)), (po, b.n(po)),
                                       (ac, b.n(ac)), (ve, b.n(ve))])
                xrdy[1] = qkv_load(1, [(dsy, b.n(dsy)), (pe, b.n(pe)), (po, b.n(po)),
                                       (ac, b.n(ac)), (ve, b.n(ve))])
                for i in range(8):
                    xv = lambda c, i=i: xpc[:, ((i % 2) * 8 + c) * TC:((i % 2) * 8 + c + 1) * TC]
                    pe_done, cast_gates[i] = ln_sums(xv, i % 2, xrdy[i])
                    if i >= 1:
                        qkv_body(i - 1, xrdy[i - 1])
                        if i + 1 <= 7:
                            gv, gp = apply_gates[i - 1]
                            xrdy[i + 1] = qkv_load(i + 1, [
                                (ac, cast_gates[i - 1]), (ve, gv), (po, gp)])
                    ln_rows(i % 2, pe_done)
                qkv_body(7, xrdy[7])

        def attention_phase():
            with ExitStack() as ps:
                K_sb = ps.enter_context(nc.sbuf_tensor("K_sb", [128, NCH * T], BF16))
                qkv_phase(K_sb)
                cmask_sb = ps.enter_context(nc.sbuf_tensor("cmask_sb", [128, 4 * 512], BF16))
                b.emit("gpsimd", lambda e: e.dma_start(
                    out=cmask_sb[:].rearrange("p (a f) -> p a f", a=4),
                    in_=cmask_d[:].rearrange("(a p) f -> p a f", p=128)),
                    waits=[(ve, b.n(ve)), (ac, b.n(ac)), (po, b.n(po))], incs=[(dsy, 16)])
                cmask_at = b.n(dsy)
                ac_qkv = b.n(ac)
                pe_gate = b.n(pe)
                b.emit("gpsimd", lambda e: e.dma_start(
                    out=wbufA[:, 0:8192].rearrange("p (n f) -> p n f", n=8),
                    in_=wproj_d[:].rearrange("(n p) f -> p n f", p=128)),
                    waits=[(pe, pe_gate)], incs=[(dwA, 16)])
                wp_at = b.n(dwA)

                pb_sb = ps.enter_context(nc.sbuf_tensor("pb_sb", [128, 2 * T], BF16))
                pT_sb = ps.enter_context(nc.sbuf_tensor("pT_sb", [128, 32 * 256], BF16))
                av_sb = ps.enter_context(nc.sbuf_tensor("av_sb", [128, 8 * 256], BF16))
                vcol = ps.enter_context(nc.sbuf_tensor("vcol", [128, 6 * 8 * 256], BF16))
                q_sb = ps.enter_context(nc.sbuf_tensor("q_sb", [128, 2 * 8 * 128], BF16))
                xg_sb = ps.enter_context(nc.sbuf_tensor("xg_sb", [128, 8 * 256], F32))
                lsum = ps.enter_context(nc.sbuf_tensor("lsum", [128, 4], F32))
                mrow_t = ps.enter_context(nc.sbuf_tensor("mrow_t", [128, 2], F32))
                mxp = ps.enter_context(nc.sbuf_tensor("mxp", [128, 16], F32))
                lsum2 = ps.enter_context(nc.sbuf_tensor("lsum2", [128, 2], F32))

                qkv_dsy = b.n(dsy)
                qkv_spills = [(dqs[0], b.n(dqs[0])), (dqs[1], b.n(dqs[1])),
                              (dvs[0], b.n(dvs[0])), (dvs[1], b.n(dvs[1]))]

                q_at = {}

                def load_q(t):
                    ring = t % 2
                    b.emit("sync", lambda e, t=t, ring=ring: e.dma_start(
                        out=q_sb[:, ring * 1024:(ring + 1) * 1024].rearrange("p (c f) -> p c f", c=8),
                        in_=qsp_d[:, t * 128:(t + 1) * 128].rearrange("(c p) f -> p c f", p=128)),
                        waits=[(dsy, qkv_dsy), (pe, b.n(pe))] + qkv_spills,
                        incs=[(dqt[ring], 16)])
                    q_at[t] = b.n(dqt[ring])

                av_kk = [0]

                def load_vcol(pair, s0, cnt, sl, gate):
                    b.emit("sync", lambda e, pair=pair, s0=s0, cnt=cnt, sl=sl: e.dma_start(
                        out=vcol[:, sl * 2048: sl * 2048 + cnt * 256].rearrange(
                            "p (j f) -> p j f", f=256),
                        in_=vsp_d[pair, s0 * 128:(s0 + cnt) * 128, :].rearrange("(j p) f -> p j f", p=128)),
                        waits=[(dsy, qkv_dsy), (pe, gate)] + qkv_spills, incs=[(dvc[sl], 16)])
                    return b.n(dvc[sl])

                load_q(0)
                load_q(1)

                for g_i in range(16):
                    smax = 2 * g_i + 2
                    ve_pb_t = {}
                    for ti in range(2):
                        t = 2 * g_i + ti
                        ring = t % 2
                        sring = t % 2
                        nb = t // 4 + 1
                        reg = t % 2
                        nlo = min(nb, 4)
                        bmap = {}
                        for bi in range(nb):
                            tens = reg if bi < 4 else 1 - reg
                            bmap[bi] = tens * 4 + (bi % 4)
                        for bi in range(nb):
                            bk = bmap[bi]
                            wl = [bank_w(bk), (dqt[ring], q_at[t]), (ac, ac_qkv)]
                            for c in range(NCH):
                                st, sp = (c == 0), (c == NCH - 1)
                                b.emit("tensor", lambda e, c=c, bi=bi, bk=bk, st=st, sp=sp, ring=ring: e.matmul(
                                    P[bk],
                                    q_sb[:, ring * 1024 + c * 128: ring * 1024 + (c + 1) * 128],
                                    K_sb[:, c * T + bi * 512: c * T + (bi + 1) * 512],
                                    start=st, stop=sp),
                                    waits=wl if c == 0 else [], incs=[(pe, 1)] if sp else [])
                            pe_blk = b.n(pe)
                            if bi == nb - 1:
                                al = t % 4
                                b.emit("vector", lambda e, bk=bk, al=al: e.tensor_tensor(
                                    P[bk], P[bk], cmask_sb[:, al * 512:(al + 1) * 512], ALU.add),
                                    waits=[(pe, pe_blk), (dsy, cmask_at)], incs=[(ve, 1)])
                            b.emit("vector", lambda e, bk=bk, bi=bi, sring=sring: e.tensor_reduce(
                                mxp[:, sring * 8 + bi: sring * 8 + bi + 1],
                                P[bk], AX.X, ALU.max),
                                waits=[(pe, pe_blk), (ve, b.n(ve))], incs=[(ve, 1)])
                        ve_evac = b.n(ve)
                        b.emit("vector", lambda e, sring=sring, nb=nb: e.tensor_reduce(
                            mrow_t[:, sring:sring + 1], mxp[:, sring * 8: sring * 8 + nb],
                            AX.X, ALU.max),
                            waits=[(ve, ve_evac), (ac, b.n(ac))], incs=[(ve, 1)])
                        g1 = b.n(ve)
                        b.emit("vector", lambda e, sring=sring: e.tensor_scalar(
                            mrow_t[:, sring:sring + 1], mrow_t[:, sring:sring + 1],
                            -1.0 / 32.0, None, ALU.mult), waits=[(ve, g1)], incs=[(ve, 1)])
                        g2 = b.n(ve)
                        rels1 = [bank_rel(bmap[bi]) for bi in range(nlo)]
                        b.emit("scalar", lambda e, sring=sring, nlo=nlo, reg=reg: e.activation(
                            pb_sb[:, sring * T: sring * T + nlo * 512],
                            PFULL[reg][:, 0:nlo * 512], AF.Exp,
                            bias=mrow_t[:, sring:sring + 1], scale=1.0 / 32.0,
                            accum_out=lsum[:, sring:sring + 1]),
                            waits=[(ve, g2), (pe, b.n(pe))], incs=[(ac, 1)] + rels1)
                        if nb > 4:
                            nhi = nb - 4
                            rels2 = [bank_rel(bmap[bi]) for bi in range(4, nb)]
                            b.emit("scalar", lambda e, sring=sring, nhi=nhi, reg=reg: e.activation(
                                pb_sb[:, sring * T + 2048: sring * T + 2048 + nhi * 512],
                                PFULL[1 - reg][:, 0:nhi * 512], AF.Exp,
                                bias=mrow_t[:, sring:sring + 1], scale=1.0 / 32.0,
                                accum_out=lsum2[:, sring:sring + 1]),
                                incs=[(ac, 1)] + rels2)
                            g3b = b.n(ac)
                            b.emit("vector", lambda e, sring=sring: e.tensor_tensor(
                                lsum[:, sring:sring + 1], lsum[:, sring:sring + 1],
                                lsum2[:, sring:sring + 1], ALU.add),
                                waits=[(ac, g3b)], incs=[(ve, 1)])
                        g3 = b.n(ac)
                        g3v = b.n(ve)
                        b.emit("vector", lambda e, sring=sring: e.reciprocal(
                            lsum[:, 2 + sring:3 + sring], lsum[:, sring:sring + 1]),
                            waits=[(ac, g3), (ve, g3v)], incs=[(ve, 1)])
                        g4 = b.n(ve)
                        b.emit("vector", lambda e, sring=sring, nb=nb: e.tensor_scalar(
                            pb_sb[:, sring * T: sring * T + nb * 512],
                            pb_sb[:, sring * T: sring * T + nb * 512],
                            lsum[:, 2 + sring:3 + sring], None, ALU.mult),
                            waits=[(ve, g4)], incs=[(ve, 1)])
                        ve_pb_t[t] = b.n(ve)
                        nfill = smax - (t + 1)
                        if nfill > 0:
                            b.emit("vector", lambda e, t=t, ti=ti, smax=smax: e.memset(
                                pT_sb[:].rearrange("p (j q) -> p j q", q=256)[
                                    :, t + 1:smax, ti * 128:(ti + 1) * 128], 0.0),
                                incs=[(ve, 1)])
                    for ti in range(2):
                        t = 2 * g_i + ti
                        sring = t % 2
                        for j in range(t + 1):
                            bk = 6 + (j % 2)
                            b.emit("tensor", lambda e, j=j, bk=bk, sring=sring: e.transpose(
                                P[bk][:].bitcast(BF16)[:, 0:128],
                                pb_sb[:, sring * T + j * 128: sring * T + (j + 1) * 128],
                                idn_b[:]),
                                waits=[(ve, ve_pb_t[t]), bank_w(bk)], incs=[(pe, 1)])
                            g5 = b.n(pe)
                            b.emit("scalar", lambda e, j=j, bk=bk, ti=ti: e.copy(
                                pT_sb[:, j * 256 + ti * 128: j * 256 + (ti + 1) * 128],
                                P[bk][:].bitcast(BF16)[:, 0:128]),
                                waits=[(pe, g5)], incs=[(ac, 1), bank_rel(bk)])
                        if t + 2 < 32:
                            load_q(t + 2)
                    ac_tr = b.n(ac)
                    ve_fill = b.n(ve)
                    pieces = [(p0, min(8, smax - p0)) for p0 in range(0, smax, 8)]
                    npc = len(pieces)
                    slot_of = {}
                    v_at = {}
                    pe_after_pair = {}

                    def emit_vload(pair, pi, gate):
                        s0, cnt = pieces[pi]
                        sl = av_kk[0] % 6
                        av_kk[0] += 1
                        slot_of[(pair, pi)] = sl
                        v_at[(pair, pi)] = load_vcol(pair, s0, cnt, sl, gate)

                    for pi in range(npc):
                        emit_vload(0, pi, b.n(pe))
                    for pair in range(4):
                        gate0 = b.n(pe)
                        if pair + 1 < 4:
                            for pi in range(min(npc, 6 - npc)):
                                emit_vload(pair + 1, pi, gate0)
                        wl0 = [(ac, ac_tr), (ve, ve_fill)]
                        for pi in range(npc):
                            wl0.append((dvc[slot_of[(pair, pi)]], v_at[(pair, pi)]))
                        for half in range(2):
                            cb = pair * 2 + half
                            bk = 4 + (cb % 2)
                            wl = wl0 + [bank_w(bk)] if half == 0 else [bank_w(bk)]
                            for pi, (s0, cnt) in enumerate(pieces):
                                sl = slot_of[(pair, pi)]
                                for j in range(cnt):
                                    st = (pi == 0 and j == 0)
                                    sp = (pi == npc - 1 and j == cnt - 1)
                                    sj = s0 + j
                                    b.emit("tensor", lambda e, j=j, sj=sj, bk=bk, sl=sl, half=half, st=st, sp=sp: e.matmul(
                                        P[bk][:, 0:256],
                                        vcol[:, sl * 2048 + j * 256 + half * 128: sl * 2048 + j * 256 + half * 128 + 128],
                                        pT_sb[:, sj * 256:(sj + 1) * 256], start=st, stop=sp),
                                        waits=wl if (pi == 0 and j == 0) else [],
                                        incs=[(pe, 1)] if sp else [])
                            pe_av = b.n(pe)
                            b.emit("scalar", lambda e, cb=cb, bk=bk: e.copy(
                                av_sb[:, cb * 256:(cb + 1) * 256], P[bk][:, 0:256]),
                                waits=[(pe, pe_av)], incs=[(ac, 1), bank_rel(bk)])
                        if pair + 1 < 4:
                            for pi in range(min(npc, 6 - npc), npc):
                                emit_vload(pair + 1, pi, b.n(pe))
                    ac_av = b.n(ac)
                    xring = 0
                    b.emit("gpsimd", lambda e, g_i=g_i: e.dma_start(
                        out=xg_sb[:].rearrange("p (c f) -> p c f", c=8),
                        in_=park_d[:, g_i * 256:(g_i + 1) * 256].rearrange("(c p) f -> p c f", p=128)),
                        waits=[(dsy, qkv_dsy), (dxw, b.n(dxw)), (ve, b.n(ve))], incs=[(dxg[0], 16)])
                    x_at = b.n(dxg[0])
                    wp_sl = lambda cc, m: wbufA[:, cc * 1024 + m * 128: cc * 1024 + (m + 1) * 128]
                    for m in range(NCH):
                        bk = 4 + (m % 2)
                        wl = [bank_w(bk), (dwA, wp_at), (ac, ac_av)]
                        if m == 0 and "proj" in ser:
                            wl += allsync()
                        for c in range(NCH):
                            st, sp = (c == 0), (c == NCH - 1)
                            b.emit("tensor", lambda e, c=c, m=m, bk=bk, st=st, sp=sp: e.matmul(
                                P[bk][:, 0:256], wp_sl(c, m), av_sb[:, c * 256:(c + 1) * 256],
                                start=st, stop=sp),
                                waits=wl if c == 0 else [], incs=[(pe, 1)] if sp else [])
                        pe_pj = b.n(pe)
                        b.emit("vector", lambda e, m=m, bk=bk: e.scalar_tensor_tensor(
                            xg_sb[:, m * 256:(m + 1) * 256],
                            P[bk][:, 0:256], bproj_sb[:, m:m + 1],
                            xg_sb[:, m * 256:(m + 1) * 256],
                            ALU.add, ALU.add),
                            waits=[(pe, pe_pj), (dxg[0], x_at)],
                            incs=[(ve, 1), bank_rel(bk)])
                    ve_res = b.n(ve)
                    b.emit("gpsimd", lambda e, g_i=g_i: e.dma_start(
                        out=park_d[:, g_i * 256:(g_i + 1) * 256].rearrange("(c p) f -> p c f", p=128),
                        in_=xg_sb[:].rearrange("p (c f) -> p c f", c=8)),
                        waits=[(ve, ve_res)], incs=[(dxw, 16)])

        if stage == "ffn1":
            ffn_call("input", "out")
        elif stage == "attn":
            ffn_call("input", "park")
            attention_phase()
            ffn_call("park", "out", compute=False)
        else:
            ffn_call("input", "park")
            attention_phase()
            ffn_call("park", "out")

        b.emit("sync", lambda e: None,
               waits=[(dsy, b.n(dsy)), (dsc, b.n(dsc)), (dwA, b.n(dwA)), (dwB, b.n(dwB)),
                      (dxw, b.n(dxw)), (dvs[0], b.n(dvs[0])), (dvs[1], b.n(dvs[1])),
                      (dqs[0], b.n(dqs[0])), (dqs[1], b.n(dqs[1]))])

        block = stack.enter_context(nc.Block())
        b.finish(block)
    return nc


_NC_CACHE = None


def kernel(L0_attn_x, L0_attn_w_attn, L0_attn_b_attn, L0_attn_w_proj, L0_attn_b_proj,
           L0_ffn_gate_w, L0_ffn_expert_gate, L0_ffn_expert_up, L0_ffn_expert_down,
           L0_ffn_shared_gate, L0_ffn_shared_up, L0_ffn_shared_down):
    global _NC_CACHE
    x = np.asarray(L0_attn_x, np.float32)
    B = x.shape[0]
    wat = np.ascontiguousarray(np.asarray(L0_attn_w_attn, np.float32))
    b_attn = np.asarray(L0_attn_b_attn, np.float32)
    batqk = np.ascontiguousarray(b_attn[:2048].reshape(16, 128).T)
    bav = np.ascontiguousarray(b_attn[2048:].reshape(1, 1024))
    wproj = np.ascontiguousarray(np.asarray(L0_attn_w_proj, np.float32))
    bproj = np.ascontiguousarray(np.asarray(L0_attn_b_proj, np.float32).reshape(8, 128).T)
    gwT = np.ascontiguousarray(np.asarray(L0_ffn_gate_w, np.float32).T)
    weg = np.ascontiguousarray(np.asarray(L0_ffn_expert_gate, np.float32))
    weu = np.ascontiguousarray(np.asarray(L0_ffn_expert_up, np.float32))
    wed = np.ascontiguousarray(np.asarray(L0_ffn_expert_down, np.float32))
    wsg = np.ascontiguousarray(np.asarray(L0_ffn_shared_gate, np.float32))
    wsu = np.ascontiguousarray(np.asarray(L0_ffn_shared_up, np.float32))
    wsd = np.ascontiguousarray(np.asarray(L0_ffn_shared_down, np.float32))

    p = np.arange(128)[:, None]
    sl = np.arange(512)[None, :]
    cmask = np.concatenate(
        [np.where(sl <= a * 128 + p, 0.0, -1e10).astype(np.float32) for a in range(4)], axis=0)
    idn = np.eye(128, dtype=np.float32)

    if _NC_CACHE is None:
        _NC_CACHE = build_program()
    nc = _NC_CACHE

    in_maps = []
    for bb in range(B):
        in_maps.append(dict(
            xT=np.ascontiguousarray(x[bb].T),
            wat=wat, batqk=batqk, bav=bav, wproj=wproj, bproj=bproj,
            gwT=gwT, weg=weg, weu=weu, wed=wed, wsg=wsg, wsu=wsu, wsd=wsd,
            cmask=cmask, idn=idn,
        ))
    res = run_bass_kernel_spmd(nc, in_maps, core_ids=list(range(B)))
    out = np.stack([np.asarray(res.results[bb]["out"]) for bb in range(B)], axis=0)
    if os.environ.get("KDUMP", "0") == "1":
        np.save("_dump_qsp.npy", np.asarray(res.results[0]["qsp"]).view(np.uint16))
        np.save("_dump_vsp.npy", np.asarray(res.results[0]["vsp"]).view(np.uint16))
        np.save("_dump_park.npy", np.asarray(res.results[0]["park"]))
    return out.astype(np.float32)


# version 23
# speedup vs baseline: 1.0139x; 1.0139x over previous
import os
import numpy as np
from contextlib import ExitStack

import concourse.bass as bass
import concourse.mybir as mybir
from concourse.bass_utils import run_bass_kernel_spmd

F32 = mybir.dt.float32
BF16 = mybir.dt.bfloat16
AF = mybir.ActivationFunctionType
ALU = mybir.AluOpType
AX = mybir.AxisListType

T = 4096
C = 1024
TH = 2048
TC = 512
NE = 16
NCH = 8

ENGS = ("tensor", "vector", "scalar", "gpsimd", "sync")


class Builder:
    def __init__(self, nc, stack):
        self.nc = nc
        self.stack = stack
        self.ops = {e: [] for e in ENGS}
        self.cnt = {}
        self.name_of = {}

    def sem(self, name):
        h = self.stack.enter_context(self.nc.semaphore(name))
        self.cnt[name] = 0
        self.name_of[id(h)] = name
        return h

    def n(self, sem):
        return self.cnt[self.name_of[id(sem)]]

    def emit(self, eng, fn, waits=(), incs=()):
        real = [(s, k) for (s, k) in incs if not (isinstance(s, tuple) and s[0] == "REL")]
        rels = [s[1] for (s, k) in incs if isinstance(s, tuple) and s[0] == "REL"]
        assert len(real) <= 1, "at most one sync update per instruction"
        own = getattr(self, "own", {}).get(eng)
        if own is not None and not any(k >= 16 for _, k in real):
            waits = list(waits) + [(own, self.n(own))]
        for s, k in real:
            self.cnt[self.name_of[id(s)]] += k
        if rels:
            assert real, "bank release needs a real inc on the same instruction"
            rs, _ = real[0]
            tok = (rs, self.n(rs))
            for bi in rels:
                self.bank_tok[bi] = tok
        waits = tuple(w for w in waits if w is not None)
        real = tuple(real)

        def run(e):
            for s, v in waits:
                if v > 0:
                    e.wait_ge(s, v)
            ins = fn(e)
            for s, k in real:
                ins = ins.then_inc(s, k)
        self.ops[eng].append(run)

    def finish(self, block):
        for eng in ENGS:
            fns = list(self.ops[eng])
            def make(fns):
                def body(e):
                    for f in fns:
                        f(e)
                return body
            getattr(block, eng)(make(fns))


def build_program():
    stage = os.environ.get("KSTAGE", "full")
    ser = os.environ.get("KSER", "")
    nc = bass.Bass("TRN2", target_bir_lowering=False)

    xT_d = nc.dram_tensor("xT", [C, T], F32, kind="ExternalInput").ap()
    wat_d = nc.dram_tensor("wat", [C, 3 * C], F32, kind="ExternalInput").ap()
    batqk_d = nc.dram_tensor("batqk", [128, 16], F32, kind="ExternalInput").ap()
    bav_d = nc.dram_tensor("bav", [1, C], F32, kind="ExternalInput").ap()
    wproj_d = nc.dram_tensor("wproj", [C, C], F32, kind="ExternalInput").ap()
    bproj_d = nc.dram_tensor("bproj", [128, 8], F32, kind="ExternalInput").ap()
    gwT_d = nc.dram_tensor("gwT", [C, NE], F32, kind="ExternalInput").ap()
    weg_d = nc.dram_tensor("weg", [NE, C, 512], F32, kind="ExternalInput").ap()
    weu_d = nc.dram_tensor("weu", [NE, C, 512], F32, kind="ExternalInput").ap()
    wed_d = nc.dram_tensor("wed", [NE, 512, C], F32, kind="ExternalInput").ap()
    wsg_d = nc.dram_tensor("wsg", [C, C], F32, kind="ExternalInput").ap()
    wsu_d = nc.dram_tensor("wsu", [C, C], F32, kind="ExternalInput").ap()
    wsd_d = nc.dram_tensor("wsd", [C, C], F32, kind="ExternalInput").ap()
    cmask_d = nc.dram_tensor("cmask", [4 * 128, 512], F32, kind="ExternalInput").ap()
    idn_d = nc.dram_tensor("idn", [128, 128], F32, kind="ExternalInput").ap()
    out_d = nc.dram_tensor("out", [T, C], F32, kind="ExternalOutput").ap()

    dump = os.environ.get("KDUMP", "0") == "1"
    dkind = dict(kind="ExternalOutput") if dump else {}
    park_d = nc.dram_tensor("park", [C, T], F32, **dkind).ap()
    qsp_d = nc.dram_tensor("qsp", [C, T], BF16, **dkind).ap()
    vsp_d = nc.dram_tensor("vsp", [4, T, 256], BF16, **dkind).ap()
    combD_d = nc.dram_tensor("combD", [NE, T], F32).ap()

    stack = ExitStack()
    with stack:
        b = Builder(nc, stack)
        sb = lambda name, shape, dt=F32: stack.enter_context(nc.sbuf_tensor(name, shape, dt))

        wbufA = sb("wbufA", [128, 12288], BF16)
        wbufB = sb("wbufB", [128, 12288], BF16)
        idn_f = sb("idn_f", [128, 128], F32)
        idn_b = sb("idn_b", [128, 128], BF16)
        ones_b = sb("ones_b", [128, 1], BF16)
        ones128 = sb("ones128", [128, 128], BF16)
        epsb = sb("epsb", [128, 1], F32)
        batqk_sb = sb("batqk_sb", [128, 16], F32)
        bproj_sb = sb("bproj_sb", [128, 8], F32)
        gwT_sb = sb("gwT_sb", [128, NCH * NE], BF16)

        PA = stack.enter_context(nc.psum_tensor("PA", [128, 2048], F32))
        PB = stack.enter_context(nc.psum_tensor("PB", [128, 2048], F32))
        PFULL = [PA, PB]
        P = [PA[:, i * 512:(i + 1) * 512] for i in range(4)] + \
            [PB[:, i * 512:(i + 1) * 512] for i in range(4)]

        b.bank_tok = [None] * 8
        pe = b.sem("pe")
        ve = b.sem("ve")
        ac = b.sem("ac")
        po = b.sem("po")
        b.own = {"vector": ve, "scalar": ac, "gpsimd": po}
        dwA = b.sem("dwA")
        dwB = b.sem("dwB")
        dini = b.sem("dini")
        dsy = b.sem("dsy")
        dsc = b.sem("dsc")
        dbc = [b.sem(f"dbc{i}") for i in range(4)]
        dln = [b.sem(f"dln{i}") for i in range(2)]
        dxc = [b.sem(f"dxc{i}") for i in range(2)]
        dqt = [b.sem(f"dqt{i}") for i in range(2)]
        dvc = [b.sem(f"dvc{i}") for i in range(6)]
        dxg = [b.sem(f"dxg{i}") for i in range(2)]
        dxl = [b.sem(f"dxl{i}") for i in range(4)]
        dvs = [b.sem(f"dvs{i}") for i in range(2)]
        dqs = [b.sem(f"dqs{i}") for i in range(2)]
        dxw = b.sem("dxw")

        def bank_w(i):
            return b.bank_tok[i]

        def bank_rel(i):
            return (("REL", i), 0)

        def allsync():
            return [(pe, b.n(pe)), (ve, b.n(ve)), (ac, b.n(ac)), (po, b.n(po))]

        FBUF = {}
        NAMEC = [0]

        b.emit("sync", lambda e: e.dma_start(out=idn_f[:], in_=idn_d[:]), incs=[(dsy, 16)])
        b.emit("sync", lambda e: e.dma_start(out=batqk_sb[:], in_=batqk_d[:]), incs=[(dsy, 16)])
        b.emit("sync", lambda e: e.dma_start(out=bproj_sb[:], in_=bproj_d[:]), incs=[(dsy, 16)])
        b.emit("gpsimd", lambda e: e.dma_start(out=idn_b[:], in_=idn_d[:]), incs=[(dini, 16)])
        b.emit("gpsimd", lambda e: e.dma_start(
            out=gwT_sb[:].rearrange("p (n f) -> p n f", n=NCH),
            in_=gwT_d[:].rearrange("(n p) f -> p n f", p=128)), incs=[(dini, 16)])
        b.emit("vector", lambda e: e.memset(ones_b[:], 1.0), incs=[(ve, 1)])
        b.emit("vector", lambda e: e.memset(ones128[:], 1.0), incs=[(ve, 1)])
        b.emit("vector", lambda e: e.memset(epsb[:], 1e-5), incs=[(ve, 1)])
        init_ready = [(dsy, b.n(dsy)), (dini, b.n(dini)), (ve, b.n(ve))]

        LNB = (6, 7)

        def ln_sums(x_view, slot, x_ready):
            xbf = FBUF["xbfc"]; xsq = FBUF["xsqc"]
            pe_now = b.n(pe)
            for c in range(NCH):
                xb = xbf[:, c * TC:(c + 1) * TC]
                xq = xsq[:, c * TC:(c + 1) * TC]
                b.emit("scalar", lambda e, c=c, xb=xb: e.copy(xb, x_view(c)),
                       waits=(list(x_ready) + [(pe, pe_now)]) if c == 0 else [], incs=[(ac, 1)])
                b.emit("scalar", lambda e, c=c, xq=xq: e.activation(xq, x_view(c), AF.Square),
                       incs=[(ac, 1)])
            ac_casts = b.n(ac)
            wl = [(ve, b.n(ve)), (ac, ac_casts), bank_w(LNB[0]), bank_w(LNB[1])]
            for c in range(NCH):
                xb = xbf[:, c * TC:(c + 1) * TC]
                xq = xsq[:, c * TC:(c + 1) * TC]
                st, sp = (c == 0), (c == NCH - 1)
                b.emit("tensor", lambda e, st=st, sp=sp, xb=xb: e.matmul(
                    P[LNB[0]][:], ones128[:], xb, start=st, stop=sp),
                    waits=wl if c == 0 else [])
                b.emit("tensor", lambda e, st=st, sp=sp, xq=xq: e.matmul(
                    P[LNB[1]][:], ones128[:], xq, start=st, stop=sp),
                    incs=[(pe, 1)] if sp else [])
            return b.n(pe), ac_casts

        def ln_rows(slot, pe_done):
            bcmr = FBUF["bcmr"]; bcrs = FBUF["bcrs"]
            bm = bcmr[:, slot * TC:(slot + 1) * TC]
            br = bcrs[:, slot * TC:(slot + 1) * TC]
            b.emit("vector", lambda e, bm=bm: e.tensor_scalar(bm, P[LNB[0]][:], 1.0 / C, None, ALU.mult),
                   waits=[(pe, pe_done)], incs=[(ve, 1), bank_rel(LNB[0])])
            g = b.n(ve)
            b.emit("vector", lambda e, bm=bm, br=br: e.tensor_tensor(br, bm, bm, ALU.mult),
                   waits=[(ve, g)], incs=[(ve, 1)])
            g = b.n(ve)
            b.emit("vector", lambda e, br=br: e.scalar_tensor_tensor(
                br, P[LNB[1]][:], 1.0 / C, br, ALU.mult, ALU.subtract),
                waits=[(ve, g)], incs=[(ve, 1), bank_rel(LNB[1])])
            ve_var = b.n(ve)
            b.emit("scalar", lambda e, br=br: e.activation(br, br, AF.Ln, bias=epsb[:, 0:1]),
                   waits=[(ve, ve_var)], incs=[(ac, 1)])
            g = b.n(ac)
            b.emit("scalar", lambda e, br=br: e.activation(br, br, AF.Exp, scale=-0.5),
                   waits=[(ac, g)], incs=[(ac, 1)])
            ac_rstd = b.n(ac)
            b.emit("vector", lambda e, bm=bm, br=br: e.scalar_tensor_tensor(
                bm, bm, -1.0, br, ALU.mult, ALU.mult),
                waits=[(ac, ac_rstd)], incs=[(ve, 1)])

        def ln_stats(x_view, gbase, slot, x_ready):
            ln_rows(slot, ln_sums(x_view, slot, x_ready)[0])

        def ln_apply(x_view, xo_view, slot, x_ready):
            tmpb = FBUF["tmpb"]; tmpb2 = FBUF["tmpb2"]
            bm = FBUF["bcmr"][:, slot * TC:(slot + 1) * TC]
            br = FBUF["bcrs"][:, slot * TC:(slot + 1) * TC]
            ready_v = b.n(ve)
            for c in range(NCH):
                eng = "vector" if c < 4 else "gpsimd"
                tb = tmpb[:, (c % 2) * TC:(c % 2 + 1) * TC] if c < 4 else tmpb2[:, 0:TC]
                sm = ve if c < 4 else po
                wl = [(ve, ready_v)] + list(x_ready) + [(pe, b.n(pe))] if c in (0, 4) else []
                if c >= 4:
                    wl = wl + [(po, b.n(po))]
                b.emit(eng, lambda e, c=c, tb=tb: e.tensor_tensor(tb, x_view(c), br, ALU.mult),
                       waits=wl, incs=[(sm, 1)])
                b.emit(eng, lambda e, c=c, tb=tb: e.tensor_tensor(xo_view(c), tb, bm, ALU.add),
                       waits=[(po, b.n(po))] if c >= 4 else [], incs=[(sm, 1)])
            return b.n(ve), b.n(po)

        def gating_topk(xln_v, half_base):
            scsig = FBUF["scsig"]; tks = FBUF["tks"]; tkm = FBUF["tkm"]
            tkmx = FBUF["tkmx"]; tkden = FBUF["tkden"]
            comb_tm = FBUF["comb_tm"]; comb_fm = FBUF["comb_fm"]
            for j in range(16):
                tcq, tt = j // 4, j % 4
                bk = j % 2
                wl = [bank_w(bk)]
                for c in range(NCH):
                    st, sp = (c == 0), (c == NCH - 1)
                    b.emit("tensor", lambda e, c=c, bk=bk, st=st, sp=sp, tcq=tcq, tt=tt: e.matmul(
                        P[bk][0:128, 0:16],
                        xln_v(c, tcq)[:, tt * 128:(tt + 1) * 128],
                        gwT_sb[:, c * NE:(c + 1) * NE], start=st, stop=sp),
                        waits=wl if c == 0 else [], incs=[(pe, 1)] if sp else [])
                g = b.n(pe)
                b.emit("scalar", lambda e, j=j, bk=bk: e.activation(
                    scsig[:, j * NE:(j + 1) * NE], P[bk][0:128, 0:16], AF.Sigmoid),
                    waits=[(pe, g)], incs=[(ac, 1), bank_rel(bk)])
            ac_sig = b.n(ac)
            v3 = lambda buf: buf.rearrange("p (j e) -> p j e", e=NE)
            mx3 = tkmx.rearrange("p (j e) -> p j e", e=1)
            mx3b = mx3.broadcast_to([128, 16, NE])
            dn3 = tkden.rearrange("p (j e) -> p j e", e=1)
            dn3b = dn3.broadcast_to([128, 16, NE])
            b.emit("vector", lambda e: e.tensor_copy(tks, scsig),
                   waits=[(ac, ac_sig)], incs=[(ve, 1)])
            for _ in range(3):
                b.emit("vector", lambda e: e.tensor_reduce(mx3, v3(tks), AX.X, ALU.max),
                       incs=[(ve, 1)])
                g = b.n(ve)
                b.emit("vector", lambda e: e.tensor_tensor(v3(tkm), v3(tks), mx3b, ALU.is_ge),
                       waits=[(ve, g)], incs=[(ve, 1)])
                b.emit("vector", lambda e: e.scalar_tensor_tensor(
                    tks, tkm, -1e30, tks, ALU.mult, ALU.add), incs=[(ve, 1)])
            b.emit("vector", lambda e: e.tensor_reduce(mx3, v3(tks), AX.X, ALU.max),
                   incs=[(ve, 1)])
            g = b.n(ve)
            b.emit("vector", lambda e: e.tensor_tensor(v3(tkm), v3(scsig), mx3b, ALU.is_ge),
                   waits=[(ve, g)], incs=[(ve, 1)])
            b.emit("vector", lambda e: e.tensor_tensor(comb_tm, scsig, tkm, ALU.mult),
                   incs=[(ve, 1)])
            b.emit("vector", lambda e: e.tensor_reduce(dn3, v3(comb_tm), AX.X, ALU.add),
                   incs=[(ve, 1)])
            g = b.n(ve)
            b.emit("vector", lambda e: e.tensor_scalar(tkden, tkden, 1e-20, None, ALU.add),
                   waits=[(ve, g)], incs=[(ve, 1)])
            g = b.n(ve)
            b.emit("vector", lambda e: e.reciprocal(tkden, tkden),
                   waits=[(ve, g)], incs=[(ve, 1)])
            g = b.n(ve)
            b.emit("vector", lambda e: e.tensor_tensor(v3(comb_tm), v3(comb_tm), dn3b, ALU.mult),
                   waits=[(ve, g)], incs=[(ve, 1)])
            b.emit("vector", lambda e: e.tensor_scalar(comb_tm, comb_tm, 0.5, None, ALU.mult),
                   incs=[(ve, 1)])
            ve_comb = b.n(ve)
            for j in range(16):
                bk = j % 2
                wl = [(ve, ve_comb), bank_w(bk)]
                if j == 0:
                    wl.append((dsc, b.n(dsc)))
                b.emit("tensor", lambda e, j=j, bk=bk: e.transpose(
                    P[bk][0:16, 0:128], comb_tm[:, j * NE:(j + 1) * NE], idn_f[:]),
                    waits=wl, incs=[(pe, 1)])
                g = b.n(pe)
                b.emit("vector", lambda e, j=j, bk=bk: e.tensor_copy(
                    comb_fm[:, j * 128:(j + 1) * 128], P[bk][0:16, 0:128]),
                    waits=[(pe, g)], incs=[(ve, 1), bank_rel(bk)])
            g = b.n(ve)
            b.emit("sync", lambda e, hb=half_base: e.dma_start(
                out=combD_d[:, hb:hb + TH], in_=comb_fm),
                waits=[(ve, g)], incs=[(dsc, 16)])
            return b.n(dsc)

        def expert_block(xln_v, x_v, cfg):
            n_i = cfg["n_i"]
            hbuf = FBUF["hbuf"]; sgb = FBUF["sgb"]; bccb = FBUF["bccb"]
            tmpb = FBUF["tmpb"]
            po_h_l = {}

            def emit_gu(tc):
                hslot = tc % 2
                if cfg["hmod"] == "comb":
                    ring = cfg["bc_state"]["n"] % 2
                    cfg["bc_state"]["n"] += 1
                    b.emit("sync", lambda e, ring=ring, tc=tc: e.dma_start(
                        out=bccb[:, ring * TC:(ring + 1) * TC],
                        in_=combD_d[cfg["e_idx"]:cfg["e_idx"] + 1,
                                    cfg["hb"] + tc * TC:cfg["hb"] + (tc + 1) * TC].broadcast_to([128, TC])),
                        waits=[(dsc, cfg["comb_at"]), (po, b.n(po))], incs=[(dbc[ring], 16)])
                    bct = bccb[:, ring * TC:(ring + 1) * TC]
                    bc_ready = (dbc[ring], b.n(dbc[ring]))
                for i in range(n_i):
                    bkG, bkU = (0, 1) if i % 2 == 0 else (2, 3)
                    wl = [bank_w(bkG), cfg["wready"]]
                    for c in range(NCH):
                        st, sp = (c == 0), (c == NCH - 1)
                        b.emit("tensor", lambda e, c=c, i=i, tc=tc, bkG=bkG, st=st, sp=sp: e.matmul(
                            P[bkG][:], cfg["wg"](c, i), xln_v(c, tc), start=st, stop=sp),
                            waits=wl if c == 0 else [], incs=[(pe, 1)] if sp else [])
                    pe_g = b.n(pe)
                    wl2 = [bank_w(bkU)]
                    if "wready2" in cfg and tc == 0 and i == 0:
                        wl2.append(cfg["wready2"])
                    for c in range(NCH):
                        st, sp = (c == 0), (c == NCH - 1)
                        b.emit("tensor", lambda e, c=c, i=i, tc=tc, bkU=bkU, st=st, sp=sp: e.matmul(
                            P[bkU][:], cfg["wu"](c, i), xln_v(c, tc), start=st, stop=sp),
                            waits=wl2 if c == 0 else [], incs=[(pe, 1)] if sp else [])
                    pe_u = b.n(pe)
                    sgt = sgb[:, (i % 2) * TC:(i % 2 + 1) * TC]
                    tmt = tmpb[:, (i % 2) * TC:(i % 2 + 1) * TC]
                    hvt = hbuf[:, (hslot * 8 + i) * TC:(hslot * 8 + i + 1) * TC]
                    b.emit("scalar", lambda e, sgt=sgt, bkG=bkG: e.activation(sgt, P[bkG][:], AF.Silu),
                           waits=[(pe, pe_g), (ve, b.n(ve))], incs=[(ac, 1), bank_rel(bkG)])
                    ac_sg = b.n(ac)
                    b.emit("vector", lambda e, sgt=sgt, tmt=tmt, bkU=bkU: e.tensor_tensor(
                        tmt, sgt, P[bkU][:], ALU.mult),
                        waits=[(pe, pe_u), (ac, ac_sg), (po, b.n(po))],
                        incs=[(ve, 1), bank_rel(bkU)])
                    ve_tm = b.n(ve)
                    if cfg["hmod"] == "comb":
                        b.emit("gpsimd", lambda e, tmt=tmt, hvt=hvt, bct=bct: e.tensor_tensor(
                            hvt, tmt, bct, ALU.mult),
                            waits=[(ve, ve_tm), bc_ready], incs=[(po, 1)])
                    else:
                        b.emit("gpsimd", lambda e, tmt=tmt, hvt=hvt: e.tensor_scalar(
                            hvt, tmt, 0.5, None, ALU.mult),
                            waits=[(ve, ve_tm)], incs=[(po, 1)])
                po_h_l[tc] = b.n(po)

            def emit_down(tc):
                hslot = tc % 2
                for c in range(NCH):
                    bkD = 4 + (c % 2)
                    wl = [bank_w(bkD)]
                    if c == 0:
                        wl.append((po, po_h_l[tc]))
                    for i in range(n_i):
                        st, sp = (i == 0), (i == n_i - 1)
                        hvt = hbuf[:, (hslot * 8 + i) * TC:(hslot * 8 + i + 1) * TC]
                        b.emit("tensor", lambda e, i=i, c=c, bkD=bkD, st=st, sp=sp, hvt=hvt: e.matmul(
                            P[bkD][:], cfg["wd"](i, c), hvt, start=st, stop=sp),
                            waits=wl if i == 0 else [], incs=[(pe, 1)] if sp else [])
                    pe_d = b.n(pe)
                    b.emit("vector", lambda e, c=c, tc=tc, bkD=bkD: e.tensor_tensor(
                        x_v(c, tc), x_v(c, tc), P[bkD][:], ALU.add),
                        waits=[(pe, pe_d)], incs=[(ve, 1), bank_rel(bkD)])
                if "on_tc_done" in cfg:
                    cfg["on_tc_done"](tc, b.n(ve))

            for tc in range(4):
                emit_gu(tc)
                if tc >= 1:
                    emit_down(tc - 1)
            emit_down(3)

        def emit_expert_loads(e_idx, wbuf, wsem, pe_gate):
            b.emit("gpsimd", lambda e, ei=e_idx, wbuf=wbuf: e.dma_start(
                out=wbuf[:, 0:4096].rearrange("p (n f) -> p n f", n=8),
                in_=weg_d[ei].rearrange("(n p) f -> p n f", p=128)),
                waits=[(pe, pe_gate)], incs=[(wsem, 16)])
            b.emit("gpsimd", lambda e, ei=e_idx, wbuf=wbuf: e.dma_start(
                out=wbuf[:, 4096:8192].rearrange("p (n f) -> p n f", n=8),
                in_=weu_d[ei].rearrange("(n p) f -> p n f", p=128)), incs=[(wsem, 16)])
            b.emit("gpsimd", lambda e, ei=e_idx, wbuf=wbuf: e.dma_start(
                out=wbuf[:, 8192:12288].rearrange("p (n f) -> p n f", n=4),
                in_=wed_d[ei].rearrange("(n p) f -> p n f", p=128)), incs=[(wsem, 16)])
            return b.n(wsem)

        def emit_sink(sink, hb, x_v, outst, ve_done, x_ready=()):
            if sink == "park":
                for c in range(NCH):
                    b.emit("sync", lambda e, c=c, hb=hb: e.dma_start(
                        out=park_d[c * 128:(c + 1) * 128, hb:hb + TH],
                        in_=x_sb_ref[0][:, c * TH:(c + 1) * TH]),
                        waits=[(ve, ve_done)] if c == 0 else [], incs=[(dsy, 16)])
            else:
                for j in range(16):
                    tcq, tt = j // 4, j % 4
                    dsy_prev = b.n(dsy)
                    for c in range(NCH):
                        bk = c % 2
                        wl = [(ve, ve_done), bank_w(bk)] + list(x_ready)
                        if c == 0 and j >= 1:
                            wl.append((dsy, dsy_prev))
                        b.emit("tensor", lambda e, c=c, tcq=tcq, tt=tt, bk=bk: e.transpose(
                            P[bk][:, 0:128],
                            x_v(c, tcq)[:, tt * 128:(tt + 1) * 128], idn_f[:]),
                            waits=wl, incs=[(pe, 1)])
                        g = b.n(pe)
                        b.emit("vector", lambda e, c=c, bk=bk, outst=outst: e.tensor_copy(
                            outst[:, c * 128:(c + 1) * 128], P[bk][:, 0:128]),
                            waits=[(pe, g)], incs=[(ve, 1), bank_rel(bk)])
                    g = b.n(ve)
                    tglob = hb + j * 128
                    b.emit("sync", lambda e, tglob=tglob, outst=outst: e.dma_start(
                        out=out_d[tglob:tglob + 128, :],
                        in_=outst.rearrange("p (c f) -> p c f", c=8)),
                        waits=[(ve, g)], incs=[(dsy, 16)])

        x_sb_ref = [None]

        def ffn_call(src, sink, compute=True):
            with ExitStack() as ps:
                NAMEC[0] += 1
                _uf = str(NAMEC[0])
                x_sb = ps.enter_context(nc.sbuf_tensor("x_ffn" + _uf, [128, NCH * TH], F32))
                xln_sb = ps.enter_context(nc.sbuf_tensor("xln_ffn" + _uf, [128, NCH * TH], BF16))
                hbuf = ps.enter_context(nc.sbuf_tensor("hbuf" + _uf, [128, 2 * 8 * TC], BF16))
                sgb = ps.enter_context(nc.sbuf_tensor("sgb" + _uf, [128, 2 * TC], F32))
                bccb = ps.enter_context(nc.sbuf_tensor("bccb" + _uf, [128, 2 * TC], F32))
                _gsc = hbuf[:, 4096:8192].bitcast(F32)
                scsig = _gsc[:, 0:256]
                tks = _gsc[:, 256:512]
                tkm = _gsc[:, 512:768]
                comb_tm = _gsc[:, 768:1024]
                tkmx = _gsc[:, 1024:1040]
                tkden = _gsc[:, 1040:1056]
                comb_fm = hbuf[:, 0:4096].bitcast(F32)[0:16, :]
                FBUF.update(dict(hbuf=hbuf, sgb=sgb, bccb=bccb, scsig=scsig, tks=tks,
                                 tkm=tkm, tkmx=tkmx, tkden=tkden, comb_tm=comb_tm,
                                 comb_fm=comb_fm))
                NAMEC[0] += 1
                _u = str(NAMEC[0])
                lnbufs = dict(
                    bcmr=ps.enter_context(nc.sbuf_tensor("bcmr" + _u, [128, 2 * TC], F32)),
                    bcrs=ps.enter_context(nc.sbuf_tensor("bcrs" + _u, [128, 2 * TC], F32)),
                    xbfc=ps.enter_context(nc.sbuf_tensor("xbfc" + _u, [128, 8 * TC], BF16)),
                    xsqc=ps.enter_context(nc.sbuf_tensor("xsqc" + _u, [128, 8 * TC], BF16)),
                    tmpb=ps.enter_context(nc.sbuf_tensor("tmpb" + _u, [128, 2 * TC], F32)),
                    tmpb2=ps.enter_context(nc.sbuf_tensor("tmpb2" + _u, [128, TC], F32)),
                )
                FBUF.update(lnbufs)
                outst = hbuf[:, 0:2048].bitcast(F32)
                x_sb_ref[0] = x_sb
                x_v = lambda c, tc: x_sb[:, c * TH + tc * TC: c * TH + (tc + 1) * TC]
                xln_v = lambda c, tc: xln_sb[:, c * TH + tc * TC: c * TH + (tc + 1) * TC]
                src_d = xT_d if src == "input" else park_d

                for half in range(2):
                    hb = half * TH
                    x_ready_tc = {}
                    for tc in range(4):
                        for c in range(NCH):
                            wl = allsync() + [(dsy, b.n(dsy)), (dxw, b.n(dxw))] + init_ready \
                                if (tc == 0 and c == 0) else []
                            b.emit("sync", lambda e, c=c, tc=tc, hb=hb: e.dma_start(
                                out=x_sb[:, c * TH + tc * TC: c * TH + (tc + 1) * TC],
                                in_=src_d[c * 128:(c + 1) * 128, hb + tc * TC: hb + (tc + 1) * TC]),
                                waits=wl, incs=[(dxl[tc], 16)])
                        x_ready_tc[tc] = [(dxl[tc], b.n(dxl[tc]))]
                    x_ready = [x_ready_tc[t][0] for t in range(4)]
                    if not compute:
                        emit_sink(sink, hb, x_v, outst, b.n(ve), x_ready)
                        continue
                    wA_at = emit_expert_loads(0, wbufA, dwA, b.n(pe))
                    wB_at = emit_expert_loads(1, wbufB, dwB, b.n(pe))
                    w_at = {0: wA_at, 1: wB_at}
                    for tc in range(4):
                        slot = tc % 2
                        ln_stats(lambda c, tc=tc: x_v(c, tc), hb + tc * TC, slot, x_ready)
                        ln_apply(lambda c, tc=tc: x_v(c, tc),
                                 lambda c, tc=tc: xln_v(c, tc), slot, x_ready)
                    b.emit("tensor", lambda e: e.matmul(
                        P[LNB[0]][0:1, 0:1], ones_b[:], ones_b[:], start=True, stop=True),
                        waits=[(ve, b.n(ve)), (po, b.n(po)), bank_w(LNB[0])], incs=[(pe, 1)])
                    g = b.n(pe)
                    b.emit("vector", lambda e, _tk=tkden: e.tensor_copy(_tk[:, 0:1], P[LNB[0]][0:128, 0:1]),
                           waits=[(pe, g)], incs=[(ve, 1), bank_rel(LNB[0])])
                    comb_at = gating_topk(xln_v, hb)
                    bc_state = {"n": 0}
                    for ei in range(NE):
                        wbuf = wbufA if ei % 2 == 0 else wbufB
                        wsem = dwA if ei % 2 == 0 else dwB
                        expert_block(xln_v, x_v, dict(
                            n_i=4, hmod="comb", e_idx=ei, hb=hb, comb_at=comb_at,
                            bc_state=bc_state, wready=(wsem, w_at[ei]),
                            wg=lambda cc, i, wbuf=wbuf: wbuf[:, cc * 512 + i * 128: cc * 512 + (i + 1) * 128],
                            wu=lambda cc, i, wbuf=wbuf: wbuf[:, 4096 + cc * 512 + i * 128: 4096 + cc * 512 + (i + 1) * 128],
                            wd=lambda i, cc, wbuf=wbuf: wbuf[:, 8192 + i * 1024 + cc * 128: 8192 + i * 1024 + (cc + 1) * 128],
                        ))
                        pe_done = b.n(pe)
                        if ei + 2 < NE:
                            tgt = ei + 2
                            w_at[tgt] = emit_expert_loads(
                                tgt, wbufA if tgt % 2 == 0 else wbufB,
                                dwA if tgt % 2 == 0 else dwB, pe_done)
                        elif ei == NE - 2:
                            b.emit("gpsimd", lambda e: e.dma_start(
                                out=wbufA[:, 0:8192].rearrange("p (n f) -> p n f", n=8),
                                in_=wsg_d[:].rearrange("(n p) f -> p n f", p=128)),
                                waits=[(pe, pe_done)], incs=[(dwA, 16)])
                            b.emit("gpsimd", lambda e: e.dma_start(
                                out=wbufA[:, 8192:12288].rearrange("p (n f) -> p n f", n=4),
                                in_=wsd_d[0:512].rearrange("(n p) f -> p n f", p=128)),
                                incs=[(dwA, 16)])
                            shA_at = b.n(dwA)
                        elif ei == NE - 1:
                            b.emit("gpsimd", lambda e: e.dma_start(
                                out=wbufB[:, 0:8192].rearrange("p (n f) -> p n f", n=8),
                                in_=wsu_d[:].rearrange("(n p) f -> p n f", p=128)),
                                waits=[(pe, pe_done)], incs=[(dwB, 16)])
                            b.emit("gpsimd", lambda e: e.dma_start(
                                out=wbufB[:, 8192:12288].rearrange("p (n f) -> p n f", n=4),
                                in_=wsd_d[512:1024].rearrange("(n p) f -> p n f", p=128)),
                                incs=[(dwB, 16)])
                            shB_at = b.n(dwB)

                    def wsd_sl(i, cc):
                        if i < 4:
                            return wbufA[:, 8192 + i * 1024 + cc * 128: 8192 + i * 1024 + (cc + 1) * 128]
                        return wbufB[:, 8192 + (i - 4) * 1024 + cc * 128: 8192 + (i - 4) * 1024 + (cc + 1) * 128]
                    expert_block(xln_v, x_v, dict(
                        n_i=8, hmod="half", wready=(dwA, shA_at), wready2=(dwB, shB_at),
                        wg=lambda cc, i: wbufA[:, cc * 1024 + i * 128: cc * 1024 + (i + 1) * 128],
                        wu=lambda cc, i: wbufB[:, cc * 1024 + i * 128: cc * 1024 + (i + 1) * 128],
                        wd=wsd_sl,
                    ))
                    ve_done = b.n(ve)
                    emit_sink(sink, hb, x_v, outst, ve_done)


        def qkv_phase(K_sb):
            with ExitStack() as ps:
                xpc = ps.enter_context(nc.sbuf_tensor("xpc", [128, 2 * 8 * TC], F32))
                xlnc = ps.enter_context(nc.sbuf_tensor("xlnc", [128, 2 * 8 * TC], BF16))
                qst = ps.enter_context(nc.sbuf_tensor("qst", [128, 2 * TC], BF16))
                vst = ps.enter_context(nc.sbuf_tensor("vst", [128, 2 * 1024], BF16))
                bvbc = ps.enter_context(nc.sbuf_tensor("bvbc", [128, 1024], F32))
                NAMEC[0] += 1
                _u = str(NAMEC[0])
                lnbufs = dict(
                    bcmr=ps.enter_context(nc.sbuf_tensor("bcmr" + _u, [128, 2 * TC], F32)),
                    bcrs=ps.enter_context(nc.sbuf_tensor("bcrs" + _u, [128, 2 * TC], F32)),
                    xbfc=ps.enter_context(nc.sbuf_tensor("xbfc" + _u, [128, 8 * TC], BF16)),
                    xsqc=ps.enter_context(nc.sbuf_tensor("xsqc" + _u, [128, 8 * TC], BF16)),
                    tmpb=ps.enter_context(nc.sbuf_tensor("tmpb" + _u, [128, 2 * TC], F32)),
                    tmpb2=ps.enter_context(nc.sbuf_tensor("tmpb2" + _u, [128, TC], F32)),
                )
                FBUF.update(lnbufs)

                pe_gate = b.n(pe)
                b.emit("gpsimd", lambda e: e.dma_start(
                    out=wbufA[:].rearrange("p (n f) -> p n f", n=4),
                    in_=wat_d[0:512].rearrange("(n p) f -> p n f", p=128)),
                    waits=[(pe, pe_gate)], incs=[(dwA, 16)])
                b.emit("gpsimd", lambda e: e.dma_start(
                    out=wbufB[:].rearrange("p (n f) -> p n f", n=4),
                    in_=wat_d[512:1024].rearrange("(n p) f -> p n f", p=128)),
                    incs=[(dwB, 16)])
                wA_at, wB_at = b.n(dwA), b.n(dwB)
                b.emit("scalar", lambda e: e.dma_start(
                    out=bvbc[:], in_=bav_d[0:1, :].broadcast_to([128, C])), incs=[(dsc, 16)])
                bv_at = b.n(dsc)

                def wat_sl(c, m):
                    buf = wbufA if c < 4 else wbufB
                    cc = c % 4
                    return buf[:, cc * 3072 + m * 128: cc * 3072 + (m + 1) * 128]

                def wat_v(c, blk):
                    buf = wbufA if c < 4 else wbufB
                    cc = c % 4
                    return buf[:, cc * 3072 + 2048 + blk * 512: cc * 3072 + 2048 + (blk + 1) * 512]

                def qkv_load(tcg, wl):
                    ring = tcg % 2
                    gb = tcg * TC
                    b.emit("gpsimd", lambda e, gb=gb, ring=ring: e.dma_start(
                        out=xpc[:, ring * 8 * TC:(ring + 1) * 8 * TC].rearrange("p (c f) -> p c f", c=8),
                        in_=park_d[:, gb:gb + TC].rearrange("(c p) f -> p c f", p=128)),
                        waits=wl, incs=[(dxc[ring], 16)])
                    return [(dxc[ring], b.n(dxc[ring]))]

                def qkv_body(tcg, x_ready):
                    ring = tcg % 2
                    gb = tcg * TC
                    xv = lambda c, ring=ring: xpc[:, (ring * 8 + c) * TC:(ring * 8 + c + 1) * TC]
                    xlv = lambda c, ring=ring: xlnc[:, (ring * 8 + c) * TC:(ring * 8 + c + 1) * TC]
                    apply_gates[tcg] = ln_apply(xv, xlv, tcg % 2, x_ready)
                    po_ln, ve_ln = b.n(po), b.n(ve)
                    for m in range(16):
                        bk = m % 2
                        wl = [bank_w(bk), (dwA, wA_at), (dwB, wB_at)]
                        if m == 0:
                            wl += [(ve, ve_ln), (po, po_ln)]
                        for c in range(NCH):
                            st, sp = (c == 0), (c == NCH - 1)
                            b.emit("tensor", lambda e, c=c, m=m, bk=bk, st=st, sp=sp, xlv=xlv: e.matmul(
                                P[bk][:], wat_sl(c, m), xlv(c), start=st, stop=sp),
                                waits=wl if c == 0 else [], incs=[(pe, 1)] if sp else [])
                        pe_m = b.n(pe)
                        if m < 8:
                            qslot = m % 2
                            b.emit("scalar", lambda e, m=m, bk=bk, qslot=qslot: e.add(
                                qst[:, qslot * TC:(qslot + 1) * TC], P[bk][:],
                                batqk_sb[:, m:m + 1]),
                                waits=[(pe, pe_m), (dqs[qslot], b.n(dqs[qslot]))],
                                incs=[(ac, 1), bank_rel(bk)])
                            g = b.n(ac)
                            b.emit("sync", lambda e, m=m, gb=gb, qslot=qslot: e.dma_start(
                                out=qsp_d[m * 128:(m + 1) * 128, gb:gb + TC],
                                in_=qst[:, qslot * TC:(qslot + 1) * TC]),
                                waits=[(ac, g)], incs=[(dqs[qslot], 16)])
                        else:
                            mk = m - 8
                            b.emit("scalar", lambda e, mk=mk, bk=bk, gb=gb: e.add(
                                K_sb[:, mk * T + gb: mk * T + gb + TC], P[bk][:],
                                batqk_sb[:, 8 + mk:9 + mk]),
                                waits=[(pe, pe_m)], incs=[(ac, 1), bank_rel(bk)])
                    for tt in range(4):
                        vslot = tt % 2
                        for blk in range(2):
                            bk = 4 + blk
                            wl = [bank_w(bk)]
                            for c in range(NCH):
                                st, sp = (c == 0), (c == NCH - 1)
                                b.emit("tensor", lambda e, c=c, tt=tt, blk=blk, bk=bk, st=st, sp=sp, xlv=xlv: e.matmul(
                                    P[bk][:], xlv(c)[:, tt * 128:(tt + 1) * 128], wat_v(c, blk),
                                    start=st, stop=sp),
                                    waits=wl if c == 0 else [], incs=[(pe, 1)] if sp else [])
                            pe_v = b.n(pe)
                            b.emit("vector", lambda e, blk=blk, bk=bk, vslot=vslot: e.tensor_tensor(
                                vst[:, vslot * 1024 + blk * 512: vslot * 1024 + (blk + 1) * 512],
                                P[bk][:], bvbc[:, blk * 512:(blk + 1) * 512], ALU.add),
                                waits=[(pe, pe_v), (dsc, bv_at), (dvs[vslot], b.n(dvs[vslot]))],
                                incs=[(ve, 1), bank_rel(bk)])
                        g = b.n(ve)
                        trow = gb + tt * 128
                        for pp in range(4):
                            b.emit("sync", lambda e, pp=pp, trow=trow, vslot=vslot: e.dma_start(
                                out=vsp_d[pp, trow:trow + 128, :],
                                in_=vst[:, vslot * 1024 + pp * 256: vslot * 1024 + (pp + 1) * 256]),
                                waits=[(ve, g)] if pp == 0 else [], incs=[(dvs[vslot], 16)])

                xrdy = {}
                apply_gates = {}
                cast_gates = {}
                xrdy[0] = qkv_load(0, [(dsy, b.n(dsy)), (pe, b.n(pe)), (po, b.n(po)),
                                       (ac, b.n(ac)), (ve, b.n(ve))])
                xrdy[1] = qkv_load(1, [(dsy, b.n(dsy)), (pe, b.n(pe)), (po, b.n(po)),
                                       (ac, b.n(ac)), (ve, b.n(ve))])
                for i in range(8):
                    xv = lambda c, i=i: xpc[:, ((i % 2) * 8 + c) * TC:((i % 2) * 8 + c + 1) * TC]
                    pe_done, cast_gates[i] = ln_sums(xv, i % 2, xrdy[i])
                    if i >= 1:
                        qkv_body(i - 1, xrdy[i - 1])
                        if i + 1 <= 7:
                            gv, gp = apply_gates[i - 1]
                            xrdy[i + 1] = qkv_load(i + 1, [
                                (ac, cast_gates[i - 1]), (ve, gv), (po, gp)])
                    ln_rows(i % 2, pe_done)
                qkv_body(7, xrdy[7])

        def attention_phase():
            with ExitStack() as ps:
                K_sb = ps.enter_context(nc.sbuf_tensor("K_sb", [128, NCH * T], BF16))
                qkv_phase(K_sb)
                cmask_sb = ps.enter_context(nc.sbuf_tensor("cmask_sb", [128, 4 * 512], BF16))
                b.emit("gpsimd", lambda e: e.dma_start(
                    out=cmask_sb[:].rearrange("p (a f) -> p a f", a=4),
                    in_=cmask_d[:].rearrange("(a p) f -> p a f", p=128)),
                    waits=[(ve, b.n(ve)), (ac, b.n(ac)), (po, b.n(po))], incs=[(dsy, 16)])
                cmask_at = b.n(dsy)
                ac_qkv = b.n(ac)
                pe_gate = b.n(pe)
                b.emit("gpsimd", lambda e: e.dma_start(
                    out=wbufA[:, 0:8192].rearrange("p (n f) -> p n f", n=8),
                    in_=wproj_d[:].rearrange("(n p) f -> p n f", p=128)),
                    waits=[(pe, pe_gate)], incs=[(dwA, 16)])
                wp_at = b.n(dwA)

                pb_sb = ps.enter_context(nc.sbuf_tensor("pb_sb", [128, 2 * T], BF16))
                pT_sb = ps.enter_context(nc.sbuf_tensor("pT_sb", [128, 32 * 256], BF16))
                av_sb = ps.enter_context(nc.sbuf_tensor("av_sb", [128, 8 * 256], BF16))
                vcol = ps.enter_context(nc.sbuf_tensor("vcol", [128, 6 * 8 * 256], BF16))
                q_sb = ps.enter_context(nc.sbuf_tensor("q_sb", [128, 2 * 8 * 128], BF16))
                xg_sb = ps.enter_context(nc.sbuf_tensor("xg_sb", [128, 8 * 256], F32))
                lsum = ps.enter_context(nc.sbuf_tensor("lsum", [128, 4], F32))
                mrow_t = ps.enter_context(nc.sbuf_tensor("mrow_t", [128, 2], F32))
                mxp = ps.enter_context(nc.sbuf_tensor("mxp", [128, 16], F32))
                lsum2 = ps.enter_context(nc.sbuf_tensor("lsum2", [128, 2], F32))

                qkv_dsy = b.n(dsy)
                qkv_spills = [(dqs[0], b.n(dqs[0])), (dqs[1], b.n(dqs[1])),
                              (dvs[0], b.n(dvs[0])), (dvs[1], b.n(dvs[1]))]

                q_at = {}

                def load_q(t):
                    ring = t % 2
                    b.emit("sync", lambda e, t=t, ring=ring: e.dma_start(
                        out=q_sb[:, ring * 1024:(ring + 1) * 1024].rearrange("p (c f) -> p c f", c=8),
                        in_=qsp_d[:, t * 128:(t + 1) * 128].rearrange("(c p) f -> p c f", p=128)),
                        waits=[(dsy, qkv_dsy), (pe, b.n(pe))] + qkv_spills,
                        incs=[(dqt[ring], 16)])
                    q_at[t] = b.n(dqt[ring])

                av_kk = [0]

                def load_vcol(pair, s0, cnt, sl, gate):
                    b.emit("sync", lambda e, pair=pair, s0=s0, cnt=cnt, sl=sl: e.dma_start(
                        out=vcol[:, sl * 2048: sl * 2048 + cnt * 256].rearrange(
                            "p (j f) -> p j f", f=256),
                        in_=vsp_d[pair, s0 * 128:(s0 + cnt) * 128, :].rearrange("(j p) f -> p j f", p=128)),
                        waits=[(dsy, qkv_dsy), (pe, gate)] + qkv_spills, incs=[(dvc[sl], 16)])
                    return b.n(dvc[sl])

                load_q(0)
                load_q(1)

                for g_i in range(16):
                    smax = 2 * g_i + 2
                    ve_pb_t = {}
                    for ti in range(2):
                        t = 2 * g_i + ti
                        ring = t % 2
                        sring = t % 2
                        nb = t // 4 + 1
                        reg = t % 2
                        nlo = min(nb, 4)
                        bmap = {}
                        for bi in range(nb):
                            tens = reg if bi < 4 else 1 - reg
                            bmap[bi] = tens * 4 + (bi % 4)
                        for bi in range(nb):
                            bk = bmap[bi]
                            wl = [bank_w(bk), (dqt[ring], q_at[t]), (ac, ac_qkv)]
                            for c in range(NCH):
                                st, sp = (c == 0), (c == NCH - 1)
                                b.emit("tensor", lambda e, c=c, bi=bi, bk=bk, st=st, sp=sp, ring=ring: e.matmul(
                                    P[bk],
                                    q_sb[:, ring * 1024 + c * 128: ring * 1024 + (c + 1) * 128],
                                    K_sb[:, c * T + bi * 512: c * T + (bi + 1) * 512],
                                    start=st, stop=sp),
                                    waits=wl if c == 0 else [], incs=[(pe, 1)] if sp else [])
                            pe_blk = b.n(pe)
                            if bi == nb - 1:
                                al = t % 4
                                b.emit("vector", lambda e, bk=bk, al=al: e.tensor_tensor(
                                    P[bk], P[bk], cmask_sb[:, al * 512:(al + 1) * 512], ALU.add),
                                    waits=[(pe, pe_blk), (dsy, cmask_at)], incs=[(ve, 1)])
                            b.emit("vector", lambda e, bk=bk, bi=bi, sring=sring: e.tensor_reduce(
                                mxp[:, sring * 8 + bi: sring * 8 + bi + 1],
                                P[bk], AX.X, ALU.max),
                                waits=[(pe, pe_blk), (ve, b.n(ve))], incs=[(ve, 1)])
                        ve_evac = b.n(ve)
                        b.emit("vector", lambda e, sring=sring, nb=nb: e.tensor_reduce(
                            mrow_t[:, sring:sring + 1], mxp[:, sring * 8: sring * 8 + nb],
                            AX.X, ALU.max),
                            waits=[(ve, ve_evac), (ac, b.n(ac))], incs=[(ve, 1)])
                        g1 = b.n(ve)
                        b.emit("vector", lambda e, sring=sring: e.tensor_scalar(
                            mrow_t[:, sring:sring + 1], mrow_t[:, sring:sring + 1],
                            -1.0 / 32.0, None, ALU.mult), waits=[(ve, g1)], incs=[(ve, 1)])
                        g2 = b.n(ve)
                        rels1 = [bank_rel(bmap[bi]) for bi in range(nlo)]
                        b.emit("scalar", lambda e, sring=sring, nlo=nlo, reg=reg: e.activation(
                            pb_sb[:, sring * T: sring * T + nlo * 512],
                            PFULL[reg][:, 0:nlo * 512], AF.Exp,
                            bias=mrow_t[:, sring:sring + 1], scale=1.0 / 32.0,
                            accum_out=lsum[:, sring:sring + 1]),
                            waits=[(ve, g2), (pe, b.n(pe))], incs=[(ac, 1)] + rels1)
                        if nb > 4:
                            nhi = nb - 4
                            rels2 = [bank_rel(bmap[bi]) for bi in range(4, nb)]
                            b.emit("scalar", lambda e, sring=sring, nhi=nhi, reg=reg: e.activation(
                                pb_sb[:, sring * T + 2048: sring * T + 2048 + nhi * 512],
                                PFULL[1 - reg][:, 0:nhi * 512], AF.Exp,
                                bias=mrow_t[:, sring:sring + 1], scale=1.0 / 32.0,
                                accum_out=lsum2[:, sring:sring + 1]),
                                incs=[(ac, 1)] + rels2)
                            g3b = b.n(ac)
                            b.emit("vector", lambda e, sring=sring: e.tensor_tensor(
                                lsum[:, sring:sring + 1], lsum[:, sring:sring + 1],
                                lsum2[:, sring:sring + 1], ALU.add),
                                waits=[(ac, g3b)], incs=[(ve, 1)])
                        g3 = b.n(ac)
                        g3v = b.n(ve)
                        b.emit("vector", lambda e, sring=sring: e.reciprocal(
                            lsum[:, 2 + sring:3 + sring], lsum[:, sring:sring + 1]),
                            waits=[(ac, g3), (ve, g3v)], incs=[(ve, 1)])
                        g4 = b.n(ve)
                        b.emit("vector", lambda e, sring=sring, nb=nb: e.tensor_scalar(
                            pb_sb[:, sring * T: sring * T + nb * 512],
                            pb_sb[:, sring * T: sring * T + nb * 512],
                            lsum[:, 2 + sring:3 + sring], None, ALU.mult),
                            waits=[(ve, g4)], incs=[(ve, 1)])
                        ve_pb_t[t] = b.n(ve)
                        nfill = smax - (t + 1)
                        if nfill > 0:
                            b.emit("vector", lambda e, t=t, ti=ti, smax=smax: e.memset(
                                pT_sb[:].rearrange("p (j q) -> p j q", q=256)[
                                    :, t + 1:smax, ti * 128:(ti + 1) * 128], 0.0),
                                incs=[(ve, 1)])
                    for ti in range(2):
                        t = 2 * g_i + ti
                        sring = t % 2
                        for j in range(t + 1):
                            bk = 6 + (j % 2)
                            b.emit("tensor", lambda e, j=j, bk=bk, sring=sring: e.transpose(
                                P[bk][:].bitcast(BF16)[:, 0:128],
                                pb_sb[:, sring * T + j * 128: sring * T + (j + 1) * 128],
                                idn_b[:]),
                                waits=[(ve, ve_pb_t[t]), bank_w(bk)], incs=[(pe, 1)])
                            g5 = b.n(pe)
                            b.emit("scalar", lambda e, j=j, bk=bk, ti=ti: e.copy(
                                pT_sb[:, j * 256 + ti * 128: j * 256 + (ti + 1) * 128],
                                P[bk][:].bitcast(BF16)[:, 0:128]),
                                waits=[(pe, g5)], incs=[(ac, 1), bank_rel(bk)])
                        if t + 2 < 32:
                            load_q(t + 2)
                    ac_tr = b.n(ac)
                    ve_fill = b.n(ve)
                    pieces = [(p0, min(8, smax - p0)) for p0 in range(0, smax, 8)]
                    npc = len(pieces)
                    slot_of = {}
                    v_at = {}
                    pe_after_pair = {}

                    def emit_vload(pair, pi, gate):
                        s0, cnt = pieces[pi]
                        sl = av_kk[0] % 6
                        av_kk[0] += 1
                        slot_of[(pair, pi)] = sl
                        v_at[(pair, pi)] = load_vcol(pair, s0, cnt, sl, gate)

                    for pi in range(npc):
                        emit_vload(0, pi, b.n(pe))
                    for pair in range(4):
                        gate0 = b.n(pe)
                        if pair + 1 < 4:
                            for pi in range(min(npc, 6 - npc)):
                                emit_vload(pair + 1, pi, gate0)
                        wl0 = [(ac, ac_tr), (ve, ve_fill)]
                        for pi in range(npc):
                            wl0.append((dvc[slot_of[(pair, pi)]], v_at[(pair, pi)]))
                        for half in range(2):
                            cb = pair * 2 + half
                            bk = 4 + (cb % 2)
                            wl = wl0 + [bank_w(bk)] if half == 0 else [bank_w(bk)]
                            for pi, (s0, cnt) in enumerate(pieces):
                                sl = slot_of[(pair, pi)]
                                for j in range(cnt):
                                    st = (pi == 0 and j == 0)
                                    sp = (pi == npc - 1 and j == cnt - 1)
                                    sj = s0 + j
                                    b.emit("tensor", lambda e, j=j, sj=sj, bk=bk, sl=sl, half=half, st=st, sp=sp: e.matmul(
                                        P[bk][:, 0:256],
                                        vcol[:, sl * 2048 + j * 256 + half * 128: sl * 2048 + j * 256 + half * 128 + 128],
                                        pT_sb[:, sj * 256:(sj + 1) * 256], start=st, stop=sp),
                                        waits=wl if (pi == 0 and j == 0) else [],
                                        incs=[(pe, 1)] if sp else [])
                            pe_av = b.n(pe)
                            b.emit("scalar", lambda e, cb=cb, bk=bk: e.copy(
                                av_sb[:, cb * 256:(cb + 1) * 256], P[bk][:, 0:256]),
                                waits=[(pe, pe_av)], incs=[(ac, 1), bank_rel(bk)])
                        if pair + 1 < 4:
                            for pi in range(min(npc, 6 - npc), npc):
                                emit_vload(pair + 1, pi, b.n(pe))
                    ac_av = b.n(ac)
                    xring = 0
                    b.emit("gpsimd", lambda e, g_i=g_i: e.dma_start(
                        out=xg_sb[:].rearrange("p (c f) -> p c f", c=8),
                        in_=park_d[:, g_i * 256:(g_i + 1) * 256].rearrange("(c p) f -> p c f", p=128)),
                        waits=[(dsy, qkv_dsy), (dxw, b.n(dxw)), (ve, b.n(ve))], incs=[(dxg[0], 16)])
                    x_at = b.n(dxg[0])
                    wp_sl = lambda cc, m: wbufA[:, cc * 1024 + m * 128: cc * 1024 + (m + 1) * 128]
                    for m in range(NCH):
                        bk = 4 + (m % 2)
                        wl = [bank_w(bk), (dwA, wp_at), (ac, ac_av)]
                        if m == 0 and "proj" in ser:
                            wl += allsync()
                        for c in range(NCH):
                            st, sp = (c == 0), (c == NCH - 1)
                            b.emit("tensor", lambda e, c=c, m=m, bk=bk, st=st, sp=sp: e.matmul(
                                P[bk][:, 0:256], wp_sl(c, m), av_sb[:, c * 256:(c + 1) * 256],
                                start=st, stop=sp),
                                waits=wl if c == 0 else [], incs=[(pe, 1)] if sp else [])
                        pe_pj = b.n(pe)
                        b.emit("vector", lambda e, m=m, bk=bk: e.scalar_tensor_tensor(
                            xg_sb[:, m * 256:(m + 1) * 256],
                            P[bk][:, 0:256], bproj_sb[:, m:m + 1],
                            xg_sb[:, m * 256:(m + 1) * 256],
                            ALU.add, ALU.add),
                            waits=[(pe, pe_pj), (dxg[0], x_at)],
                            incs=[(ve, 1), bank_rel(bk)])
                    ve_res = b.n(ve)
                    b.emit("gpsimd", lambda e, g_i=g_i: e.dma_start(
                        out=park_d[:, g_i * 256:(g_i + 1) * 256].rearrange("(c p) f -> p c f", p=128),
                        in_=xg_sb[:].rearrange("p (c f) -> p c f", c=8)),
                        waits=[(ve, ve_res)], incs=[(dxw, 16)])

        if stage == "ffn1":
            ffn_call("input", "out")
        elif stage == "attn":
            ffn_call("input", "park")
            attention_phase()
            ffn_call("park", "out", compute=False)
        else:
            ffn_call("input", "park")
            attention_phase()
            ffn_call("park", "out")

        b.emit("sync", lambda e: None,
               waits=[(dsy, b.n(dsy)), (dsc, b.n(dsc)), (dwA, b.n(dwA)), (dwB, b.n(dwB)),
                      (dxw, b.n(dxw)), (dvs[0], b.n(dvs[0])), (dvs[1], b.n(dvs[1])),
                      (dqs[0], b.n(dqs[0])), (dqs[1], b.n(dqs[1]))])

        block = stack.enter_context(nc.Block())
        b.finish(block)
    return nc


_NC_CACHE = None


def kernel(L0_attn_x, L0_attn_w_attn, L0_attn_b_attn, L0_attn_w_proj, L0_attn_b_proj,
           L0_ffn_gate_w, L0_ffn_expert_gate, L0_ffn_expert_up, L0_ffn_expert_down,
           L0_ffn_shared_gate, L0_ffn_shared_up, L0_ffn_shared_down):
    global _NC_CACHE
    x = np.asarray(L0_attn_x, np.float32)
    B = x.shape[0]
    wat = np.ascontiguousarray(np.asarray(L0_attn_w_attn, np.float32))
    b_attn = np.asarray(L0_attn_b_attn, np.float32)
    batqk = np.ascontiguousarray(b_attn[:2048].reshape(16, 128).T)
    bav = np.ascontiguousarray(b_attn[2048:].reshape(1, 1024))
    wproj = np.ascontiguousarray(np.asarray(L0_attn_w_proj, np.float32))
    bproj = np.ascontiguousarray(np.asarray(L0_attn_b_proj, np.float32).reshape(8, 128).T)
    gwT = np.ascontiguousarray(np.asarray(L0_ffn_gate_w, np.float32).T)
    weg = np.ascontiguousarray(np.asarray(L0_ffn_expert_gate, np.float32))
    weu = np.ascontiguousarray(np.asarray(L0_ffn_expert_up, np.float32))
    wed = np.ascontiguousarray(np.asarray(L0_ffn_expert_down, np.float32))
    wsg = np.ascontiguousarray(np.asarray(L0_ffn_shared_gate, np.float32))
    wsu = np.ascontiguousarray(np.asarray(L0_ffn_shared_up, np.float32))
    wsd = np.ascontiguousarray(np.asarray(L0_ffn_shared_down, np.float32))

    p = np.arange(128)[:, None]
    sl = np.arange(512)[None, :]
    cmask = np.concatenate(
        [np.where(sl <= a * 128 + p, 0.0, -1e10).astype(np.float32) for a in range(4)], axis=0)
    idn = np.eye(128, dtype=np.float32)

    if _NC_CACHE is None:
        _NC_CACHE = build_program()
    nc = _NC_CACHE

    in_maps = []
    for bb in range(B):
        in_maps.append(dict(
            xT=np.ascontiguousarray(x[bb].T),
            wat=wat, batqk=batqk, bav=bav, wproj=wproj, bproj=bproj,
            gwT=gwT, weg=weg, weu=weu, wed=wed, wsg=wsg, wsu=wsu, wsd=wsd,
            cmask=cmask, idn=idn,
        ))
    res = run_bass_kernel_spmd(nc, in_maps, core_ids=list(range(B)))
    out = np.stack([np.asarray(res.results[bb]["out"]) for bb in range(B)], axis=0)
    if os.environ.get("KDUMP", "0") == "1":
        np.save("_dump_qsp.npy", np.asarray(res.results[0]["qsp"]).view(np.uint16))
        np.save("_dump_vsp.npy", np.asarray(res.results[0]["vsp"]).view(np.uint16))
        np.save("_dump_park.npy", np.asarray(res.results[0]["park"]))
    return out.astype(np.float32)


# version 24
# speedup vs baseline: 1.0185x; 1.0045x over previous
import os
import numpy as np
from contextlib import ExitStack

import concourse.bass as bass
import concourse.mybir as mybir
from concourse.bass_utils import run_bass_kernel_spmd

F32 = mybir.dt.float32
BF16 = mybir.dt.bfloat16
AF = mybir.ActivationFunctionType
ALU = mybir.AluOpType
AX = mybir.AxisListType

T = 4096
C = 1024
TH = 2048
TC = 512
NE = 16
NCH = 8

ENGS = ("tensor", "vector", "scalar", "gpsimd", "sync")


class Builder:
    def __init__(self, nc, stack):
        self.nc = nc
        self.stack = stack
        self.ops = {e: [] for e in ENGS}
        self.cnt = {}
        self.name_of = {}

    def sem(self, name):
        h = self.stack.enter_context(self.nc.semaphore(name))
        self.cnt[name] = 0
        self.name_of[id(h)] = name
        return h

    def n(self, sem):
        return self.cnt[self.name_of[id(sem)]]

    def emit(self, eng, fn, waits=(), incs=()):
        real = [(s, k) for (s, k) in incs if not (isinstance(s, tuple) and s[0] == "REL")]
        rels = [s[1] for (s, k) in incs if isinstance(s, tuple) and s[0] == "REL"]
        assert len(real) <= 1, "at most one sync update per instruction"
        own = getattr(self, "own", {}).get(eng)
        if own is not None and not any(k >= 16 for _, k in real):
            waits = list(waits) + [(own, self.n(own))]
        for s, k in real:
            self.cnt[self.name_of[id(s)]] += k
        if rels:
            assert real, "bank release needs a real inc on the same instruction"
            rs, _ = real[0]
            tok = (rs, self.n(rs))
            for bi in rels:
                self.bank_tok[bi] = tok
        waits = tuple(w for w in waits if w is not None)
        real = tuple(real)

        def run(e):
            for s, v in waits:
                if v > 0:
                    e.wait_ge(s, v)
            ins = fn(e)
            for s, k in real:
                ins = ins.then_inc(s, k)
        self.ops[eng].append(run)

    def finish(self, block):
        for eng in ENGS:
            fns = list(self.ops[eng])
            def make(fns):
                def body(e):
                    for f in fns:
                        f(e)
                return body
            getattr(block, eng)(make(fns))


def build_program():
    stage = os.environ.get("KSTAGE", "full")
    ser = os.environ.get("KSER", "")
    nc = bass.Bass("TRN2", target_bir_lowering=False)

    xT_d = nc.dram_tensor("xT", [C, T], F32, kind="ExternalInput").ap()
    wat_d = nc.dram_tensor("wat", [C, 3 * C], F32, kind="ExternalInput").ap()
    batqk_d = nc.dram_tensor("batqk", [128, 16], F32, kind="ExternalInput").ap()
    bav_d = nc.dram_tensor("bav", [1, C], F32, kind="ExternalInput").ap()
    wproj_d = nc.dram_tensor("wproj", [C, C], F32, kind="ExternalInput").ap()
    bproj_d = nc.dram_tensor("bproj", [128, 8], F32, kind="ExternalInput").ap()
    gwT_d = nc.dram_tensor("gwT", [C, NE], F32, kind="ExternalInput").ap()
    weg_d = nc.dram_tensor("weg", [NE, C, 512], F32, kind="ExternalInput").ap()
    weu_d = nc.dram_tensor("weu", [NE, C, 512], F32, kind="ExternalInput").ap()
    wed_d = nc.dram_tensor("wed", [NE, 512, C], F32, kind="ExternalInput").ap()
    wsg_d = nc.dram_tensor("wsg", [C, C], F32, kind="ExternalInput").ap()
    wsu_d = nc.dram_tensor("wsu", [C, C], F32, kind="ExternalInput").ap()
    wsd_d = nc.dram_tensor("wsd", [C, C], F32, kind="ExternalInput").ap()
    cmask_d = nc.dram_tensor("cmask", [4 * 128, 512], F32, kind="ExternalInput").ap()
    idn_d = nc.dram_tensor("idn", [128, 128], F32, kind="ExternalInput").ap()
    out_d = nc.dram_tensor("out", [T, C], F32, kind="ExternalOutput").ap()

    dump = os.environ.get("KDUMP", "0") == "1"
    dkind = dict(kind="ExternalOutput") if dump else {}
    park_d = nc.dram_tensor("park", [C, T], F32, **dkind).ap()
    qsp_d = nc.dram_tensor("qsp", [C, T], BF16, **dkind).ap()
    vsp_d = nc.dram_tensor("vsp", [4, T, 256], BF16, **dkind).ap()
    combD_d = nc.dram_tensor("combD", [NE, T], F32).ap()

    stack = ExitStack()
    with stack:
        b = Builder(nc, stack)
        sb = lambda name, shape, dt=F32: stack.enter_context(nc.sbuf_tensor(name, shape, dt))

        wbufA = sb("wbufA", [128, 12288], BF16)
        wbufB = sb("wbufB", [128, 12288], BF16)
        idn_f = sb("idn_f", [128, 128], F32)
        idn_b = sb("idn_b", [128, 128], BF16)
        ones_b = sb("ones_b", [128, 1], BF16)
        ones128 = sb("ones128", [128, 128], BF16)
        epsb = sb("epsb", [128, 1], F32)
        batqk_sb = sb("batqk_sb", [128, 16], F32)
        bproj_sb = sb("bproj_sb", [128, 8], F32)
        gwT_sb = sb("gwT_sb", [128, NCH * NE], BF16)

        PA = stack.enter_context(nc.psum_tensor("PA", [128, 2048], F32))
        PB = stack.enter_context(nc.psum_tensor("PB", [128, 2048], F32))
        PFULL = [PA, PB]
        P = [PA[:, i * 512:(i + 1) * 512] for i in range(4)] + \
            [PB[:, i * 512:(i + 1) * 512] for i in range(4)]

        b.bank_tok = [None] * 8
        pe = b.sem("pe")
        ve = b.sem("ve")
        ac = b.sem("ac")
        po = b.sem("po")
        b.own = {"vector": ve, "scalar": ac, "gpsimd": po}
        dwA = b.sem("dwA")
        dwB = b.sem("dwB")
        dini = b.sem("dini")
        dsy = b.sem("dsy")
        dsc = b.sem("dsc")
        dbc = [b.sem(f"dbc{i}") for i in range(4)]
        dln = [b.sem(f"dln{i}") for i in range(2)]
        dxc = [b.sem(f"dxc{i}") for i in range(2)]
        dqt = [b.sem(f"dqt{i}") for i in range(2)]
        dvc = [b.sem(f"dvc{i}") for i in range(6)]
        dxg = [b.sem(f"dxg{i}") for i in range(2)]
        dxl = [b.sem(f"dxl{i}") for i in range(4)]
        dvs = [b.sem(f"dvs{i}") for i in range(2)]
        dqs = [b.sem(f"dqs{i}") for i in range(2)]
        dxw = b.sem("dxw")

        def bank_w(i):
            return b.bank_tok[i]

        def bank_rel(i):
            return (("REL", i), 0)

        def allsync():
            return [(pe, b.n(pe)), (ve, b.n(ve)), (ac, b.n(ac)), (po, b.n(po))]

        FBUF = {}
        NAMEC = [0]

        b.emit("sync", lambda e: e.dma_start(out=idn_f[:], in_=idn_d[:]), incs=[(dsy, 16)])
        b.emit("sync", lambda e: e.dma_start(out=batqk_sb[:], in_=batqk_d[:]), incs=[(dsy, 16)])
        b.emit("sync", lambda e: e.dma_start(out=bproj_sb[:], in_=bproj_d[:]), incs=[(dsy, 16)])
        b.emit("gpsimd", lambda e: e.dma_start(out=idn_b[:], in_=idn_d[:]), incs=[(dini, 16)])
        b.emit("gpsimd", lambda e: e.dma_start(
            out=gwT_sb[:].rearrange("p (n f) -> p n f", n=NCH),
            in_=gwT_d[:].rearrange("(n p) f -> p n f", p=128)), incs=[(dini, 16)])
        b.emit("vector", lambda e: e.memset(ones_b[:], 1.0), incs=[(ve, 1)])
        b.emit("vector", lambda e: e.memset(ones128[:], 1.0), incs=[(ve, 1)])
        b.emit("vector", lambda e: e.memset(epsb[:], 1e-5), incs=[(ve, 1)])
        init_ready = [(dsy, b.n(dsy)), (dini, b.n(dini)), (ve, b.n(ve))]

        LNB = (6, 7)

        def ln_sums(x_view, slot, x_ready):
            xbf = FBUF["xbfc"]; xsq = FBUF["xsqc"]
            pe_now = b.n(pe)
            for c in range(NCH):
                xb = xbf[:, c * TC:(c + 1) * TC]
                xq = xsq[:, c * TC:(c + 1) * TC]
                b.emit("scalar", lambda e, c=c, xb=xb: e.copy(xb, x_view(c)),
                       waits=(list(x_ready) + [(pe, pe_now)]) if c == 0 else [], incs=[(ac, 1)])
                b.emit("scalar", lambda e, c=c, xq=xq: e.activation(xq, x_view(c), AF.Square),
                       incs=[(ac, 1)])
            ac_casts = b.n(ac)
            wl = [(ve, b.n(ve)), (ac, ac_casts), bank_w(LNB[0]), bank_w(LNB[1])]
            for c in range(NCH):
                xb = xbf[:, c * TC:(c + 1) * TC]
                xq = xsq[:, c * TC:(c + 1) * TC]
                st, sp = (c == 0), (c == NCH - 1)
                b.emit("tensor", lambda e, st=st, sp=sp, xb=xb: e.matmul(
                    P[LNB[0]][:], ones128[:], xb, start=st, stop=sp),
                    waits=wl if c == 0 else [])
                b.emit("tensor", lambda e, st=st, sp=sp, xq=xq: e.matmul(
                    P[LNB[1]][:], ones128[:], xq, start=st, stop=sp),
                    incs=[(pe, 1)] if sp else [])
            return b.n(pe), ac_casts

        def ln_rows(slot, pe_done):
            bcmr = FBUF["bcmr"]; bcrs = FBUF["bcrs"]
            bm = bcmr[:, slot * TC:(slot + 1) * TC]
            br = bcrs[:, slot * TC:(slot + 1) * TC]
            b.emit("vector", lambda e, bm=bm: e.tensor_scalar(bm, P[LNB[0]][:], 1.0 / C, None, ALU.mult),
                   waits=[(pe, pe_done)], incs=[(ve, 1), bank_rel(LNB[0])])
            g = b.n(ve)
            b.emit("vector", lambda e, bm=bm, br=br: e.tensor_tensor(br, bm, bm, ALU.mult),
                   waits=[(ve, g)], incs=[(ve, 1)])
            g = b.n(ve)
            b.emit("vector", lambda e, br=br: e.scalar_tensor_tensor(
                br, P[LNB[1]][:], 1.0 / C, br, ALU.mult, ALU.subtract),
                waits=[(ve, g)], incs=[(ve, 1), bank_rel(LNB[1])])
            ve_var = b.n(ve)
            b.emit("scalar", lambda e, br=br: e.activation(br, br, AF.Ln, bias=epsb[:, 0:1]),
                   waits=[(ve, ve_var)], incs=[(ac, 1)])
            g = b.n(ac)
            b.emit("scalar", lambda e, br=br: e.activation(br, br, AF.Exp, scale=-0.5),
                   waits=[(ac, g)], incs=[(ac, 1)])
            ac_rstd = b.n(ac)
            b.emit("vector", lambda e, bm=bm, br=br: e.scalar_tensor_tensor(
                bm, bm, -1.0, br, ALU.mult, ALU.mult),
                waits=[(ac, ac_rstd)], incs=[(ve, 1)])

        def ln_stats(x_view, gbase, slot, x_ready):
            ln_rows(slot, ln_sums(x_view, slot, x_ready)[0])

        def ln_apply(x_view, xo_view, slot, x_ready):
            tmpb = FBUF["tmpb"]; tmpb2 = FBUF["tmpb2"]
            bm = FBUF["bcmr"][:, slot * TC:(slot + 1) * TC]
            br = FBUF["bcrs"][:, slot * TC:(slot + 1) * TC]
            ready_v = b.n(ve)
            for c in range(NCH):
                eng = "vector" if c < 4 else "gpsimd"
                tb = tmpb[:, (c % 2) * TC:(c % 2 + 1) * TC] if c < 4 else tmpb2[:, 0:TC]
                sm = ve if c < 4 else po
                wl = [(ve, ready_v)] + list(x_ready) + [(pe, b.n(pe))] if c in (0, 4) else []
                if c >= 4:
                    wl = wl + [(po, b.n(po))]
                b.emit(eng, lambda e, c=c, tb=tb: e.tensor_tensor(tb, x_view(c), br, ALU.mult),
                       waits=wl, incs=[(sm, 1)])
                b.emit(eng, lambda e, c=c, tb=tb: e.tensor_tensor(xo_view(c), tb, bm, ALU.add),
                       waits=[(po, b.n(po))] if c >= 4 else [], incs=[(sm, 1)])
            return b.n(ve), b.n(po)

        def gating_topk(xln_v, half_base):
            scsig = FBUF["scsig"]; tks = FBUF["tks"]; tkm = FBUF["tkm"]
            tkmx = FBUF["tkmx"]; tkden = FBUF["tkden"]
            comb_tm = FBUF["comb_tm"]; comb_fm = FBUF["comb_fm"]
            for j in range(16):
                tcq, tt = j // 4, j % 4
                bk = j % 2
                wl = [bank_w(bk)]
                for c in range(NCH):
                    st, sp = (c == 0), (c == NCH - 1)
                    b.emit("tensor", lambda e, c=c, bk=bk, st=st, sp=sp, tcq=tcq, tt=tt: e.matmul(
                        P[bk][0:128, 0:16],
                        xln_v(c, tcq)[:, tt * 128:(tt + 1) * 128],
                        gwT_sb[:, c * NE:(c + 1) * NE], start=st, stop=sp),
                        waits=wl if c == 0 else [], incs=[(pe, 1)] if sp else [])
                g = b.n(pe)
                b.emit("scalar", lambda e, j=j, bk=bk: e.activation(
                    scsig[:, j * NE:(j + 1) * NE], P[bk][0:128, 0:16], AF.Sigmoid),
                    waits=[(pe, g)], incs=[(ac, 1), bank_rel(bk)])
            ac_sig = b.n(ac)
            v3 = lambda buf: buf.rearrange("p (j e) -> p j e", e=NE)
            mx3 = tkmx.rearrange("p (j e) -> p j e", e=1)
            mx3b = mx3.broadcast_to([128, 16, NE])
            dn3 = tkden.rearrange("p (j e) -> p j e", e=1)
            dn3b = dn3.broadcast_to([128, 16, NE])
            b.emit("vector", lambda e: e.tensor_copy(tks, scsig),
                   waits=[(ac, ac_sig)], incs=[(ve, 1)])
            for _ in range(3):
                b.emit("vector", lambda e: e.tensor_reduce(mx3, v3(tks), AX.X, ALU.max),
                       incs=[(ve, 1)])
                g = b.n(ve)
                b.emit("vector", lambda e: e.tensor_tensor(v3(tkm), v3(tks), mx3b, ALU.is_ge),
                       waits=[(ve, g)], incs=[(ve, 1)])
                b.emit("vector", lambda e: e.scalar_tensor_tensor(
                    tks, tkm, -1e30, tks, ALU.mult, ALU.add), incs=[(ve, 1)])
            b.emit("vector", lambda e: e.tensor_reduce(mx3, v3(tks), AX.X, ALU.max),
                   incs=[(ve, 1)])
            g = b.n(ve)
            b.emit("vector", lambda e: e.tensor_tensor(v3(tkm), v3(scsig), mx3b, ALU.is_ge),
                   waits=[(ve, g)], incs=[(ve, 1)])
            b.emit("vector", lambda e: e.tensor_tensor(comb_tm, scsig, tkm, ALU.mult),
                   incs=[(ve, 1)])
            b.emit("vector", lambda e: e.tensor_reduce(dn3, v3(comb_tm), AX.X, ALU.add),
                   incs=[(ve, 1)])
            g = b.n(ve)
            b.emit("vector", lambda e: e.tensor_scalar(tkden, tkden, 1e-20, None, ALU.add),
                   waits=[(ve, g)], incs=[(ve, 1)])
            g = b.n(ve)
            b.emit("vector", lambda e: e.reciprocal(tkden, tkden),
                   waits=[(ve, g)], incs=[(ve, 1)])
            g = b.n(ve)
            b.emit("vector", lambda e: e.tensor_tensor(v3(comb_tm), v3(comb_tm), dn3b, ALU.mult),
                   waits=[(ve, g)], incs=[(ve, 1)])
            b.emit("vector", lambda e: e.tensor_scalar(comb_tm, comb_tm, 0.5, None, ALU.mult),
                   incs=[(ve, 1)])
            ve_comb = b.n(ve)
            for j in range(16):
                bk = j % 2
                wl = [(ve, ve_comb), bank_w(bk)]
                if j == 0:
                    wl.append((dsc, b.n(dsc)))
                b.emit("tensor", lambda e, j=j, bk=bk: e.transpose(
                    P[bk][0:16, 0:128], comb_tm[:, j * NE:(j + 1) * NE], idn_f[:]),
                    waits=wl, incs=[(pe, 1)])
                g = b.n(pe)
                b.emit("vector", lambda e, j=j, bk=bk: e.tensor_copy(
                    comb_fm[:, j * 128:(j + 1) * 128], P[bk][0:16, 0:128]),
                    waits=[(pe, g)], incs=[(ve, 1), bank_rel(bk)])
            g = b.n(ve)
            b.emit("sync", lambda e, hb=half_base: e.dma_start(
                out=combD_d[:, hb:hb + TH], in_=comb_fm),
                waits=[(ve, g)], incs=[(dsc, 16)])
            return b.n(dsc)

        def expert_block(xln_v, x_v, cfg):
            n_i = cfg["n_i"]
            hbuf = FBUF["hbuf"]; sgb = FBUF["sgb"]; bccb = FBUF["bccb"]
            tmpb = FBUF["tmpb"]
            po_h_l = {}

            def emit_gu(tc):
                hslot = tc % 2
                if cfg["hmod"] == "comb":
                    ring = cfg["bc_state"]["n"] % 2
                    cfg["bc_state"]["n"] += 1
                    b.emit("sync", lambda e, ring=ring, tc=tc: e.dma_start(
                        out=bccb[:, ring * TC:(ring + 1) * TC],
                        in_=combD_d[cfg["e_idx"]:cfg["e_idx"] + 1,
                                    cfg["hb"] + tc * TC:cfg["hb"] + (tc + 1) * TC].broadcast_to([128, TC])),
                        waits=[(dsc, cfg["comb_at"]), (po, b.n(po)), (dsy, b.n(dsy))],
                        incs=[(dbc[ring], 16)])
                    bct = bccb[:, ring * TC:(ring + 1) * TC]
                    bc_ready = (dbc[ring], b.n(dbc[ring]))
                for i in range(n_i):
                    bkG, bkU = (0, 1) if i % 2 == 0 else (2, 3)
                    wl = [bank_w(bkG), cfg["wready"]]
                    for c in range(NCH):
                        st, sp = (c == 0), (c == NCH - 1)
                        b.emit("tensor", lambda e, c=c, i=i, tc=tc, bkG=bkG, st=st, sp=sp: e.matmul(
                            P[bkG][:], cfg["wg"](c, i), xln_v(c, tc), start=st, stop=sp),
                            waits=wl if c == 0 else [], incs=[(pe, 1)] if sp else [])
                    pe_g = b.n(pe)
                    wl2 = [bank_w(bkU)]
                    if "wready2" in cfg and tc == 0 and i == 0:
                        wl2.append(cfg["wready2"])
                    for c in range(NCH):
                        st, sp = (c == 0), (c == NCH - 1)
                        b.emit("tensor", lambda e, c=c, i=i, tc=tc, bkU=bkU, st=st, sp=sp: e.matmul(
                            P[bkU][:], cfg["wu"](c, i), xln_v(c, tc), start=st, stop=sp),
                            waits=wl2 if c == 0 else [], incs=[(pe, 1)] if sp else [])
                    pe_u = b.n(pe)
                    sgt = sgb[:, (i % 2) * TC:(i % 2 + 1) * TC]
                    tmt = tmpb[:, (i % 2) * TC:(i % 2 + 1) * TC]
                    hvt = hbuf[:, (hslot * 8 + i) * TC:(hslot * 8 + i + 1) * TC]
                    b.emit("scalar", lambda e, sgt=sgt, bkG=bkG: e.activation(sgt, P[bkG][:], AF.Silu),
                           waits=[(pe, pe_g), (ve, b.n(ve))], incs=[(ac, 1), bank_rel(bkG)])
                    ac_sg = b.n(ac)
                    b.emit("vector", lambda e, sgt=sgt, tmt=tmt, bkU=bkU: e.tensor_tensor(
                        tmt, sgt, P[bkU][:], ALU.mult),
                        waits=[(pe, pe_u), (ac, ac_sg), (po, b.n(po))],
                        incs=[(ve, 1), bank_rel(bkU)])
                    ve_tm = b.n(ve)
                    if cfg["hmod"] == "comb":
                        b.emit("gpsimd", lambda e, tmt=tmt, hvt=hvt, bct=bct: e.tensor_tensor(
                            hvt, tmt, bct, ALU.mult),
                            waits=[(ve, ve_tm), bc_ready], incs=[(po, 1)])
                    else:
                        b.emit("gpsimd", lambda e, tmt=tmt, hvt=hvt: e.tensor_scalar(
                            hvt, tmt, 0.5, None, ALU.mult),
                            waits=[(ve, ve_tm)], incs=[(po, 1)])
                po_h_l[tc] = b.n(po)

            def emit_down(tc):
                hslot = tc % 2
                for c in range(NCH):
                    bkD = 4 + (c % 2)
                    wl = [bank_w(bkD)]
                    if c == 0:
                        wl.append((po, po_h_l[tc]))
                    for i in range(n_i):
                        st, sp = (i == 0), (i == n_i - 1)
                        hvt = hbuf[:, (hslot * 8 + i) * TC:(hslot * 8 + i + 1) * TC]
                        b.emit("tensor", lambda e, i=i, c=c, bkD=bkD, st=st, sp=sp, hvt=hvt: e.matmul(
                            P[bkD][:], cfg["wd"](i, c), hvt, start=st, stop=sp),
                            waits=wl if i == 0 else [], incs=[(pe, 1)] if sp else [])
                    pe_d = b.n(pe)
                    b.emit("vector", lambda e, c=c, tc=tc, bkD=bkD: e.tensor_tensor(
                        x_v(c, tc), x_v(c, tc), P[bkD][:], ALU.add),
                        waits=[(pe, pe_d)], incs=[(ve, 1), bank_rel(bkD)])
                if "on_tc_done" in cfg:
                    cfg["on_tc_done"](tc, b.n(ve))

            for tc in range(4):
                emit_gu(tc)
                if tc >= 1:
                    emit_down(tc - 1)
            emit_down(3)

        def emit_expert_loads(e_idx, wbuf, wsem, pe_gate):
            b.emit("gpsimd", lambda e, ei=e_idx, wbuf=wbuf: e.dma_start(
                out=wbuf[:, 0:4096].rearrange("p (n f) -> p n f", n=8),
                in_=weg_d[ei].rearrange("(n p) f -> p n f", p=128)),
                waits=[(pe, pe_gate)], incs=[(wsem, 16)])
            b.emit("gpsimd", lambda e, ei=e_idx, wbuf=wbuf: e.dma_start(
                out=wbuf[:, 4096:8192].rearrange("p (n f) -> p n f", n=8),
                in_=weu_d[ei].rearrange("(n p) f -> p n f", p=128)), incs=[(wsem, 16)])
            b.emit("gpsimd", lambda e, ei=e_idx, wbuf=wbuf: e.dma_start(
                out=wbuf[:, 8192:12288].rearrange("p (n f) -> p n f", n=4),
                in_=wed_d[ei].rearrange("(n p) f -> p n f", p=128)), incs=[(wsem, 16)])
            return b.n(wsem)

        def emit_sink(sink, hb, x_v, outst, ve_done, x_ready=()):
            if sink == "park":
                for c in range(NCH):
                    b.emit("sync", lambda e, c=c, hb=hb: e.dma_start(
                        out=park_d[c * 128:(c + 1) * 128, hb:hb + TH],
                        in_=x_sb_ref[0][:, c * TH:(c + 1) * TH]),
                        waits=[(ve, ve_done)] if c == 0 else [], incs=[(dsy, 16)])
            else:
                for j in range(16):
                    tcq, tt = j // 4, j % 4
                    dsy_prev = b.n(dsy)
                    for c in range(NCH):
                        bk = c % 2
                        wl = [(ve, ve_done), bank_w(bk)] + list(x_ready)
                        if c == 0 and j >= 1:
                            wl.append((dsy, dsy_prev))
                        b.emit("tensor", lambda e, c=c, tcq=tcq, tt=tt, bk=bk: e.transpose(
                            P[bk][:, 0:128],
                            x_v(c, tcq)[:, tt * 128:(tt + 1) * 128], idn_f[:]),
                            waits=wl, incs=[(pe, 1)])
                        g = b.n(pe)
                        b.emit("vector", lambda e, c=c, bk=bk, outst=outst: e.tensor_copy(
                            outst[:, c * 128:(c + 1) * 128], P[bk][:, 0:128]),
                            waits=[(pe, g)], incs=[(ve, 1), bank_rel(bk)])
                    g = b.n(ve)
                    tglob = hb + j * 128
                    b.emit("sync", lambda e, tglob=tglob, outst=outst: e.dma_start(
                        out=out_d[tglob:tglob + 128, :],
                        in_=outst.rearrange("p (c f) -> p c f", c=8)),
                        waits=[(ve, g)], incs=[(dsy, 16)])

        def emit_sink_tc(sink, hb, x_v, tc, ve_gate, sink_done):
            if sink == "park":
                for c in range(NCH):
                    b.emit("sync", lambda e, c=c, tc=tc, hb=hb: e.dma_start(
                        out=park_d[c * 128:(c + 1) * 128, hb + tc * TC: hb + (tc + 1) * TC],
                        in_=x_v(c, tc)),
                        waits=[(ve, ve_gate)] if c == 0 else [], incs=[(dsy, 16)])
                sink_done[tc] = (dsy, b.n(dsy))
            else:
                outst = FBUF["bccb"]
                for j in range(tc * 4, tc * 4 + 4):
                    tt = j % 4
                    dsy_prev = b.n(dsy)
                    for c in range(NCH):
                        bk = c % 2
                        wl = [(ve, ve_gate), bank_w(bk)]
                        if c == 0 and (j % 4 >= 1 or tc >= 1):
                            wl.append((dsy, dsy_prev))
                        b.emit("tensor", lambda e, c=c, tc=tc, tt=tt, bk=bk: e.transpose(
                            P[bk][:, 0:128],
                            x_v(c, tc)[:, tt * 128:(tt + 1) * 128], idn_f[:]),
                            waits=wl, incs=[(pe, 1)])
                        g = b.n(pe)
                        b.emit("vector", lambda e, c=c, bk=bk, outst=outst: e.tensor_copy(
                            outst[:, c * 128:(c + 1) * 128], P[bk][:, 0:128]),
                            waits=[(pe, g)], incs=[(ve, 1), bank_rel(bk)])
                    g = b.n(ve)
                    tglob = hb + j * 128
                    b.emit("sync", lambda e, tglob=tglob, outst=outst: e.dma_start(
                        out=out_d[tglob:tglob + 128, :],
                        in_=outst.rearrange("p (c f) -> p c f", c=8)),
                        waits=[(ve, g)], incs=[(dsy, 16)])
                sink_done[tc] = (pe, b.n(pe))

        x_sb_ref = [None]

        def ffn_call(src, sink, compute=True):
            with ExitStack() as ps:
                NAMEC[0] += 1
                _uf = str(NAMEC[0])
                x_sb = ps.enter_context(nc.sbuf_tensor("x_ffn" + _uf, [128, NCH * TH], F32))
                xln_sb = ps.enter_context(nc.sbuf_tensor("xln_ffn" + _uf, [128, NCH * TH], BF16))
                hbuf = ps.enter_context(nc.sbuf_tensor("hbuf" + _uf, [128, 2 * 8 * TC], BF16))
                sgb = ps.enter_context(nc.sbuf_tensor("sgb" + _uf, [128, 2 * TC], F32))
                bccb = ps.enter_context(nc.sbuf_tensor("bccb" + _uf, [128, 2 * TC], F32))
                _gsc = hbuf[:, 4096:8192].bitcast(F32)
                scsig = _gsc[:, 0:256]
                tks = _gsc[:, 256:512]
                tkm = _gsc[:, 512:768]
                comb_tm = _gsc[:, 768:1024]
                tkmx = _gsc[:, 1024:1040]
                tkden = _gsc[:, 1040:1056]
                comb_fm = hbuf[:, 0:4096].bitcast(F32)[0:16, :]
                FBUF.update(dict(hbuf=hbuf, sgb=sgb, bccb=bccb, scsig=scsig, tks=tks,
                                 tkm=tkm, tkmx=tkmx, tkden=tkden, comb_tm=comb_tm,
                                 comb_fm=comb_fm))
                NAMEC[0] += 1
                _u = str(NAMEC[0])
                lnbufs = dict(
                    bcmr=ps.enter_context(nc.sbuf_tensor("bcmr" + _u, [128, 2 * TC], F32)),
                    bcrs=ps.enter_context(nc.sbuf_tensor("bcrs" + _u, [128, 2 * TC], F32)),
                    xbfc=ps.enter_context(nc.sbuf_tensor("xbfc" + _u, [128, 8 * TC], BF16)),
                    xsqc=ps.enter_context(nc.sbuf_tensor("xsqc" + _u, [128, 8 * TC], BF16)),
                    tmpb=ps.enter_context(nc.sbuf_tensor("tmpb" + _u, [128, 2 * TC], F32)),
                    tmpb2=ps.enter_context(nc.sbuf_tensor("tmpb2" + _u, [128, TC], F32)),
                )
                FBUF.update(lnbufs)
                outst = hbuf[:, 0:2048].bitcast(F32)
                x_sb_ref[0] = x_sb
                x_v = lambda c, tc: x_sb[:, c * TH + tc * TC: c * TH + (tc + 1) * TC]
                xln_v = lambda c, tc: xln_sb[:, c * TH + tc * TC: c * TH + (tc + 1) * TC]
                src_d = xT_d if src == "input" else park_d

                sink_done = {}
                next_ready = {}

                def emit_half1_loads(tc):
                    for c in range(NCH):
                        wl = [sink_done[tc]] if c == 0 else []
                        b.emit("sync", lambda e, c=c, tc=tc: e.dma_start(
                            out=x_sb[:, c * TH + tc * TC: c * TH + (tc + 1) * TC],
                            in_=src_d[c * 128:(c + 1) * 128, TH + tc * TC: TH + (tc + 1) * TC]),
                            waits=wl, incs=[(dxl[tc], 16)])
                    next_ready[tc] = (dxl[tc], b.n(dxl[tc]))

                for half in range(2):
                    hb = half * TH
                    x_ready_tc = {}
                    for tc in range(4):
                        if half == 1 and compute:
                            x_ready_tc[tc] = [next_ready[tc]]
                            continue
                        for c in range(NCH):
                            eng = "sync" if c < 4 else "scalar"
                            wl = allsync() + [(dsy, b.n(dsy)), (dxw, b.n(dxw))] + init_ready \
                                if (tc == 0 and c in (0, 4)) else []
                            b.emit(eng, lambda e, c=c, tc=tc, hb=hb: e.dma_start(
                                out=x_sb[:, c * TH + tc * TC: c * TH + (tc + 1) * TC],
                                in_=src_d[c * 128:(c + 1) * 128, hb + tc * TC: hb + (tc + 1) * TC]),
                                waits=wl, incs=[(dxl[tc], 16)])
                        x_ready_tc[tc] = [(dxl[tc], b.n(dxl[tc]))]
                    x_ready = [x_ready_tc[t][0] for t in range(4)]
                    if not compute:
                        emit_sink(sink, hb, x_v, outst, b.n(ve), x_ready)
                        continue
                    wA_at = emit_expert_loads(0, wbufA, dwA, b.n(pe))
                    wB_at = emit_expert_loads(1, wbufB, dwB, b.n(pe))
                    w_at = {0: wA_at, 1: wB_at}
                    for tc in range(4):
                        slot = tc % 2
                        ln_stats(lambda c, tc=tc: x_v(c, tc), hb + tc * TC, slot, x_ready_tc[tc])
                        ln_apply(lambda c, tc=tc: x_v(c, tc),
                                 lambda c, tc=tc: xln_v(c, tc), slot, x_ready_tc[tc])
                    b.emit("tensor", lambda e: e.matmul(
                        P[LNB[0]][0:1, 0:1], ones_b[:], ones_b[:], start=True, stop=True),
                        waits=[(ve, b.n(ve)), (po, b.n(po)), bank_w(LNB[0])], incs=[(pe, 1)])
                    g = b.n(pe)
                    b.emit("vector", lambda e, _tk=tkden: e.tensor_copy(_tk[:, 0:1], P[LNB[0]][0:128, 0:1]),
                           waits=[(pe, g)], incs=[(ve, 1), bank_rel(LNB[0])])
                    comb_at = gating_topk(xln_v, hb)
                    bc_state = {"n": 0}
                    for ei in range(NE):
                        wbuf = wbufA if ei % 2 == 0 else wbufB
                        wsem = dwA if ei % 2 == 0 else dwB
                        expert_block(xln_v, x_v, dict(
                            n_i=4, hmod="comb", e_idx=ei, hb=hb, comb_at=comb_at,
                            bc_state=bc_state, wready=(wsem, w_at[ei]),
                            wg=lambda cc, i, wbuf=wbuf: wbuf[:, cc * 512 + i * 128: cc * 512 + (i + 1) * 128],
                            wu=lambda cc, i, wbuf=wbuf: wbuf[:, 4096 + cc * 512 + i * 128: 4096 + cc * 512 + (i + 1) * 128],
                            wd=lambda i, cc, wbuf=wbuf: wbuf[:, 8192 + i * 1024 + cc * 128: 8192 + i * 1024 + (cc + 1) * 128],
                        ))
                        pe_done = b.n(pe)
                        if ei + 2 < NE:
                            tgt = ei + 2
                            w_at[tgt] = emit_expert_loads(
                                tgt, wbufA if tgt % 2 == 0 else wbufB,
                                dwA if tgt % 2 == 0 else dwB, pe_done)
                        elif ei == NE - 2:
                            b.emit("gpsimd", lambda e: e.dma_start(
                                out=wbufA[:, 0:8192].rearrange("p (n f) -> p n f", n=8),
                                in_=wsg_d[:].rearrange("(n p) f -> p n f", p=128)),
                                waits=[(pe, pe_done)], incs=[(dwA, 16)])
                            b.emit("gpsimd", lambda e: e.dma_start(
                                out=wbufA[:, 8192:12288].rearrange("p (n f) -> p n f", n=4),
                                in_=wsd_d[0:512].rearrange("(n p) f -> p n f", p=128)),
                                incs=[(dwA, 16)])
                            shA_at = b.n(dwA)
                        elif ei == NE - 1:
                            b.emit("gpsimd", lambda e: e.dma_start(
                                out=wbufB[:, 0:8192].rearrange("p (n f) -> p n f", n=8),
                                in_=wsu_d[:].rearrange("(n p) f -> p n f", p=128)),
                                waits=[(pe, pe_done)], incs=[(dwB, 16)])
                            b.emit("gpsimd", lambda e: e.dma_start(
                                out=wbufB[:, 8192:12288].rearrange("p (n f) -> p n f", n=4),
                                in_=wsd_d[512:1024].rearrange("(n p) f -> p n f", p=128)),
                                incs=[(dwB, 16)])
                            shB_at = b.n(dwB)

                    def wsd_sl(i, cc):
                        if i < 4:
                            return wbufA[:, 8192 + i * 1024 + cc * 128: 8192 + i * 1024 + (cc + 1) * 128]
                        return wbufB[:, 8192 + (i - 4) * 1024 + cc * 128: 8192 + (i - 4) * 1024 + (cc + 1) * 128]
                    expert_block(xln_v, x_v, dict(
                        n_i=8, hmod="half", wready=(dwA, shA_at), wready2=(dwB, shB_at),
                        wg=lambda cc, i: wbufA[:, cc * 1024 + i * 128: cc * 1024 + (i + 1) * 128],
                        wu=lambda cc, i: wbufB[:, cc * 1024 + i * 128: cc * 1024 + (i + 1) * 128],
                        wd=wsd_sl,
                        on_tc_done=lambda tc, veg, hb=hb, half=half: (
                            emit_sink_tc(sink, hb, x_v, tc, veg, sink_done),
                            emit_half1_loads(tc) if half == 0 else None),
                    ))


        def qkv_phase(K_sb):
            with ExitStack() as ps:
                xpc = ps.enter_context(nc.sbuf_tensor("xpc", [128, 2 * 8 * TC], F32))
                xlnc = ps.enter_context(nc.sbuf_tensor("xlnc", [128, 2 * 8 * TC], BF16))
                qst = ps.enter_context(nc.sbuf_tensor("qst", [128, 2 * TC], BF16))
                vst = ps.enter_context(nc.sbuf_tensor("vst", [128, 2 * 1024], BF16))
                bvbc = ps.enter_context(nc.sbuf_tensor("bvbc", [128, 1024], F32))
                NAMEC[0] += 1
                _u = str(NAMEC[0])
                lnbufs = dict(
                    bcmr=ps.enter_context(nc.sbuf_tensor("bcmr" + _u, [128, 2 * TC], F32)),
                    bcrs=ps.enter_context(nc.sbuf_tensor("bcrs" + _u, [128, 2 * TC], F32)),
                    xbfc=ps.enter_context(nc.sbuf_tensor("xbfc" + _u, [128, 8 * TC], BF16)),
                    xsqc=ps.enter_context(nc.sbuf_tensor("xsqc" + _u, [128, 8 * TC], BF16)),
                    tmpb=ps.enter_context(nc.sbuf_tensor("tmpb" + _u, [128, 2 * TC], F32)),
                    tmpb2=ps.enter_context(nc.sbuf_tensor("tmpb2" + _u, [128, TC], F32)),
                )
                FBUF.update(lnbufs)

                pe_gate = b.n(pe)
                b.emit("gpsimd", lambda e: e.dma_start(
                    out=wbufA[:].rearrange("p (n f) -> p n f", n=4),
                    in_=wat_d[0:512].rearrange("(n p) f -> p n f", p=128)),
                    waits=[(pe, pe_gate)], incs=[(dwA, 16)])
                b.emit("gpsimd", lambda e: e.dma_start(
                    out=wbufB[:].rearrange("p (n f) -> p n f", n=4),
                    in_=wat_d[512:1024].rearrange("(n p) f -> p n f", p=128)),
                    incs=[(dwB, 16)])
                wA_at, wB_at = b.n(dwA), b.n(dwB)
                b.emit("scalar", lambda e: e.dma_start(
                    out=bvbc[:], in_=bav_d[0:1, :].broadcast_to([128, C])), incs=[(dsc, 16)])
                bv_at = b.n(dsc)

                def wat_sl(c, m):
                    buf = wbufA if c < 4 else wbufB
                    cc = c % 4
                    return buf[:, cc * 3072 + m * 128: cc * 3072 + (m + 1) * 128]

                def wat_v(c, blk):
                    buf = wbufA if c < 4 else wbufB
                    cc = c % 4
                    return buf[:, cc * 3072 + 2048 + blk * 512: cc * 3072 + 2048 + (blk + 1) * 512]

                def qkv_load(tcg, wl):
                    ring = tcg % 2
                    gb = tcg * TC
                    b.emit("gpsimd", lambda e, gb=gb, ring=ring: e.dma_start(
                        out=xpc[:, ring * 8 * TC: ring * 8 * TC + 4 * TC].rearrange("p (c f) -> p c f", c=4),
                        in_=park_d[0:512, gb:gb + TC].rearrange("(c p) f -> p c f", p=128)),
                        waits=wl, incs=[(dxc[ring], 16)])
                    b.emit("sync", lambda e, gb=gb, ring=ring: e.dma_start(
                        out=xpc[:, ring * 8 * TC + 4 * TC:(ring + 1) * 8 * TC].rearrange("p (c f) -> p c f", c=4),
                        in_=park_d[512:1024, gb:gb + TC].rearrange("(c p) f -> p c f", p=128)),
                        waits=wl, incs=[(dxc[ring], 16)])
                    return [(dxc[ring], b.n(dxc[ring]))]

                def qkv_body(tcg, x_ready):
                    ring = tcg % 2
                    gb = tcg * TC
                    xv = lambda c, ring=ring: xpc[:, (ring * 8 + c) * TC:(ring * 8 + c + 1) * TC]
                    xlv = lambda c, ring=ring: xlnc[:, (ring * 8 + c) * TC:(ring * 8 + c + 1) * TC]
                    apply_gates[tcg] = ln_apply(xv, xlv, tcg % 2, x_ready)
                    po_ln, ve_ln = b.n(po), b.n(ve)
                    for m in range(16):
                        bk = m % 2
                        wl = [bank_w(bk), (dwA, wA_at), (dwB, wB_at)]
                        if m == 0:
                            wl += [(ve, ve_ln), (po, po_ln)]
                        for c in range(NCH):
                            st, sp = (c == 0), (c == NCH - 1)
                            b.emit("tensor", lambda e, c=c, m=m, bk=bk, st=st, sp=sp, xlv=xlv: e.matmul(
                                P[bk][:], wat_sl(c, m), xlv(c), start=st, stop=sp),
                                waits=wl if c == 0 else [], incs=[(pe, 1)] if sp else [])
                        pe_m = b.n(pe)
                        if m < 8:
                            qslot = m % 2
                            b.emit("scalar", lambda e, m=m, bk=bk, qslot=qslot: e.add(
                                qst[:, qslot * TC:(qslot + 1) * TC], P[bk][:],
                                batqk_sb[:, m:m + 1]),
                                waits=[(pe, pe_m), (dqs[qslot], b.n(dqs[qslot]))],
                                incs=[(ac, 1), bank_rel(bk)])
                            g = b.n(ac)
                            b.emit("sync", lambda e, m=m, gb=gb, qslot=qslot: e.dma_start(
                                out=qsp_d[m * 128:(m + 1) * 128, gb:gb + TC],
                                in_=qst[:, qslot * TC:(qslot + 1) * TC]),
                                waits=[(ac, g)], incs=[(dqs[qslot], 16)])
                        else:
                            mk = m - 8
                            b.emit("scalar", lambda e, mk=mk, bk=bk, gb=gb: e.add(
                                K_sb[:, mk * T + gb: mk * T + gb + TC], P[bk][:],
                                batqk_sb[:, 8 + mk:9 + mk]),
                                waits=[(pe, pe_m)], incs=[(ac, 1), bank_rel(bk)])
                    for tt in range(4):
                        vslot = tt % 2
                        for blk in range(2):
                            bk = 4 + blk
                            wl = [bank_w(bk)]
                            for c in range(NCH):
                                st, sp = (c == 0), (c == NCH - 1)
                                b.emit("tensor", lambda e, c=c, tt=tt, blk=blk, bk=bk, st=st, sp=sp, xlv=xlv: e.matmul(
                                    P[bk][:], xlv(c)[:, tt * 128:(tt + 1) * 128], wat_v(c, blk),
                                    start=st, stop=sp),
                                    waits=wl if c == 0 else [], incs=[(pe, 1)] if sp else [])
                            pe_v = b.n(pe)
                            b.emit("vector", lambda e, blk=blk, bk=bk, vslot=vslot: e.tensor_tensor(
                                vst[:, vslot * 1024 + blk * 512: vslot * 1024 + (blk + 1) * 512],
                                P[bk][:], bvbc[:, blk * 512:(blk + 1) * 512], ALU.add),
                                waits=[(pe, pe_v), (dsc, bv_at), (dvs[vslot], b.n(dvs[vslot]))],
                                incs=[(ve, 1), bank_rel(bk)])
                        g = b.n(ve)
                        trow = gb + tt * 128
                        for pp in range(4):
                            b.emit("sync", lambda e, pp=pp, trow=trow, vslot=vslot: e.dma_start(
                                out=vsp_d[pp, trow:trow + 128, :],
                                in_=vst[:, vslot * 1024 + pp * 256: vslot * 1024 + (pp + 1) * 256]),
                                waits=[(ve, g)] if pp == 0 else [], incs=[(dvs[vslot], 16)])

                xrdy = {}
                apply_gates = {}
                cast_gates = {}
                xrdy[0] = qkv_load(0, [(dsy, b.n(dsy)), (pe, b.n(pe)), (po, b.n(po)),
                                       (ac, b.n(ac)), (ve, b.n(ve))])
                xrdy[1] = qkv_load(1, [(dsy, b.n(dsy)), (pe, b.n(pe)), (po, b.n(po)),
                                       (ac, b.n(ac)), (ve, b.n(ve))])
                for i in range(8):
                    xv = lambda c, i=i: xpc[:, ((i % 2) * 8 + c) * TC:((i % 2) * 8 + c + 1) * TC]
                    pe_done, cast_gates[i] = ln_sums(xv, i % 2, xrdy[i])
                    if i >= 1:
                        qkv_body(i - 1, xrdy[i - 1])
                        if i + 1 <= 7:
                            gv, gp = apply_gates[i - 1]
                            xrdy[i + 1] = qkv_load(i + 1, [
                                (ac, cast_gates[i - 1]), (ve, gv), (po, gp)])
                    ln_rows(i % 2, pe_done)
                qkv_body(7, xrdy[7])

        def attention_phase():
            with ExitStack() as ps:
                K_sb = ps.enter_context(nc.sbuf_tensor("K_sb", [128, NCH * T], BF16))
                qkv_phase(K_sb)
                cmask_sb = ps.enter_context(nc.sbuf_tensor("cmask_sb", [128, 4 * 512], BF16))
                b.emit("gpsimd", lambda e: e.dma_start(
                    out=cmask_sb[:].rearrange("p (a f) -> p a f", a=4),
                    in_=cmask_d[:].rearrange("(a p) f -> p a f", p=128)),
                    waits=[(ve, b.n(ve)), (ac, b.n(ac)), (po, b.n(po))], incs=[(dsy, 16)])
                cmask_at = b.n(dsy)
                ac_qkv = b.n(ac)
                pe_gate = b.n(pe)
                b.emit("gpsimd", lambda e: e.dma_start(
                    out=wbufA[:, 0:8192].rearrange("p (n f) -> p n f", n=8),
                    in_=wproj_d[:].rearrange("(n p) f -> p n f", p=128)),
                    waits=[(pe, pe_gate)], incs=[(dwA, 16)])
                wp_at = b.n(dwA)

                pb_sb = ps.enter_context(nc.sbuf_tensor("pb_sb", [128, 2 * T], BF16))
                pT_sb = ps.enter_context(nc.sbuf_tensor("pT_sb", [128, 32 * 256], BF16))
                av_sb = ps.enter_context(nc.sbuf_tensor("av_sb", [128, 8 * 256], BF16))
                vcol = ps.enter_context(nc.sbuf_tensor("vcol", [128, 6 * 8 * 256], BF16))
                q_sb = ps.enter_context(nc.sbuf_tensor("q_sb", [128, 2 * 8 * 128], BF16))
                xg_sb = ps.enter_context(nc.sbuf_tensor("xg_sb", [128, 8 * 256], F32))
                lsum = ps.enter_context(nc.sbuf_tensor("lsum", [128, 4], F32))
                mrow_t = ps.enter_context(nc.sbuf_tensor("mrow_t", [128, 2], F32))
                mxp = ps.enter_context(nc.sbuf_tensor("mxp", [128, 16], F32))
                lsum2 = ps.enter_context(nc.sbuf_tensor("lsum2", [128, 2], F32))

                qkv_dsy = b.n(dsy)
                qkv_spills = [(dqs[0], b.n(dqs[0])), (dqs[1], b.n(dqs[1])),
                              (dvs[0], b.n(dvs[0])), (dvs[1], b.n(dvs[1]))]

                q_at = {}

                def load_q(t):
                    ring = t % 2
                    b.emit("sync", lambda e, t=t, ring=ring: e.dma_start(
                        out=q_sb[:, ring * 1024:(ring + 1) * 1024].rearrange("p (c f) -> p c f", c=8),
                        in_=qsp_d[:, t * 128:(t + 1) * 128].rearrange("(c p) f -> p c f", p=128)),
                        waits=[(dsy, qkv_dsy), (pe, b.n(pe))] + qkv_spills,
                        incs=[(dqt[ring], 16)])
                    q_at[t] = b.n(dqt[ring])

                av_kk = [0]

                def load_vcol(pair, s0, cnt, sl, gate):
                    b.emit("sync", lambda e, pair=pair, s0=s0, cnt=cnt, sl=sl: e.dma_start(
                        out=vcol[:, sl * 2048: sl * 2048 + cnt * 256].rearrange(
                            "p (j f) -> p j f", f=256),
                        in_=vsp_d[pair, s0 * 128:(s0 + cnt) * 128, :].rearrange("(j p) f -> p j f", p=128)),
                        waits=[(dsy, qkv_dsy), (pe, gate)] + qkv_spills, incs=[(dvc[sl], 16)])
                    return b.n(dvc[sl])

                load_q(0)
                load_q(1)

                for g_i in range(16):
                    smax = 2 * g_i + 2
                    ve_pb_t = {}
                    for ti in range(2):
                        t = 2 * g_i + ti
                        ring = t % 2
                        sring = t % 2
                        nb = t // 4 + 1
                        reg = t % 2
                        nlo = min(nb, 4)
                        bmap = {}
                        for bi in range(nb):
                            tens = reg if bi < 4 else 1 - reg
                            bmap[bi] = tens * 4 + (bi % 4)
                        for bi in range(nb):
                            bk = bmap[bi]
                            wl = [bank_w(bk), (dqt[ring], q_at[t]), (ac, ac_qkv)]
                            for c in range(NCH):
                                st, sp = (c == 0), (c == NCH - 1)
                                b.emit("tensor", lambda e, c=c, bi=bi, bk=bk, st=st, sp=sp, ring=ring: e.matmul(
                                    P[bk],
                                    q_sb[:, ring * 1024 + c * 128: ring * 1024 + (c + 1) * 128],
                                    K_sb[:, c * T + bi * 512: c * T + (bi + 1) * 512],
                                    start=st, stop=sp),
                                    waits=wl if c == 0 else [], incs=[(pe, 1)] if sp else [])
                            pe_blk = b.n(pe)
                            if bi == nb - 1:
                                al = t % 4
                                b.emit("vector", lambda e, bk=bk, al=al: e.tensor_tensor(
                                    P[bk], P[bk], cmask_sb[:, al * 512:(al + 1) * 512], ALU.add),
                                    waits=[(pe, pe_blk), (dsy, cmask_at)], incs=[(ve, 1)])
                            b.emit("vector", lambda e, bk=bk, bi=bi, sring=sring: e.tensor_reduce(
                                mxp[:, sring * 8 + bi: sring * 8 + bi + 1],
                                P[bk], AX.X, ALU.max),
                                waits=[(pe, pe_blk), (ve, b.n(ve))], incs=[(ve, 1)])
                        ve_evac = b.n(ve)
                        b.emit("vector", lambda e, sring=sring, nb=nb: e.tensor_reduce(
                            mrow_t[:, sring:sring + 1], mxp[:, sring * 8: sring * 8 + nb],
                            AX.X, ALU.max),
                            waits=[(ve, ve_evac), (ac, b.n(ac))], incs=[(ve, 1)])
                        g1 = b.n(ve)
                        b.emit("vector", lambda e, sring=sring: e.tensor_scalar(
                            mrow_t[:, sring:sring + 1], mrow_t[:, sring:sring + 1],
                            -1.0 / 32.0, None, ALU.mult), waits=[(ve, g1)], incs=[(ve, 1)])
                        g2 = b.n(ve)
                        rels1 = [bank_rel(bmap[bi]) for bi in range(nlo)]
                        b.emit("scalar", lambda e, sring=sring, nlo=nlo, reg=reg: e.activation(
                            pb_sb[:, sring * T: sring * T + nlo * 512],
                            PFULL[reg][:, 0:nlo * 512], AF.Exp,
                            bias=mrow_t[:, sring:sring + 1], scale=1.0 / 32.0,
                            accum_out=lsum[:, sring:sring + 1]),
                            waits=[(ve, g2), (pe, b.n(pe))], incs=[(ac, 1)] + rels1)
                        if nb > 4:
                            nhi = nb - 4
                            rels2 = [bank_rel(bmap[bi]) for bi in range(4, nb)]
                            b.emit("scalar", lambda e, sring=sring, nhi=nhi, reg=reg: e.activation(
                                pb_sb[:, sring * T + 2048: sring * T + 2048 + nhi * 512],
                                PFULL[1 - reg][:, 0:nhi * 512], AF.Exp,
                                bias=mrow_t[:, sring:sring + 1], scale=1.0 / 32.0,
                                accum_out=lsum2[:, sring:sring + 1]),
                                incs=[(ac, 1)] + rels2)
                            g3b = b.n(ac)
                            b.emit("vector", lambda e, sring=sring: e.tensor_tensor(
                                lsum[:, sring:sring + 1], lsum[:, sring:sring + 1],
                                lsum2[:, sring:sring + 1], ALU.add),
                                waits=[(ac, g3b)], incs=[(ve, 1)])
                        g3 = b.n(ac)
                        g3v = b.n(ve)
                        b.emit("vector", lambda e, sring=sring: e.reciprocal(
                            lsum[:, 2 + sring:3 + sring], lsum[:, sring:sring + 1]),
                            waits=[(ac, g3), (ve, g3v)], incs=[(ve, 1)])
                        g4 = b.n(ve)
                        b.emit("vector", lambda e, sring=sring, nb=nb: e.tensor_scalar(
                            pb_sb[:, sring * T: sring * T + nb * 512],
                            pb_sb[:, sring * T: sring * T + nb * 512],
                            lsum[:, 2 + sring:3 + sring], None, ALU.mult),
                            waits=[(ve, g4)], incs=[(ve, 1)])
                        ve_pb_t[t] = b.n(ve)
                        nfill = smax - (t + 1)
                        if nfill > 0:
                            b.emit("vector", lambda e, t=t, ti=ti, smax=smax: e.memset(
                                pT_sb[:].rearrange("p (j q) -> p j q", q=256)[
                                    :, t + 1:smax, ti * 128:(ti + 1) * 128], 0.0),
                                incs=[(ve, 1)])
                    for ti in range(2):
                        t = 2 * g_i + ti
                        sring = t % 2
                        for j in range(t + 1):
                            bk = 6 + (j % 2)
                            b.emit("tensor", lambda e, j=j, bk=bk, sring=sring: e.transpose(
                                P[bk][:].bitcast(BF16)[:, 0:128],
                                pb_sb[:, sring * T + j * 128: sring * T + (j + 1) * 128],
                                idn_b[:]),
                                waits=[(ve, ve_pb_t[t]), bank_w(bk)], incs=[(pe, 1)])
                            g5 = b.n(pe)
                            b.emit("scalar", lambda e, j=j, bk=bk, ti=ti: e.copy(
                                pT_sb[:, j * 256 + ti * 128: j * 256 + (ti + 1) * 128],
                                P[bk][:].bitcast(BF16)[:, 0:128]),
                                waits=[(pe, g5)], incs=[(ac, 1), bank_rel(bk)])
                        if t + 2 < 32:
                            load_q(t + 2)
                    ac_tr = b.n(ac)
                    ve_fill = b.n(ve)
                    pieces = [(p0, min(8, smax - p0)) for p0 in range(0, smax, 8)]
                    npc = len(pieces)
                    slot_of = {}
                    v_at = {}
                    pe_after_pair = {}

                    def emit_vload(pair, pi, gate):
                        s0, cnt = pieces[pi]
                        sl = av_kk[0] % 6
                        av_kk[0] += 1
                        slot_of[(pair, pi)] = sl
                        v_at[(pair, pi)] = load_vcol(pair, s0, cnt, sl, gate)

                    for pi in range(npc):
                        emit_vload(0, pi, b.n(pe))
                    for pair in range(4):
                        gate0 = b.n(pe)
                        if pair + 1 < 4:
                            for pi in range(min(npc, 6 - npc)):
                                emit_vload(pair + 1, pi, gate0)
                        wl0 = [(ac, ac_tr), (ve, ve_fill)]
                        for pi in range(npc):
                            wl0.append((dvc[slot_of[(pair, pi)]], v_at[(pair, pi)]))
                        for half in range(2):
                            cb = pair * 2 + half
                            bk = 4 + (cb % 2)
                            wl = wl0 + [bank_w(bk)] if half == 0 else [bank_w(bk)]
                            for pi, (s0, cnt) in enumerate(pieces):
                                sl = slot_of[(pair, pi)]
                                for j in range(cnt):
                                    st = (pi == 0 and j == 0)
                                    sp = (pi == npc - 1 and j == cnt - 1)
                                    sj = s0 + j
                                    b.emit("tensor", lambda e, j=j, sj=sj, bk=bk, sl=sl, half=half, st=st, sp=sp: e.matmul(
                                        P[bk][:, 0:256],
                                        vcol[:, sl * 2048 + j * 256 + half * 128: sl * 2048 + j * 256 + half * 128 + 128],
                                        pT_sb[:, sj * 256:(sj + 1) * 256], start=st, stop=sp),
                                        waits=wl if (pi == 0 and j == 0) else [],
                                        incs=[(pe, 1)] if sp else [])
                            pe_av = b.n(pe)
                            b.emit("scalar", lambda e, cb=cb, bk=bk: e.copy(
                                av_sb[:, cb * 256:(cb + 1) * 256], P[bk][:, 0:256]),
                                waits=[(pe, pe_av)], incs=[(ac, 1), bank_rel(bk)])
                        if pair + 1 < 4:
                            for pi in range(min(npc, 6 - npc), npc):
                                emit_vload(pair + 1, pi, b.n(pe))
                    ac_av = b.n(ac)
                    xring = 0
                    b.emit("gpsimd", lambda e, g_i=g_i: e.dma_start(
                        out=xg_sb[:].rearrange("p (c f) -> p c f", c=8),
                        in_=park_d[:, g_i * 256:(g_i + 1) * 256].rearrange("(c p) f -> p c f", p=128)),
                        waits=[(dsy, qkv_dsy), (dxw, b.n(dxw)), (ve, b.n(ve))], incs=[(dxg[0], 16)])
                    x_at = b.n(dxg[0])
                    wp_sl = lambda cc, m: wbufA[:, cc * 1024 + m * 128: cc * 1024 + (m + 1) * 128]
                    for m in range(NCH):
                        bk = 4 + (m % 2)
                        wl = [bank_w(bk), (dwA, wp_at), (ac, ac_av)]
                        if m == 0 and "proj" in ser:
                            wl += allsync()
                        for c in range(NCH):
                            st, sp = (c == 0), (c == NCH - 1)
                            b.emit("tensor", lambda e, c=c, m=m, bk=bk, st=st, sp=sp: e.matmul(
                                P[bk][:, 0:256], wp_sl(c, m), av_sb[:, c * 256:(c + 1) * 256],
                                start=st, stop=sp),
                                waits=wl if c == 0 else [], incs=[(pe, 1)] if sp else [])
                        pe_pj = b.n(pe)
                        b.emit("vector", lambda e, m=m, bk=bk: e.scalar_tensor_tensor(
                            xg_sb[:, m * 256:(m + 1) * 256],
                            P[bk][:, 0:256], bproj_sb[:, m:m + 1],
                            xg_sb[:, m * 256:(m + 1) * 256],
                            ALU.add, ALU.add),
                            waits=[(pe, pe_pj), (dxg[0], x_at)],
                            incs=[(ve, 1), bank_rel(bk)])
                    ve_res = b.n(ve)
                    b.emit("gpsimd", lambda e, g_i=g_i: e.dma_start(
                        out=park_d[:, g_i * 256:(g_i + 1) * 256].rearrange("(c p) f -> p c f", p=128),
                        in_=xg_sb[:].rearrange("p (c f) -> p c f", c=8)),
                        waits=[(ve, ve_res)], incs=[(dxw, 16)])

        if stage == "ffn1":
            ffn_call("input", "out")
        elif stage == "attn":
            ffn_call("input", "park")
            attention_phase()
            ffn_call("park", "out", compute=False)
        else:
            ffn_call("input", "park")
            attention_phase()
            ffn_call("park", "out")

        b.emit("sync", lambda e: None,
               waits=[(dsy, b.n(dsy)), (dsc, b.n(dsc)), (dwA, b.n(dwA)), (dwB, b.n(dwB)),
                      (dxw, b.n(dxw)), (dvs[0], b.n(dvs[0])), (dvs[1], b.n(dvs[1])),
                      (dqs[0], b.n(dqs[0])), (dqs[1], b.n(dqs[1]))])

        block = stack.enter_context(nc.Block())
        b.finish(block)
    return nc


_NC_CACHE = None


def kernel(L0_attn_x, L0_attn_w_attn, L0_attn_b_attn, L0_attn_w_proj, L0_attn_b_proj,
           L0_ffn_gate_w, L0_ffn_expert_gate, L0_ffn_expert_up, L0_ffn_expert_down,
           L0_ffn_shared_gate, L0_ffn_shared_up, L0_ffn_shared_down):
    global _NC_CACHE
    x = np.asarray(L0_attn_x, np.float32)
    B = x.shape[0]
    wat = np.ascontiguousarray(np.asarray(L0_attn_w_attn, np.float32))
    b_attn = np.asarray(L0_attn_b_attn, np.float32)
    batqk = np.ascontiguousarray(b_attn[:2048].reshape(16, 128).T)
    bav = np.ascontiguousarray(b_attn[2048:].reshape(1, 1024))
    wproj = np.ascontiguousarray(np.asarray(L0_attn_w_proj, np.float32))
    bproj = np.ascontiguousarray(np.asarray(L0_attn_b_proj, np.float32).reshape(8, 128).T)
    gwT = np.ascontiguousarray(np.asarray(L0_ffn_gate_w, np.float32).T)
    weg = np.ascontiguousarray(np.asarray(L0_ffn_expert_gate, np.float32))
    weu = np.ascontiguousarray(np.asarray(L0_ffn_expert_up, np.float32))
    wed = np.ascontiguousarray(np.asarray(L0_ffn_expert_down, np.float32))
    wsg = np.ascontiguousarray(np.asarray(L0_ffn_shared_gate, np.float32))
    wsu = np.ascontiguousarray(np.asarray(L0_ffn_shared_up, np.float32))
    wsd = np.ascontiguousarray(np.asarray(L0_ffn_shared_down, np.float32))

    p = np.arange(128)[:, None]
    sl = np.arange(512)[None, :]
    cmask = np.concatenate(
        [np.where(sl <= a * 128 + p, 0.0, -1e10).astype(np.float32) for a in range(4)], axis=0)
    idn = np.eye(128, dtype=np.float32)

    if _NC_CACHE is None:
        _NC_CACHE = build_program()
    nc = _NC_CACHE

    in_maps = []
    for bb in range(B):
        in_maps.append(dict(
            xT=np.ascontiguousarray(x[bb].T),
            wat=wat, batqk=batqk, bav=bav, wproj=wproj, bproj=bproj,
            gwT=gwT, weg=weg, weu=weu, wed=wed, wsg=wsg, wsu=wsu, wsd=wsd,
            cmask=cmask, idn=idn,
        ))
    res = run_bass_kernel_spmd(nc, in_maps, core_ids=list(range(B)))
    out = np.stack([np.asarray(res.results[bb]["out"]) for bb in range(B)], axis=0)
    if os.environ.get("KDUMP", "0") == "1":
        np.save("_dump_qsp.npy", np.asarray(res.results[0]["qsp"]).view(np.uint16))
        np.save("_dump_vsp.npy", np.asarray(res.results[0]["vsp"]).view(np.uint16))
        np.save("_dump_park.npy", np.asarray(res.results[0]["park"]))
    return out.astype(np.float32)


# version 25
# speedup vs baseline: 1.0207x; 1.0021x over previous
import os
import numpy as np
from contextlib import ExitStack

import concourse.bass as bass
import concourse.mybir as mybir
from concourse.bass_utils import run_bass_kernel_spmd

F32 = mybir.dt.float32
BF16 = mybir.dt.bfloat16
AF = mybir.ActivationFunctionType
ALU = mybir.AluOpType
AX = mybir.AxisListType

T = 4096
C = 1024
TH = 2048
TC = 512
NE = 16
NCH = 8

ENGS = ("tensor", "vector", "scalar", "gpsimd", "sync")


class Builder:
    def __init__(self, nc, stack):
        self.nc = nc
        self.stack = stack
        self.ops = {e: [] for e in ENGS}
        self.cnt = {}
        self.name_of = {}

    def sem(self, name):
        h = self.stack.enter_context(self.nc.semaphore(name))
        self.cnt[name] = 0
        self.name_of[id(h)] = name
        return h

    def n(self, sem):
        return self.cnt[self.name_of[id(sem)]]

    def emit(self, eng, fn, waits=(), incs=()):
        real = [(s, k) for (s, k) in incs if not (isinstance(s, tuple) and s[0] == "REL")]
        rels = [s[1] for (s, k) in incs if isinstance(s, tuple) and s[0] == "REL"]
        assert len(real) <= 1, "at most one sync update per instruction"
        own = getattr(self, "own", {}).get(eng)
        if own is not None and not any(k >= 16 for _, k in real):
            waits = list(waits) + [(own, self.n(own))]
        for s, k in real:
            self.cnt[self.name_of[id(s)]] += k
        if rels:
            assert real, "bank release needs a real inc on the same instruction"
            rs, _ = real[0]
            tok = (rs, self.n(rs))
            for bi in rels:
                self.bank_tok[bi] = tok
        waits = tuple(w for w in waits if w is not None)
        real = tuple(real)

        def run(e):
            for s, v in waits:
                if v > 0:
                    e.wait_ge(s, v)
            ins = fn(e)
            for s, k in real:
                ins = ins.then_inc(s, k)
        self.ops[eng].append(run)

    def finish(self, block):
        for eng in ENGS:
            fns = list(self.ops[eng])
            def make(fns):
                def body(e):
                    for f in fns:
                        f(e)
                return body
            getattr(block, eng)(make(fns))


def build_program():
    stage = os.environ.get("KSTAGE", "full")
    ser = os.environ.get("KSER", "")
    nc = bass.Bass("TRN2", target_bir_lowering=False)

    xT_d = nc.dram_tensor("xT", [C, T], F32, kind="ExternalInput").ap()
    wat_d = nc.dram_tensor("wat", [C, 3 * C], F32, kind="ExternalInput").ap()
    batqk_d = nc.dram_tensor("batqk", [128, 16], F32, kind="ExternalInput").ap()
    bav_d = nc.dram_tensor("bav", [1, C], F32, kind="ExternalInput").ap()
    wproj_d = nc.dram_tensor("wproj", [C, C], F32, kind="ExternalInput").ap()
    bproj_d = nc.dram_tensor("bproj", [128, 8], F32, kind="ExternalInput").ap()
    gwT_d = nc.dram_tensor("gwT", [C, NE], F32, kind="ExternalInput").ap()
    weg_d = nc.dram_tensor("weg", [NE, C, 512], F32, kind="ExternalInput").ap()
    weu_d = nc.dram_tensor("weu", [NE, C, 512], F32, kind="ExternalInput").ap()
    wed_d = nc.dram_tensor("wed", [NE, 512, C], F32, kind="ExternalInput").ap()
    wsg_d = nc.dram_tensor("wsg", [C, C], F32, kind="ExternalInput").ap()
    wsu_d = nc.dram_tensor("wsu", [C, C], F32, kind="ExternalInput").ap()
    wsd_d = nc.dram_tensor("wsd", [C, C], F32, kind="ExternalInput").ap()
    cmask_d = nc.dram_tensor("cmask", [4 * 128, 512], F32, kind="ExternalInput").ap()
    idn_d = nc.dram_tensor("idn", [128, 128], F32, kind="ExternalInput").ap()
    out_d = nc.dram_tensor("out", [T, C], F32, kind="ExternalOutput").ap()

    dump = os.environ.get("KDUMP", "0") == "1"
    dkind = dict(kind="ExternalOutput") if dump else {}
    park_d = nc.dram_tensor("park", [C, T], F32, **dkind).ap()
    qsp_d = nc.dram_tensor("qsp", [C, T], BF16, **dkind).ap()
    vsp_d = nc.dram_tensor("vsp", [4, T, 256], BF16, **dkind).ap()
    combD_d = nc.dram_tensor("combD", [NE, T], F32).ap()

    stack = ExitStack()
    with stack:
        b = Builder(nc, stack)
        sb = lambda name, shape, dt=F32: stack.enter_context(nc.sbuf_tensor(name, shape, dt))

        wbufA = sb("wbufA", [128, 12288], BF16)
        wbufB = sb("wbufB", [128, 12288], BF16)
        idn_f = sb("idn_f", [128, 128], F32)
        idn_b = sb("idn_b", [128, 128], BF16)
        ones_b = sb("ones_b", [128, 1], BF16)
        ones128 = sb("ones128", [128, 128], BF16)
        epsb = sb("epsb", [128, 1], F32)
        batqk_sb = sb("batqk_sb", [128, 16], F32)
        bproj_sb = sb("bproj_sb", [128, 8], F32)
        gwT_sb = sb("gwT_sb", [128, NCH * NE], BF16)

        PA = stack.enter_context(nc.psum_tensor("PA", [128, 2048], F32))
        PB = stack.enter_context(nc.psum_tensor("PB", [128, 2048], F32))
        PFULL = [PA, PB]
        P = [PA[:, i * 512:(i + 1) * 512] for i in range(4)] + \
            [PB[:, i * 512:(i + 1) * 512] for i in range(4)]

        b.bank_tok = [None] * 8
        pe = b.sem("pe")
        ve = b.sem("ve")
        ac = b.sem("ac")
        po = b.sem("po")
        b.own = {"vector": ve, "scalar": ac, "gpsimd": po}
        dwA = b.sem("dwA")
        dwB = b.sem("dwB")
        dini = b.sem("dini")
        dsy = b.sem("dsy")
        dsc = b.sem("dsc")
        dbc = [b.sem(f"dbc{i}") for i in range(4)]
        dln = [b.sem(f"dln{i}") for i in range(2)]
        dxc = [b.sem(f"dxc{i}") for i in range(2)]
        dqt = [b.sem(f"dqt{i}") for i in range(2)]
        dvc = [b.sem(f"dvc{i}") for i in range(6)]
        dxg = [b.sem(f"dxg{i}") for i in range(2)]
        dxl = [b.sem(f"dxl{i}") for i in range(4)]
        dvs = [b.sem(f"dvs{i}") for i in range(2)]
        dqs = [b.sem(f"dqs{i}") for i in range(2)]
        dxw = b.sem("dxw")

        def bank_w(i):
            return b.bank_tok[i]

        def bank_rel(i):
            return (("REL", i), 0)

        def allsync():
            return [(pe, b.n(pe)), (ve, b.n(ve)), (ac, b.n(ac)), (po, b.n(po))]

        FBUF = {}
        NAMEC = [0]

        b.emit("sync", lambda e: e.dma_start(out=idn_f[:], in_=idn_d[:]), incs=[(dsy, 16)])
        b.emit("sync", lambda e: e.dma_start(out=batqk_sb[:], in_=batqk_d[:]), incs=[(dsy, 16)])
        b.emit("sync", lambda e: e.dma_start(out=bproj_sb[:], in_=bproj_d[:]), incs=[(dsy, 16)])
        b.emit("gpsimd", lambda e: e.dma_start(out=idn_b[:], in_=idn_d[:]), incs=[(dini, 16)])
        b.emit("gpsimd", lambda e: e.dma_start(
            out=gwT_sb[:].rearrange("p (n f) -> p n f", n=NCH),
            in_=gwT_d[:].rearrange("(n p) f -> p n f", p=128)), incs=[(dini, 16)])
        b.emit("vector", lambda e: e.memset(ones_b[:], 1.0), incs=[(ve, 1)])
        b.emit("vector", lambda e: e.memset(ones128[:], 1.0), incs=[(ve, 1)])
        b.emit("vector", lambda e: e.memset(epsb[:], 1e-5), incs=[(ve, 1)])
        init_ready = [(dsy, b.n(dsy)), (dini, b.n(dini)), (ve, b.n(ve))]

        LNB = (6, 7)

        def ln_sums(x_view, slot, x_ready):
            xbf = FBUF["xbfc"]; xsq = FBUF["xsqc"]
            pe_now = b.n(pe)
            for c in range(NCH):
                xb = xbf[:, c * TC:(c + 1) * TC]
                xq = xsq[:, c * TC:(c + 1) * TC]
                b.emit("scalar", lambda e, c=c, xb=xb: e.copy(xb, x_view(c)),
                       waits=(list(x_ready) + [(pe, pe_now)]) if c == 0 else [], incs=[(ac, 1)])
                b.emit("scalar", lambda e, c=c, xq=xq: e.activation(xq, x_view(c), AF.Square),
                       incs=[(ac, 1)])
            ac_casts = b.n(ac)
            wl = [(ve, b.n(ve)), (ac, ac_casts), bank_w(LNB[0]), bank_w(LNB[1])]
            for c in range(NCH):
                xb = xbf[:, c * TC:(c + 1) * TC]
                xq = xsq[:, c * TC:(c + 1) * TC]
                st, sp = (c == 0), (c == NCH - 1)
                b.emit("tensor", lambda e, st=st, sp=sp, xb=xb: e.matmul(
                    P[LNB[0]][:], ones128[:], xb, start=st, stop=sp),
                    waits=wl if c == 0 else [])
                b.emit("tensor", lambda e, st=st, sp=sp, xq=xq: e.matmul(
                    P[LNB[1]][:], ones128[:], xq, start=st, stop=sp),
                    incs=[(pe, 1)] if sp else [])
            return b.n(pe), ac_casts

        def ln_rows(slot, pe_done):
            bcmr = FBUF["bcmr"]; bcrs = FBUF["bcrs"]
            bm = bcmr[:, slot * TC:(slot + 1) * TC]
            br = bcrs[:, slot * TC:(slot + 1) * TC]
            b.emit("vector", lambda e, bm=bm: e.tensor_scalar(bm, P[LNB[0]][:], 1.0 / C, None, ALU.mult),
                   waits=[(pe, pe_done)], incs=[(ve, 1), bank_rel(LNB[0])])
            g = b.n(ve)
            b.emit("vector", lambda e, bm=bm, br=br: e.tensor_tensor(br, bm, bm, ALU.mult),
                   waits=[(ve, g)], incs=[(ve, 1)])
            g = b.n(ve)
            b.emit("vector", lambda e, br=br: e.scalar_tensor_tensor(
                br, P[LNB[1]][:], 1.0 / C, br, ALU.mult, ALU.subtract),
                waits=[(ve, g)], incs=[(ve, 1), bank_rel(LNB[1])])
            ve_var = b.n(ve)
            b.emit("scalar", lambda e, br=br: e.activation(br, br, AF.Ln, bias=epsb[:, 0:1]),
                   waits=[(ve, ve_var)], incs=[(ac, 1)])
            g = b.n(ac)
            b.emit("scalar", lambda e, br=br: e.activation(br, br, AF.Exp, scale=-0.5),
                   waits=[(ac, g)], incs=[(ac, 1)])
            ac_rstd = b.n(ac)
            b.emit("vector", lambda e, bm=bm, br=br: e.scalar_tensor_tensor(
                bm, bm, -1.0, br, ALU.mult, ALU.mult),
                waits=[(ac, ac_rstd)], incs=[(ve, 1)])

        def ln_stats(x_view, gbase, slot, x_ready):
            ln_rows(slot, ln_sums(x_view, slot, x_ready)[0])

        def ln_apply(x_view, xo_view, slot, x_ready):
            tmpb = FBUF["tmpb"]; tmpb2 = FBUF["tmpb2"]
            bm = FBUF["bcmr"][:, slot * TC:(slot + 1) * TC]
            br = FBUF["bcrs"][:, slot * TC:(slot + 1) * TC]
            ready_v = b.n(ve)
            for c in range(NCH):
                eng = "vector" if c < 4 else "gpsimd"
                tb = tmpb[:, (c % 2) * TC:(c % 2 + 1) * TC] if c < 4 else tmpb2[:, 0:TC]
                sm = ve if c < 4 else po
                wl = [(ve, ready_v)] + list(x_ready) + [(pe, b.n(pe))] if c in (0, 4) else []
                if c >= 4:
                    wl = wl + [(po, b.n(po))]
                b.emit(eng, lambda e, c=c, tb=tb: e.tensor_tensor(tb, x_view(c), br, ALU.mult),
                       waits=wl, incs=[(sm, 1)])
                b.emit(eng, lambda e, c=c, tb=tb: e.tensor_tensor(xo_view(c), tb, bm, ALU.add),
                       waits=[(po, b.n(po))] if c >= 4 else [], incs=[(sm, 1)])
            return b.n(ve), b.n(po)

        def gating_topk(xln_v, half_base):
            scsig = FBUF["scsig"]; tks = FBUF["tks"]; tkm = FBUF["tkm"]
            tkmx = FBUF["tkmx"]; tkden = FBUF["tkden"]
            comb_tm = FBUF["comb_tm"]; comb_fm = FBUF["comb_fm"]
            for j in range(16):
                tcq, tt = j // 4, j % 4
                bk = j % 2
                wl = [bank_w(bk)]
                for c in range(NCH):
                    st, sp = (c == 0), (c == NCH - 1)
                    b.emit("tensor", lambda e, c=c, bk=bk, st=st, sp=sp, tcq=tcq, tt=tt: e.matmul(
                        P[bk][0:128, 0:16],
                        xln_v(c, tcq)[:, tt * 128:(tt + 1) * 128],
                        gwT_sb[:, c * NE:(c + 1) * NE], start=st, stop=sp),
                        waits=wl if c == 0 else [], incs=[(pe, 1)] if sp else [])
                g = b.n(pe)
                b.emit("scalar", lambda e, j=j, bk=bk: e.activation(
                    scsig[:, j * NE:(j + 1) * NE], P[bk][0:128, 0:16], AF.Sigmoid),
                    waits=[(pe, g)], incs=[(ac, 1), bank_rel(bk)])
            ac_sig = b.n(ac)
            v3 = lambda buf: buf.rearrange("p (j e) -> p j e", e=NE)
            mx3 = tkmx.rearrange("p (j e) -> p j e", e=1)
            mx3b = mx3.broadcast_to([128, 16, NE])
            dn3 = tkden.rearrange("p (j e) -> p j e", e=1)
            dn3b = dn3.broadcast_to([128, 16, NE])
            b.emit("vector", lambda e: e.tensor_copy(tks, scsig),
                   waits=[(ac, ac_sig)], incs=[(ve, 1)])
            for _ in range(3):
                b.emit("vector", lambda e: e.tensor_reduce(mx3, v3(tks), AX.X, ALU.max),
                       incs=[(ve, 1)])
                g = b.n(ve)
                b.emit("vector", lambda e: e.tensor_tensor(v3(tkm), v3(tks), mx3b, ALU.is_ge),
                       waits=[(ve, g)], incs=[(ve, 1)])
                b.emit("vector", lambda e: e.scalar_tensor_tensor(
                    tks, tkm, -1e30, tks, ALU.mult, ALU.add), incs=[(ve, 1)])
            b.emit("vector", lambda e: e.tensor_reduce(mx3, v3(tks), AX.X, ALU.max),
                   incs=[(ve, 1)])
            g = b.n(ve)
            b.emit("vector", lambda e: e.tensor_tensor(v3(tkm), v3(scsig), mx3b, ALU.is_ge),
                   waits=[(ve, g)], incs=[(ve, 1)])
            b.emit("vector", lambda e: e.tensor_tensor(comb_tm, scsig, tkm, ALU.mult),
                   incs=[(ve, 1)])
            b.emit("vector", lambda e: e.tensor_reduce(dn3, v3(comb_tm), AX.X, ALU.add),
                   incs=[(ve, 1)])
            g = b.n(ve)
            b.emit("vector", lambda e: e.tensor_scalar(tkden, tkden, 1e-20, None, ALU.add),
                   waits=[(ve, g)], incs=[(ve, 1)])
            g = b.n(ve)
            b.emit("vector", lambda e: e.reciprocal(tkden, tkden),
                   waits=[(ve, g)], incs=[(ve, 1)])
            g = b.n(ve)
            b.emit("vector", lambda e: e.tensor_tensor(v3(comb_tm), v3(comb_tm), dn3b, ALU.mult),
                   waits=[(ve, g)], incs=[(ve, 1)])
            b.emit("vector", lambda e: e.tensor_scalar(comb_tm, comb_tm, 0.5, None, ALU.mult),
                   incs=[(ve, 1)])
            ve_comb = b.n(ve)
            for j in range(16):
                bk = j % 2
                wl = [(ve, ve_comb), bank_w(bk)]
                if j == 0:
                    wl.append((dsc, b.n(dsc)))
                b.emit("tensor", lambda e, j=j, bk=bk: e.transpose(
                    P[bk][0:16, 0:128], comb_tm[:, j * NE:(j + 1) * NE], idn_f[:]),
                    waits=wl, incs=[(pe, 1)])
                g = b.n(pe)
                b.emit("vector", lambda e, j=j, bk=bk: e.tensor_copy(
                    comb_fm[:, j * 128:(j + 1) * 128], P[bk][0:16, 0:128]),
                    waits=[(pe, g)], incs=[(ve, 1), bank_rel(bk)])
            g = b.n(ve)
            b.emit("sync", lambda e, hb=half_base: e.dma_start(
                out=combD_d[:, hb:hb + TH], in_=comb_fm),
                waits=[(ve, g)], incs=[(dsc, 16)])
            return b.n(dsc)

        def expert_block(xln_v, x_v, cfg):
            n_i = cfg["n_i"]
            hbuf = FBUF["hbuf"]; sgb = FBUF["sgb"]; bccb = FBUF["bccb"]
            tmpb = FBUF["tmpb"]
            po_h_l = {}

            def emit_gu(tc):
                hslot = tc % 2
                if cfg["hmod"] == "comb":
                    ring = cfg["bc_state"]["n"] % 2
                    cfg["bc_state"]["n"] += 1
                    b.emit("sync", lambda e, ring=ring, tc=tc: e.dma_start(
                        out=bccb[:, ring * TC:(ring + 1) * TC],
                        in_=combD_d[cfg["e_idx"]:cfg["e_idx"] + 1,
                                    cfg["hb"] + tc * TC:cfg["hb"] + (tc + 1) * TC].broadcast_to([128, TC])),
                        waits=[(dsc, cfg["comb_at"]), (po, b.n(po)), (dsy, b.n(dsy))],
                        incs=[(dbc[ring], 16)])
                    bct = bccb[:, ring * TC:(ring + 1) * TC]
                    bc_ready = (dbc[ring], b.n(dbc[ring]))
                for i in range(n_i):
                    bkG, bkU = (0, 1) if i % 2 == 0 else (2, 3)
                    wl = [bank_w(bkG), cfg["wready"]]
                    for c in range(NCH):
                        st, sp = (c == 0), (c == NCH - 1)
                        b.emit("tensor", lambda e, c=c, i=i, tc=tc, bkG=bkG, st=st, sp=sp: e.matmul(
                            P[bkG][:], cfg["wg"](c, i), xln_v(c, tc), start=st, stop=sp),
                            waits=wl if c == 0 else [], incs=[(pe, 1)] if sp else [])
                    pe_g = b.n(pe)
                    wl2 = [bank_w(bkU)]
                    if "wready2" in cfg and tc == 0 and i == 0:
                        wl2.append(cfg["wready2"])
                    for c in range(NCH):
                        st, sp = (c == 0), (c == NCH - 1)
                        b.emit("tensor", lambda e, c=c, i=i, tc=tc, bkU=bkU, st=st, sp=sp: e.matmul(
                            P[bkU][:], cfg["wu"](c, i), xln_v(c, tc), start=st, stop=sp),
                            waits=wl2 if c == 0 else [], incs=[(pe, 1)] if sp else [])
                    pe_u = b.n(pe)
                    sgt = sgb[:, (i % 2) * TC:(i % 2 + 1) * TC]
                    tmt = tmpb[:, (i % 2) * TC:(i % 2 + 1) * TC]
                    hvt = hbuf[:, (hslot * 8 + i) * TC:(hslot * 8 + i + 1) * TC]
                    b.emit("scalar", lambda e, sgt=sgt, bkG=bkG: e.activation(sgt, P[bkG][:], AF.Silu),
                           waits=[(pe, pe_g), (ve, b.n(ve))], incs=[(ac, 1), bank_rel(bkG)])
                    ac_sg = b.n(ac)
                    b.emit("vector", lambda e, sgt=sgt, tmt=tmt, bkU=bkU: e.tensor_tensor(
                        tmt, sgt, P[bkU][:], ALU.mult),
                        waits=[(pe, pe_u), (ac, ac_sg), (po, b.n(po))],
                        incs=[(ve, 1), bank_rel(bkU)])
                    ve_tm = b.n(ve)
                    if cfg["hmod"] == "comb":
                        b.emit("gpsimd", lambda e, tmt=tmt, hvt=hvt, bct=bct: e.tensor_tensor(
                            hvt, tmt, bct, ALU.mult),
                            waits=[(ve, ve_tm), bc_ready], incs=[(po, 1)])
                    else:
                        b.emit("gpsimd", lambda e, tmt=tmt, hvt=hvt: e.tensor_scalar(
                            hvt, tmt, 0.5, None, ALU.mult),
                            waits=[(ve, ve_tm)], incs=[(po, 1)])
                po_h_l[tc] = b.n(po)

            def emit_down(tc):
                hslot = tc % 2
                for c in range(NCH):
                    bkD = 4 + (c % 2)
                    wl = [bank_w(bkD)]
                    if c == 0:
                        wl.append((po, po_h_l[tc]))
                    for i in range(n_i):
                        st, sp = (i == 0), (i == n_i - 1)
                        hvt = hbuf[:, (hslot * 8 + i) * TC:(hslot * 8 + i + 1) * TC]
                        b.emit("tensor", lambda e, i=i, c=c, bkD=bkD, st=st, sp=sp, hvt=hvt: e.matmul(
                            P[bkD][:], cfg["wd"](i, c), hvt, start=st, stop=sp),
                            waits=wl if i == 0 else [], incs=[(pe, 1)] if sp else [])
                    pe_d = b.n(pe)
                    b.emit("vector", lambda e, c=c, tc=tc, bkD=bkD: e.tensor_tensor(
                        x_v(c, tc), x_v(c, tc), P[bkD][:], ALU.add),
                        waits=[(pe, pe_d)], incs=[(ve, 1), bank_rel(bkD)])
                if cfg.get("on_tc_done"):
                    cfg["on_tc_done"](tc, b.n(ve))

            for tc in range(4):
                emit_gu(tc)
                if tc >= 1:
                    emit_down(tc - 1)
            emit_down(3)

        def emit_expert_loads(e_idx, wbuf, wsem, pe_gate):
            b.emit("gpsimd", lambda e, ei=e_idx, wbuf=wbuf: e.dma_start(
                out=wbuf[:, 0:4096].rearrange("p (n f) -> p n f", n=8),
                in_=weg_d[ei].rearrange("(n p) f -> p n f", p=128)),
                waits=[(pe, pe_gate)], incs=[(wsem, 16)])
            b.emit("gpsimd", lambda e, ei=e_idx, wbuf=wbuf: e.dma_start(
                out=wbuf[:, 4096:8192].rearrange("p (n f) -> p n f", n=8),
                in_=weu_d[ei].rearrange("(n p) f -> p n f", p=128)), incs=[(wsem, 16)])
            b.emit("gpsimd", lambda e, ei=e_idx, wbuf=wbuf: e.dma_start(
                out=wbuf[:, 8192:12288].rearrange("p (n f) -> p n f", n=4),
                in_=wed_d[ei].rearrange("(n p) f -> p n f", p=128)), incs=[(wsem, 16)])
            return b.n(wsem)

        def emit_sink(sink, hb, x_v, outst, ve_done, x_ready=(), sink_done=None):
            if sink == "park":
                for c in range(NCH):
                    b.emit("sync", lambda e, c=c, hb=hb: e.dma_start(
                        out=park_d[c * 128:(c + 1) * 128, hb:hb + TH],
                        in_=x_sb_ref[0][:, c * TH:(c + 1) * TH]),
                        waits=[(ve, ve_done)] if c == 0 else [], incs=[(dsy, 16)])
            else:
                for j in range(16):
                    tcq, tt = j // 4, j % 4
                    dsy_prev = b.n(dsy)
                    for c in range(NCH):
                        bk = c % 2
                        wl = [(ve, ve_done), bank_w(bk)] + list(x_ready)
                        if c == 0 and j >= 1:
                            wl.append((dsy, dsy_prev))
                        b.emit("tensor", lambda e, c=c, tcq=tcq, tt=tt, bk=bk: e.transpose(
                            P[bk][:, 0:128],
                            x_v(c, tcq)[:, tt * 128:(tt + 1) * 128], idn_f[:]),
                            waits=wl, incs=[(pe, 1)])
                        g = b.n(pe)
                        b.emit("vector", lambda e, c=c, bk=bk, outst=outst: e.tensor_copy(
                            outst[:, c * 128:(c + 1) * 128], P[bk][:, 0:128]),
                            waits=[(pe, g)], incs=[(ve, 1), bank_rel(bk)])
                    if sink_done is not None and tt == 3:
                        sink_done[tcq] = (pe, b.n(pe))
                    g = b.n(ve)
                    tglob = hb + j * 128
                    b.emit("sync", lambda e, tglob=tglob, outst=outst: e.dma_start(
                        out=out_d[tglob:tglob + 128, :],
                        in_=outst.rearrange("p (c f) -> p c f", c=8)),
                        waits=[(ve, g)], incs=[(dsy, 16)])

        def emit_sink_tc(hb, x_v, tc, ve_gate, sink_done):
            for c in range(NCH):
                b.emit("sync", lambda e, c=c, tc=tc, hb=hb: e.dma_start(
                    out=park_d[c * 128:(c + 1) * 128, hb + tc * TC: hb + (tc + 1) * TC],
                    in_=x_v(c, tc)),
                    waits=[(ve, ve_gate)] if c == 0 else [], incs=[(dsy, 16)])
            sink_done[tc] = (dsy, b.n(dsy))

        x_sb_ref = [None]

        def ffn_call(src, sink, compute=True):
            with ExitStack() as ps:
                NAMEC[0] += 1
                _uf = str(NAMEC[0])
                x_sb = ps.enter_context(nc.sbuf_tensor("x_ffn" + _uf, [128, NCH * TH], F32))
                xln_sb = ps.enter_context(nc.sbuf_tensor("xln_ffn" + _uf, [128, NCH * TH], BF16))
                hbuf = ps.enter_context(nc.sbuf_tensor("hbuf" + _uf, [128, 2 * 8 * TC], BF16))
                sgb = ps.enter_context(nc.sbuf_tensor("sgb" + _uf, [128, 2 * TC], F32))
                bccb = ps.enter_context(nc.sbuf_tensor("bccb" + _uf, [128, 2 * TC], F32))
                _gsc = hbuf[:, 4096:8192].bitcast(F32)
                scsig = _gsc[:, 0:256]
                tks = _gsc[:, 256:512]
                tkm = _gsc[:, 512:768]
                comb_tm = _gsc[:, 768:1024]
                tkmx = _gsc[:, 1024:1040]
                tkden = _gsc[:, 1040:1056]
                comb_fm = hbuf[:, 0:4096].bitcast(F32)[0:16, :]
                FBUF.update(dict(hbuf=hbuf, sgb=sgb, bccb=bccb, scsig=scsig, tks=tks,
                                 tkm=tkm, tkmx=tkmx, tkden=tkden, comb_tm=comb_tm,
                                 comb_fm=comb_fm))
                NAMEC[0] += 1
                _u = str(NAMEC[0])
                lnbufs = dict(
                    bcmr=ps.enter_context(nc.sbuf_tensor("bcmr" + _u, [128, 2 * TC], F32)),
                    bcrs=ps.enter_context(nc.sbuf_tensor("bcrs" + _u, [128, 2 * TC], F32)),
                    xbfc=ps.enter_context(nc.sbuf_tensor("xbfc" + _u, [128, 8 * TC], BF16)),
                    xsqc=ps.enter_context(nc.sbuf_tensor("xsqc" + _u, [128, 8 * TC], BF16)),
                    tmpb=ps.enter_context(nc.sbuf_tensor("tmpb" + _u, [128, 2 * TC], F32)),
                    tmpb2=ps.enter_context(nc.sbuf_tensor("tmpb2" + _u, [128, TC], F32)),
                )
                FBUF.update(lnbufs)
                outst = hbuf[:, 0:2048].bitcast(F32)
                x_sb_ref[0] = x_sb
                x_v = lambda c, tc: x_sb[:, c * TH + tc * TC: c * TH + (tc + 1) * TC]
                xln_v = lambda c, tc: xln_sb[:, c * TH + tc * TC: c * TH + (tc + 1) * TC]
                src_d = xT_d if src == "input" else park_d

                sink_done = {}
                next_ready = {}

                def emit_half1_loads(tc, eng="sync"):
                    for c in range(NCH):
                        wl = [sink_done[tc]] if c == 0 else []
                        b.emit(eng, lambda e, c=c, tc=tc: e.dma_start(
                            out=x_sb[:, c * TH + tc * TC: c * TH + (tc + 1) * TC],
                            in_=src_d[c * 128:(c + 1) * 128, TH + tc * TC: TH + (tc + 1) * TC]),
                            waits=wl, incs=[(dxl[tc], 16)])
                    next_ready[tc] = (dxl[tc], b.n(dxl[tc]))

                for half in range(2):
                    hb = half * TH
                    x_ready_tc = {}
                    for tc in range(4):
                        if half == 1 and compute:
                            x_ready_tc[tc] = [next_ready[tc]]
                            continue
                        for c in range(NCH):
                            eng = "sync" if c < 4 else "scalar"
                            wl = allsync() + [(dsy, b.n(dsy)), (dxw, b.n(dxw))] + init_ready \
                                if (tc == 0 and c in (0, 4)) else []
                            b.emit(eng, lambda e, c=c, tc=tc, hb=hb: e.dma_start(
                                out=x_sb[:, c * TH + tc * TC: c * TH + (tc + 1) * TC],
                                in_=src_d[c * 128:(c + 1) * 128, hb + tc * TC: hb + (tc + 1) * TC]),
                                waits=wl, incs=[(dxl[tc], 16)])
                        x_ready_tc[tc] = [(dxl[tc], b.n(dxl[tc]))]
                    x_ready = [x_ready_tc[t][0] for t in range(4)]
                    if not compute:
                        emit_sink(sink, hb, x_v, outst, b.n(ve), x_ready)
                        continue
                    wA_at = emit_expert_loads(0, wbufA, dwA, b.n(pe))
                    wB_at = emit_expert_loads(1, wbufB, dwB, b.n(pe))
                    w_at = {0: wA_at, 1: wB_at}
                    for tc in range(4):
                        slot = tc % 2
                        ln_stats(lambda c, tc=tc: x_v(c, tc), hb + tc * TC, slot, x_ready_tc[tc])
                        ln_apply(lambda c, tc=tc: x_v(c, tc),
                                 lambda c, tc=tc: xln_v(c, tc), slot, x_ready_tc[tc])
                    b.emit("tensor", lambda e: e.matmul(
                        P[LNB[0]][0:1, 0:1], ones_b[:], ones_b[:], start=True, stop=True),
                        waits=[(ve, b.n(ve)), (po, b.n(po)), bank_w(LNB[0])], incs=[(pe, 1)])
                    g = b.n(pe)
                    b.emit("vector", lambda e, _tk=tkden: e.tensor_copy(_tk[:, 0:1], P[LNB[0]][0:128, 0:1]),
                           waits=[(pe, g)], incs=[(ve, 1), bank_rel(LNB[0])])
                    comb_at = gating_topk(xln_v, hb)
                    bc_state = {"n": 0}
                    for ei in range(NE):
                        wbuf = wbufA if ei % 2 == 0 else wbufB
                        wsem = dwA if ei % 2 == 0 else dwB
                        expert_block(xln_v, x_v, dict(
                            n_i=4, hmod="comb", e_idx=ei, hb=hb, comb_at=comb_at,
                            bc_state=bc_state, wready=(wsem, w_at[ei]),
                            wg=lambda cc, i, wbuf=wbuf: wbuf[:, cc * 512 + i * 128: cc * 512 + (i + 1) * 128],
                            wu=lambda cc, i, wbuf=wbuf: wbuf[:, 4096 + cc * 512 + i * 128: 4096 + cc * 512 + (i + 1) * 128],
                            wd=lambda i, cc, wbuf=wbuf: wbuf[:, 8192 + i * 1024 + cc * 128: 8192 + i * 1024 + (cc + 1) * 128],
                        ))
                        pe_done = b.n(pe)
                        if ei + 2 < NE:
                            tgt = ei + 2
                            w_at[tgt] = emit_expert_loads(
                                tgt, wbufA if tgt % 2 == 0 else wbufB,
                                dwA if tgt % 2 == 0 else dwB, pe_done)
                        elif ei == NE - 2:
                            b.emit("gpsimd", lambda e: e.dma_start(
                                out=wbufA[:, 0:8192].rearrange("p (n f) -> p n f", n=8),
                                in_=wsg_d[:].rearrange("(n p) f -> p n f", p=128)),
                                waits=[(pe, pe_done)], incs=[(dwA, 16)])
                            b.emit("gpsimd", lambda e: e.dma_start(
                                out=wbufA[:, 8192:12288].rearrange("p (n f) -> p n f", n=4),
                                in_=wsd_d[0:512].rearrange("(n p) f -> p n f", p=128)),
                                incs=[(dwA, 16)])
                            shA_at = b.n(dwA)
                        elif ei == NE - 1:
                            b.emit("gpsimd", lambda e: e.dma_start(
                                out=wbufB[:, 0:8192].rearrange("p (n f) -> p n f", n=8),
                                in_=wsu_d[:].rearrange("(n p) f -> p n f", p=128)),
                                waits=[(pe, pe_done)], incs=[(dwB, 16)])
                            b.emit("gpsimd", lambda e: e.dma_start(
                                out=wbufB[:, 8192:12288].rearrange("p (n f) -> p n f", n=4),
                                in_=wsd_d[512:1024].rearrange("(n p) f -> p n f", p=128)),
                                incs=[(dwB, 16)])
                            shB_at = b.n(dwB)

                    def wsd_sl(i, cc):
                        if i < 4:
                            return wbufA[:, 8192 + i * 1024 + cc * 128: 8192 + i * 1024 + (cc + 1) * 128]
                        return wbufB[:, 8192 + (i - 4) * 1024 + cc * 128: 8192 + (i - 4) * 1024 + (cc + 1) * 128]
                    expert_block(xln_v, x_v, dict(
                        n_i=8, hmod="half", wready=(dwA, shA_at), wready2=(dwB, shB_at),
                        wg=lambda cc, i: wbufA[:, cc * 1024 + i * 128: cc * 1024 + (i + 1) * 128],
                        wu=lambda cc, i: wbufB[:, cc * 1024 + i * 128: cc * 1024 + (i + 1) * 128],
                        wd=wsd_sl,
                        on_tc_done=lambda tc, veg, hb=hb, half=half: (
                            emit_sink_tc(hb, x_v, tc, veg, sink_done),
                            emit_half1_loads(tc) if half == 0 else None)
                        if sink == "park" else None,
                    ))
                    if sink == "out":
                        ve_done = b.n(ve)
                        emit_sink(sink, hb, x_v, outst, ve_done, sink_done=sink_done)
                        if half == 0:
                            for tc in range(4):
                                emit_half1_loads(tc, eng="gpsimd")


        def qkv_phase(K_sb):
            with ExitStack() as ps:
                xpc = ps.enter_context(nc.sbuf_tensor("xpc", [128, 2 * 8 * TC], F32))
                xlnc = ps.enter_context(nc.sbuf_tensor("xlnc", [128, 2 * 8 * TC], BF16))
                qst = ps.enter_context(nc.sbuf_tensor("qst", [128, 2 * TC], BF16))
                vst = ps.enter_context(nc.sbuf_tensor("vst", [128, 2 * 1024], BF16))
                bvbc = ps.enter_context(nc.sbuf_tensor("bvbc", [128, 1024], F32))
                NAMEC[0] += 1
                _u = str(NAMEC[0])
                lnbufs = dict(
                    bcmr=ps.enter_context(nc.sbuf_tensor("bcmr" + _u, [128, 2 * TC], F32)),
                    bcrs=ps.enter_context(nc.sbuf_tensor("bcrs" + _u, [128, 2 * TC], F32)),
                    xbfc=ps.enter_context(nc.sbuf_tensor("xbfc" + _u, [128, 8 * TC], BF16)),
                    xsqc=ps.enter_context(nc.sbuf_tensor("xsqc" + _u, [128, 8 * TC], BF16)),
                    tmpb=ps.enter_context(nc.sbuf_tensor("tmpb" + _u, [128, 2 * TC], F32)),
                    tmpb2=ps.enter_context(nc.sbuf_tensor("tmpb2" + _u, [128, TC], F32)),
                )
                FBUF.update(lnbufs)

                pe_gate = b.n(pe)
                b.emit("gpsimd", lambda e: e.dma_start(
                    out=wbufA[:].rearrange("p (n f) -> p n f", n=4),
                    in_=wat_d[0:512].rearrange("(n p) f -> p n f", p=128)),
                    waits=[(pe, pe_gate)], incs=[(dwA, 16)])
                b.emit("gpsimd", lambda e: e.dma_start(
                    out=wbufB[:].rearrange("p (n f) -> p n f", n=4),
                    in_=wat_d[512:1024].rearrange("(n p) f -> p n f", p=128)),
                    incs=[(dwB, 16)])
                wA_at, wB_at = b.n(dwA), b.n(dwB)
                b.emit("scalar", lambda e: e.dma_start(
                    out=bvbc[:], in_=bav_d[0:1, :].broadcast_to([128, C])), incs=[(dsc, 16)])
                bv_at = b.n(dsc)

                def wat_sl(c, m):
                    buf = wbufA if c < 4 else wbufB
                    cc = c % 4
                    return buf[:, cc * 3072 + m * 128: cc * 3072 + (m + 1) * 128]

                def wat_v(c, blk):
                    buf = wbufA if c < 4 else wbufB
                    cc = c % 4
                    return buf[:, cc * 3072 + 2048 + blk * 512: cc * 3072 + 2048 + (blk + 1) * 512]

                def qkv_load(tcg, wl):
                    ring = tcg % 2
                    gb = tcg * TC
                    b.emit("gpsimd", lambda e, gb=gb, ring=ring: e.dma_start(
                        out=xpc[:, ring * 8 * TC: ring * 8 * TC + 4 * TC].rearrange("p (c f) -> p c f", c=4),
                        in_=park_d[0:512, gb:gb + TC].rearrange("(c p) f -> p c f", p=128)),
                        waits=wl, incs=[(dxc[ring], 16)])
                    b.emit("sync", lambda e, gb=gb, ring=ring: e.dma_start(
                        out=xpc[:, ring * 8 * TC + 4 * TC:(ring + 1) * 8 * TC].rearrange("p (c f) -> p c f", c=4),
                        in_=park_d[512:1024, gb:gb + TC].rearrange("(c p) f -> p c f", p=128)),
                        waits=wl, incs=[(dxc[ring], 16)])
                    return [(dxc[ring], b.n(dxc[ring]))]

                def qkv_body(tcg, x_ready):
                    ring = tcg % 2
                    gb = tcg * TC
                    xv = lambda c, ring=ring: xpc[:, (ring * 8 + c) * TC:(ring * 8 + c + 1) * TC]
                    xlv = lambda c, ring=ring: xlnc[:, (ring * 8 + c) * TC:(ring * 8 + c + 1) * TC]
                    apply_gates[tcg] = ln_apply(xv, xlv, tcg % 2, x_ready)
                    po_ln, ve_ln = b.n(po), b.n(ve)
                    for m in range(16):
                        bk = m % 2
                        wl = [bank_w(bk), (dwA, wA_at), (dwB, wB_at)]
                        if m == 0:
                            wl += [(ve, ve_ln), (po, po_ln)]
                        for c in range(NCH):
                            st, sp = (c == 0), (c == NCH - 1)
                            b.emit("tensor", lambda e, c=c, m=m, bk=bk, st=st, sp=sp, xlv=xlv: e.matmul(
                                P[bk][:], wat_sl(c, m), xlv(c), start=st, stop=sp),
                                waits=wl if c == 0 else [], incs=[(pe, 1)] if sp else [])
                        pe_m = b.n(pe)
                        if m < 8:
                            qslot = m % 2
                            b.emit("scalar", lambda e, m=m, bk=bk, qslot=qslot: e.add(
                                qst[:, qslot * TC:(qslot + 1) * TC], P[bk][:],
                                batqk_sb[:, m:m + 1]),
                                waits=[(pe, pe_m), (dqs[qslot], b.n(dqs[qslot]))],
                                incs=[(ac, 1), bank_rel(bk)])
                            g = b.n(ac)
                            b.emit("sync", lambda e, m=m, gb=gb, qslot=qslot: e.dma_start(
                                out=qsp_d[m * 128:(m + 1) * 128, gb:gb + TC],
                                in_=qst[:, qslot * TC:(qslot + 1) * TC]),
                                waits=[(ac, g)], incs=[(dqs[qslot], 16)])
                        else:
                            mk = m - 8
                            b.emit("scalar", lambda e, mk=mk, bk=bk, gb=gb: e.add(
                                K_sb[:, mk * T + gb: mk * T + gb + TC], P[bk][:],
                                batqk_sb[:, 8 + mk:9 + mk]),
                                waits=[(pe, pe_m)], incs=[(ac, 1), bank_rel(bk)])
                    for tt in range(4):
                        vslot = tt % 2
                        for blk in range(2):
                            bk = 4 + blk
                            wl = [bank_w(bk)]
                            for c in range(NCH):
                                st, sp = (c == 0), (c == NCH - 1)
                                b.emit("tensor", lambda e, c=c, tt=tt, blk=blk, bk=bk, st=st, sp=sp, xlv=xlv: e.matmul(
                                    P[bk][:], xlv(c)[:, tt * 128:(tt + 1) * 128], wat_v(c, blk),
                                    start=st, stop=sp),
                                    waits=wl if c == 0 else [], incs=[(pe, 1)] if sp else [])
                            pe_v = b.n(pe)
                            b.emit("vector", lambda e, blk=blk, bk=bk, vslot=vslot: e.tensor_tensor(
                                vst[:, vslot * 1024 + blk * 512: vslot * 1024 + (blk + 1) * 512],
                                P[bk][:], bvbc[:, blk * 512:(blk + 1) * 512], ALU.add),
                                waits=[(pe, pe_v), (dsc, bv_at), (dvs[vslot], b.n(dvs[vslot]))],
                                incs=[(ve, 1), bank_rel(bk)])
                        g = b.n(ve)
                        trow = gb + tt * 128
                        for pp in range(4):
                            b.emit("sync", lambda e, pp=pp, trow=trow, vslot=vslot: e.dma_start(
                                out=vsp_d[pp, trow:trow + 128, :],
                                in_=vst[:, vslot * 1024 + pp * 256: vslot * 1024 + (pp + 1) * 256]),
                                waits=[(ve, g)] if pp == 0 else [], incs=[(dvs[vslot], 16)])

                xrdy = {}
                apply_gates = {}
                cast_gates = {}
                xrdy[0] = qkv_load(0, [(dsy, b.n(dsy)), (pe, b.n(pe)), (po, b.n(po)),
                                       (ac, b.n(ac)), (ve, b.n(ve))])
                xrdy[1] = qkv_load(1, [(dsy, b.n(dsy)), (pe, b.n(pe)), (po, b.n(po)),
                                       (ac, b.n(ac)), (ve, b.n(ve))])
                for i in range(8):
                    xv = lambda c, i=i: xpc[:, ((i % 2) * 8 + c) * TC:((i % 2) * 8 + c + 1) * TC]
                    pe_done, cast_gates[i] = ln_sums(xv, i % 2, xrdy[i])
                    if i >= 1:
                        qkv_body(i - 1, xrdy[i - 1])
                        if i + 1 <= 7:
                            gv, gp = apply_gates[i - 1]
                            xrdy[i + 1] = qkv_load(i + 1, [
                                (ac, cast_gates[i - 1]), (ve, gv), (po, gp)])
                    ln_rows(i % 2, pe_done)
                qkv_body(7, xrdy[7])

        def attention_phase():
            with ExitStack() as ps:
                K_sb = ps.enter_context(nc.sbuf_tensor("K_sb", [128, NCH * T], BF16))
                qkv_phase(K_sb)
                cmask_sb = ps.enter_context(nc.sbuf_tensor("cmask_sb", [128, 4 * 512], BF16))
                b.emit("gpsimd", lambda e: e.dma_start(
                    out=cmask_sb[:].rearrange("p (a f) -> p a f", a=4),
                    in_=cmask_d[:].rearrange("(a p) f -> p a f", p=128)),
                    waits=[(ve, b.n(ve)), (ac, b.n(ac)), (po, b.n(po))], incs=[(dsy, 16)])
                cmask_at = b.n(dsy)
                ac_qkv = b.n(ac)
                pe_gate = b.n(pe)
                b.emit("gpsimd", lambda e: e.dma_start(
                    out=wbufA[:, 0:8192].rearrange("p (n f) -> p n f", n=8),
                    in_=wproj_d[:].rearrange("(n p) f -> p n f", p=128)),
                    waits=[(pe, pe_gate)], incs=[(dwA, 16)])
                wp_at = b.n(dwA)

                pb_sb = ps.enter_context(nc.sbuf_tensor("pb_sb", [128, 2 * T], BF16))
                pT_sb = ps.enter_context(nc.sbuf_tensor("pT_sb", [128, 32 * 256], BF16))
                av_sb = ps.enter_context(nc.sbuf_tensor("av_sb", [128, 8 * 256], BF16))
                vcol = ps.enter_context(nc.sbuf_tensor("vcol", [128, 6 * 8 * 256], BF16))
                q_sb = ps.enter_context(nc.sbuf_tensor("q_sb", [128, 2 * 8 * 128], BF16))
                xg_sb = ps.enter_context(nc.sbuf_tensor("xg_sb", [128, 8 * 256], F32))
                lsum = ps.enter_context(nc.sbuf_tensor("lsum", [128, 4], F32))
                mrow_t = ps.enter_context(nc.sbuf_tensor("mrow_t", [128, 2], F32))
                mxp = ps.enter_context(nc.sbuf_tensor("mxp", [128, 16], F32))
                lsum2 = ps.enter_context(nc.sbuf_tensor("lsum2", [128, 2], F32))

                qkv_dsy = b.n(dsy)
                qkv_spills = [(dqs[0], b.n(dqs[0])), (dqs[1], b.n(dqs[1])),
                              (dvs[0], b.n(dvs[0])), (dvs[1], b.n(dvs[1]))]

                q_at = {}

                def load_q(t):
                    ring = t % 2
                    b.emit("sync", lambda e, t=t, ring=ring: e.dma_start(
                        out=q_sb[:, ring * 1024:(ring + 1) * 1024].rearrange("p (c f) -> p c f", c=8),
                        in_=qsp_d[:, t * 128:(t + 1) * 128].rearrange("(c p) f -> p c f", p=128)),
                        waits=[(dsy, qkv_dsy), (pe, b.n(pe))] + qkv_spills,
                        incs=[(dqt[ring], 16)])
                    q_at[t] = b.n(dqt[ring])

                av_kk = [0]

                def load_vcol(pair, s0, cnt, sl, gate):
                    b.emit("sync", lambda e, pair=pair, s0=s0, cnt=cnt, sl=sl: e.dma_start(
                        out=vcol[:, sl * 2048: sl * 2048 + cnt * 256].rearrange(
                            "p (j f) -> p j f", f=256),
                        in_=vsp_d[pair, s0 * 128:(s0 + cnt) * 128, :].rearrange("(j p) f -> p j f", p=128)),
                        waits=[(dsy, qkv_dsy), (pe, gate)] + qkv_spills, incs=[(dvc[sl], 16)])
                    return b.n(dvc[sl])

                load_q(0)
                load_q(1)

                for g_i in range(16):
                    smax = 2 * g_i + 2
                    ve_pb_t = {}
                    for ti in range(2):
                        t = 2 * g_i + ti
                        ring = t % 2
                        sring = t % 2
                        nb = t // 4 + 1
                        reg = t % 2
                        nlo = min(nb, 4)
                        bmap = {}
                        for bi in range(nb):
                            tens = reg if bi < 4 else 1 - reg
                            bmap[bi] = tens * 4 + (bi % 4)
                        for bi in range(nb):
                            bk = bmap[bi]
                            wl = [bank_w(bk), (dqt[ring], q_at[t]), (ac, ac_qkv)]
                            for c in range(NCH):
                                st, sp = (c == 0), (c == NCH - 1)
                                b.emit("tensor", lambda e, c=c, bi=bi, bk=bk, st=st, sp=sp, ring=ring: e.matmul(
                                    P[bk],
                                    q_sb[:, ring * 1024 + c * 128: ring * 1024 + (c + 1) * 128],
                                    K_sb[:, c * T + bi * 512: c * T + (bi + 1) * 512],
                                    start=st, stop=sp),
                                    waits=wl if c == 0 else [], incs=[(pe, 1)] if sp else [])
                            pe_blk = b.n(pe)
                            if bi == nb - 1:
                                al = t % 4
                                b.emit("vector", lambda e, bk=bk, al=al: e.tensor_tensor(
                                    P[bk], P[bk], cmask_sb[:, al * 512:(al + 1) * 512], ALU.add),
                                    waits=[(pe, pe_blk), (dsy, cmask_at)], incs=[(ve, 1)])
                            b.emit("vector", lambda e, bk=bk, bi=bi, sring=sring: e.tensor_reduce(
                                mxp[:, sring * 8 + bi: sring * 8 + bi + 1],
                                P[bk], AX.X, ALU.max),
                                waits=[(pe, pe_blk), (ve, b.n(ve))], incs=[(ve, 1)])
                        ve_evac = b.n(ve)
                        b.emit("vector", lambda e, sring=sring, nb=nb: e.tensor_reduce(
                            mrow_t[:, sring:sring + 1], mxp[:, sring * 8: sring * 8 + nb],
                            AX.X, ALU.max),
                            waits=[(ve, ve_evac), (ac, b.n(ac))], incs=[(ve, 1)])
                        g1 = b.n(ve)
                        b.emit("vector", lambda e, sring=sring: e.tensor_scalar(
                            mrow_t[:, sring:sring + 1], mrow_t[:, sring:sring + 1],
                            -1.0 / 32.0, None, ALU.mult), waits=[(ve, g1)], incs=[(ve, 1)])
                        g2 = b.n(ve)
                        rels1 = [bank_rel(bmap[bi]) for bi in range(nlo)]
                        b.emit("scalar", lambda e, sring=sring, nlo=nlo, reg=reg: e.activation(
                            pb_sb[:, sring * T: sring * T + nlo * 512],
                            PFULL[reg][:, 0:nlo * 512], AF.Exp,
                            bias=mrow_t[:, sring:sring + 1], scale=1.0 / 32.0,
                            accum_out=lsum[:, sring:sring + 1]),
                            waits=[(ve, g2), (pe, b.n(pe))], incs=[(ac, 1)] + rels1)
                        if nb > 4:
                            nhi = nb - 4
                            rels2 = [bank_rel(bmap[bi]) for bi in range(4, nb)]
                            b.emit("scalar", lambda e, sring=sring, nhi=nhi, reg=reg: e.activation(
                                pb_sb[:, sring * T + 2048: sring * T + 2048 + nhi * 512],
                                PFULL[1 - reg][:, 0:nhi * 512], AF.Exp,
                                bias=mrow_t[:, sring:sring + 1], scale=1.0 / 32.0,
                                accum_out=lsum2[:, sring:sring + 1]),
                                incs=[(ac, 1)] + rels2)
                            g3b = b.n(ac)
                            b.emit("vector", lambda e, sring=sring: e.tensor_tensor(
                                lsum[:, sring:sring + 1], lsum[:, sring:sring + 1],
                                lsum2[:, sring:sring + 1], ALU.add),
                                waits=[(ac, g3b)], incs=[(ve, 1)])
                        g3 = b.n(ac)
                        g3v = b.n(ve)
                        b.emit("vector", lambda e, sring=sring: e.reciprocal(
                            lsum[:, 2 + sring:3 + sring], lsum[:, sring:sring + 1]),
                            waits=[(ac, g3), (ve, g3v)], incs=[(ve, 1)])
                        g4 = b.n(ve)
                        b.emit("vector", lambda e, sring=sring, nb=nb: e.tensor_scalar(
                            pb_sb[:, sring * T: sring * T + nb * 512],
                            pb_sb[:, sring * T: sring * T + nb * 512],
                            lsum[:, 2 + sring:3 + sring], None, ALU.mult),
                            waits=[(ve, g4)], incs=[(ve, 1)])
                        ve_pb_t[t] = b.n(ve)
                        nfill = smax - (t + 1)
                        if nfill > 0:
                            b.emit("vector", lambda e, t=t, ti=ti, smax=smax: e.memset(
                                pT_sb[:].rearrange("p (j q) -> p j q", q=256)[
                                    :, t + 1:smax, ti * 128:(ti + 1) * 128], 0.0),
                                incs=[(ve, 1)])
                    for ti in range(2):
                        t = 2 * g_i + ti
                        sring = t % 2
                        for j in range(t + 1):
                            bk = 6 + (j % 2)
                            b.emit("tensor", lambda e, j=j, bk=bk, sring=sring: e.transpose(
                                P[bk][:].bitcast(BF16)[:, 0:128],
                                pb_sb[:, sring * T + j * 128: sring * T + (j + 1) * 128],
                                idn_b[:]),
                                waits=[(ve, ve_pb_t[t]), bank_w(bk)], incs=[(pe, 1)])
                            g5 = b.n(pe)
                            b.emit("scalar", lambda e, j=j, bk=bk, ti=ti: e.copy(
                                pT_sb[:, j * 256 + ti * 128: j * 256 + (ti + 1) * 128],
                                P[bk][:].bitcast(BF16)[:, 0:128]),
                                waits=[(pe, g5)], incs=[(ac, 1), bank_rel(bk)])
                        if t + 2 < 32:
                            load_q(t + 2)
                    ac_tr = b.n(ac)
                    ve_fill = b.n(ve)
                    pieces = [(p0, min(8, smax - p0)) for p0 in range(0, smax, 8)]
                    npc = len(pieces)
                    slot_of = {}
                    v_at = {}
                    pe_after_pair = {}

                    def emit_vload(pair, pi, gate):
                        s0, cnt = pieces[pi]
                        sl = av_kk[0] % 6
                        av_kk[0] += 1
                        slot_of[(pair, pi)] = sl
                        v_at[(pair, pi)] = load_vcol(pair, s0, cnt, sl, gate)

                    for pi in range(npc):
                        emit_vload(0, pi, b.n(pe))
                    for pair in range(4):
                        gate0 = b.n(pe)
                        if pair + 1 < 4:
                            for pi in range(min(npc, 6 - npc)):
                                emit_vload(pair + 1, pi, gate0)
                        wl0 = [(ac, ac_tr), (ve, ve_fill)]
                        for pi in range(npc):
                            wl0.append((dvc[slot_of[(pair, pi)]], v_at[(pair, pi)]))
                        for half in range(2):
                            cb = pair * 2 + half
                            bk = 4 + (cb % 2)
                            wl = wl0 + [bank_w(bk)] if half == 0 else [bank_w(bk)]
                            for pi, (s0, cnt) in enumerate(pieces):
                                sl = slot_of[(pair, pi)]
                                for j in range(cnt):
                                    st = (pi == 0 and j == 0)
                                    sp = (pi == npc - 1 and j == cnt - 1)
                                    sj = s0 + j
                                    b.emit("tensor", lambda e, j=j, sj=sj, bk=bk, sl=sl, half=half, st=st, sp=sp: e.matmul(
                                        P[bk][:, 0:256],
                                        vcol[:, sl * 2048 + j * 256 + half * 128: sl * 2048 + j * 256 + half * 128 + 128],
                                        pT_sb[:, sj * 256:(sj + 1) * 256], start=st, stop=sp),
                                        waits=wl if (pi == 0 and j == 0) else [],
                                        incs=[(pe, 1)] if sp else [])
                            pe_av = b.n(pe)
                            b.emit("scalar", lambda e, cb=cb, bk=bk: e.copy(
                                av_sb[:, cb * 256:(cb + 1) * 256], P[bk][:, 0:256]),
                                waits=[(pe, pe_av)], incs=[(ac, 1), bank_rel(bk)])
                        if pair + 1 < 4:
                            for pi in range(min(npc, 6 - npc), npc):
                                emit_vload(pair + 1, pi, b.n(pe))
                    ac_av = b.n(ac)
                    xring = 0
                    b.emit("gpsimd", lambda e, g_i=g_i: e.dma_start(
                        out=xg_sb[:].rearrange("p (c f) -> p c f", c=8),
                        in_=park_d[:, g_i * 256:(g_i + 1) * 256].rearrange("(c p) f -> p c f", p=128)),
                        waits=[(dsy, qkv_dsy), (dxw, b.n(dxw)), (ve, b.n(ve))], incs=[(dxg[0], 16)])
                    x_at = b.n(dxg[0])
                    wp_sl = lambda cc, m: wbufA[:, cc * 1024 + m * 128: cc * 1024 + (m + 1) * 128]
                    for m in range(NCH):
                        bk = 4 + (m % 2)
                        wl = [bank_w(bk), (dwA, wp_at), (ac, ac_av)]
                        if m == 0 and "proj" in ser:
                            wl += allsync()
                        for c in range(NCH):
                            st, sp = (c == 0), (c == NCH - 1)
                            b.emit("tensor", lambda e, c=c, m=m, bk=bk, st=st, sp=sp: e.matmul(
                                P[bk][:, 0:256], wp_sl(c, m), av_sb[:, c * 256:(c + 1) * 256],
                                start=st, stop=sp),
                                waits=wl if c == 0 else [], incs=[(pe, 1)] if sp else [])
                        pe_pj = b.n(pe)
                        b.emit("vector", lambda e, m=m, bk=bk: e.scalar_tensor_tensor(
                            xg_sb[:, m * 256:(m + 1) * 256],
                            P[bk][:, 0:256], bproj_sb[:, m:m + 1],
                            xg_sb[:, m * 256:(m + 1) * 256],
                            ALU.add, ALU.add),
                            waits=[(pe, pe_pj), (dxg[0], x_at)],
                            incs=[(ve, 1), bank_rel(bk)])
                    ve_res = b.n(ve)
                    b.emit("gpsimd", lambda e, g_i=g_i: e.dma_start(
                        out=park_d[:, g_i * 256:(g_i + 1) * 256].rearrange("(c p) f -> p c f", p=128),
                        in_=xg_sb[:].rearrange("p (c f) -> p c f", c=8)),
                        waits=[(ve, ve_res)], incs=[(dxw, 16)])

        if stage == "ffn1":
            ffn_call("input", "out")
        elif stage == "attn":
            ffn_call("input", "park")
            attention_phase()
            ffn_call("park", "out", compute=False)
        else:
            ffn_call("input", "park")
            attention_phase()
            ffn_call("park", "out")

        b.emit("sync", lambda e: None,
               waits=[(dsy, b.n(dsy)), (dsc, b.n(dsc)), (dwA, b.n(dwA)), (dwB, b.n(dwB)),
                      (dxw, b.n(dxw)), (dvs[0], b.n(dvs[0])), (dvs[1], b.n(dvs[1])),
                      (dqs[0], b.n(dqs[0])), (dqs[1], b.n(dqs[1]))])

        block = stack.enter_context(nc.Block())
        b.finish(block)
    return nc


_NC_CACHE = None


def kernel(L0_attn_x, L0_attn_w_attn, L0_attn_b_attn, L0_attn_w_proj, L0_attn_b_proj,
           L0_ffn_gate_w, L0_ffn_expert_gate, L0_ffn_expert_up, L0_ffn_expert_down,
           L0_ffn_shared_gate, L0_ffn_shared_up, L0_ffn_shared_down):
    global _NC_CACHE
    x = np.asarray(L0_attn_x, np.float32)
    B = x.shape[0]
    wat = np.ascontiguousarray(np.asarray(L0_attn_w_attn, np.float32))
    b_attn = np.asarray(L0_attn_b_attn, np.float32)
    batqk = np.ascontiguousarray(b_attn[:2048].reshape(16, 128).T)
    bav = np.ascontiguousarray(b_attn[2048:].reshape(1, 1024))
    wproj = np.ascontiguousarray(np.asarray(L0_attn_w_proj, np.float32))
    bproj = np.ascontiguousarray(np.asarray(L0_attn_b_proj, np.float32).reshape(8, 128).T)
    gwT = np.ascontiguousarray(np.asarray(L0_ffn_gate_w, np.float32).T)
    weg = np.ascontiguousarray(np.asarray(L0_ffn_expert_gate, np.float32))
    weu = np.ascontiguousarray(np.asarray(L0_ffn_expert_up, np.float32))
    wed = np.ascontiguousarray(np.asarray(L0_ffn_expert_down, np.float32))
    wsg = np.ascontiguousarray(np.asarray(L0_ffn_shared_gate, np.float32))
    wsu = np.ascontiguousarray(np.asarray(L0_ffn_shared_up, np.float32))
    wsd = np.ascontiguousarray(np.asarray(L0_ffn_shared_down, np.float32))

    p = np.arange(128)[:, None]
    sl = np.arange(512)[None, :]
    cmask = np.concatenate(
        [np.where(sl <= a * 128 + p, 0.0, -1e10).astype(np.float32) for a in range(4)], axis=0)
    idn = np.eye(128, dtype=np.float32)

    if _NC_CACHE is None:
        _NC_CACHE = build_program()
    nc = _NC_CACHE

    in_maps = []
    for bb in range(B):
        in_maps.append(dict(
            xT=np.ascontiguousarray(x[bb].T),
            wat=wat, batqk=batqk, bav=bav, wproj=wproj, bproj=bproj,
            gwT=gwT, weg=weg, weu=weu, wed=wed, wsg=wsg, wsu=wsu, wsd=wsd,
            cmask=cmask, idn=idn,
        ))
    res = run_bass_kernel_spmd(nc, in_maps, core_ids=list(range(B)))
    out = np.stack([np.asarray(res.results[bb]["out"]) for bb in range(B)], axis=0)
    if os.environ.get("KDUMP", "0") == "1":
        np.save("_dump_qsp.npy", np.asarray(res.results[0]["qsp"]).view(np.uint16))
        np.save("_dump_vsp.npy", np.asarray(res.results[0]["vsp"]).view(np.uint16))
        np.save("_dump_park.npy", np.asarray(res.results[0]["park"]))
    return out.astype(np.float32)


# version 26
# speedup vs baseline: 1.0429x; 1.0218x over previous
import os
import numpy as np
from contextlib import ExitStack

import concourse.bass as bass
import concourse.mybir as mybir
from concourse.bass_utils import run_bass_kernel_spmd

F32 = mybir.dt.float32
BF16 = mybir.dt.bfloat16
AF = mybir.ActivationFunctionType
ALU = mybir.AluOpType
AX = mybir.AxisListType

T = 4096
C = 1024
TH = 2048
TC = 512
NE = 16
NCH = 8

ENGS = ("tensor", "vector", "scalar", "gpsimd", "sync")


class Builder:
    def __init__(self, nc, stack):
        self.nc = nc
        self.stack = stack
        self.ops = {e: [] for e in ENGS}
        self.cnt = {}
        self.name_of = {}

    def sem(self, name):
        h = self.stack.enter_context(self.nc.semaphore(name))
        self.cnt[name] = 0
        self.name_of[id(h)] = name
        return h

    def n(self, sem):
        return self.cnt[self.name_of[id(sem)]]

    def emit(self, eng, fn, waits=(), incs=()):
        real = [(s, k) for (s, k) in incs if not (isinstance(s, tuple) and s[0] == "REL")]
        rels = [s[1] for (s, k) in incs if isinstance(s, tuple) and s[0] == "REL"]
        assert len(real) <= 1, "at most one sync update per instruction"
        own = getattr(self, "own", {}).get(eng)
        if own is not None and not any(k >= 16 for _, k in real):
            waits = list(waits) + [(own, self.n(own))]
        for s, k in real:
            self.cnt[self.name_of[id(s)]] += k
        if rels:
            assert real, "bank release needs a real inc on the same instruction"
            rs, _ = real[0]
            tok = (rs, self.n(rs))
            for bi in rels:
                self.bank_tok[bi] = tok
        waits = tuple(w for w in waits if w is not None)
        real = tuple(real)

        def run(e):
            for s, v in waits:
                if v > 0:
                    e.wait_ge(s, v)
            ins = fn(e)
            for s, k in real:
                ins = ins.then_inc(s, k)
        self.ops[eng].append(run)

    def finish(self, block):
        for eng in ENGS:
            fns = list(self.ops[eng])
            def make(fns):
                def body(e):
                    for f in fns:
                        f(e)
                return body
            getattr(block, eng)(make(fns))


def build_program():
    stage = os.environ.get("KSTAGE", "full")
    ser = os.environ.get("KSER", "")
    nc = bass.Bass("TRN2", target_bir_lowering=False)

    xT_d = nc.dram_tensor("xT", [C, T], F32, kind="ExternalInput").ap()
    wat_d = nc.dram_tensor("wat", [C, 3 * C], F32, kind="ExternalInput").ap()
    batqk_d = nc.dram_tensor("batqk", [128, 16], F32, kind="ExternalInput").ap()
    bav_d = nc.dram_tensor("bav", [1, C], F32, kind="ExternalInput").ap()
    wproj_d = nc.dram_tensor("wproj", [C, C], F32, kind="ExternalInput").ap()
    bproj_d = nc.dram_tensor("bproj", [128, 8], F32, kind="ExternalInput").ap()
    gwT_d = nc.dram_tensor("gwT", [C, NE], F32, kind="ExternalInput").ap()
    weg_d = nc.dram_tensor("weg", [NE, C, 512], F32, kind="ExternalInput").ap()
    weu_d = nc.dram_tensor("weu", [NE, C, 512], F32, kind="ExternalInput").ap()
    wed_d = nc.dram_tensor("wed", [NE, 512, C], F32, kind="ExternalInput").ap()
    wsg_d = nc.dram_tensor("wsg", [C, C], F32, kind="ExternalInput").ap()
    wsu_d = nc.dram_tensor("wsu", [C, C], F32, kind="ExternalInput").ap()
    wsd_d = nc.dram_tensor("wsd", [C, C], F32, kind="ExternalInput").ap()
    cmask_d = nc.dram_tensor("cmask", [4 * 128, 512], F32, kind="ExternalInput").ap()
    idn_d = nc.dram_tensor("idn", [128, 128], F32, kind="ExternalInput").ap()
    out_d = nc.dram_tensor("out", [T, C], F32, kind="ExternalOutput").ap()

    dump = os.environ.get("KDUMP", "0") == "1"
    dkind = dict(kind="ExternalOutput") if dump else {}
    park_d = nc.dram_tensor("park", [C, T], F32, **dkind).ap()
    qsp_d = nc.dram_tensor("qsp", [C, T], BF16, **dkind).ap()
    vsp_d = nc.dram_tensor("vsp", [4, T, 256], BF16, **dkind).ap()
    combD_d = nc.dram_tensor("combD", [NE, T], F32).ap()

    stack = ExitStack()
    with stack:
        b = Builder(nc, stack)
        sb = lambda name, shape, dt=F32: stack.enter_context(nc.sbuf_tensor(name, shape, dt))

        wbufA = sb("wbufA", [128, 12288], BF16)
        wbufB = sb("wbufB", [128, 12288], BF16)
        idn_f = sb("idn_f", [128, 128], F32)
        idn_b = sb("idn_b", [128, 128], BF16)
        ones_b = sb("ones_b", [128, 1], BF16)
        ones128 = sb("ones128", [128, 128], BF16)
        epsb = sb("epsb", [128, 1], F32)
        batqk_sb = sb("batqk_sb", [128, 16], F32)
        bproj_sb = sb("bproj_sb", [128, 8], F32)
        gwT_sb = sb("gwT_sb", [128, NCH * NE], BF16)

        PA = stack.enter_context(nc.psum_tensor("PA", [128, 2048], F32))
        PB = stack.enter_context(nc.psum_tensor("PB", [128, 2048], F32))
        PFULL = [PA, PB]
        P = [PA[:, i * 512:(i + 1) * 512] for i in range(4)] + \
            [PB[:, i * 512:(i + 1) * 512] for i in range(4)]

        b.bank_tok = [None] * 8
        pe = b.sem("pe")
        ve = b.sem("ve")
        ac = b.sem("ac")
        po = b.sem("po")
        b.own = {"vector": ve, "scalar": ac, "gpsimd": po}
        dwA = b.sem("dwA")
        dwB = b.sem("dwB")
        dini = b.sem("dini")
        dsy = b.sem("dsy")
        dsc = b.sem("dsc")
        dbc = [b.sem(f"dbc{i}") for i in range(4)]
        dln = [b.sem(f"dln{i}") for i in range(2)]
        dxc = [b.sem(f"dxc{i}") for i in range(2)]
        dqt = [b.sem(f"dqt{i}") for i in range(2)]
        dvc = [b.sem(f"dvc{i}") for i in range(6)]
        dxg = [b.sem(f"dxg{i}") for i in range(2)]
        dxl = [b.sem(f"dxl{i}") for i in range(4)]
        dvs = [b.sem(f"dvs{i}") for i in range(2)]
        dqs = [b.sem(f"dqs{i}") for i in range(2)]
        dxw = b.sem("dxw")

        def bank_w(i):
            return b.bank_tok[i]

        def bank_rel(i):
            return (("REL", i), 0)

        def allsync():
            return [(pe, b.n(pe)), (ve, b.n(ve)), (ac, b.n(ac)), (po, b.n(po))]

        FBUF = {}
        NAMEC = [0]

        b.emit("sync", lambda e: e.dma_start(out=idn_f[:], in_=idn_d[:]), incs=[(dsy, 16)])
        b.emit("sync", lambda e: e.dma_start(out=batqk_sb[:], in_=batqk_d[:]), incs=[(dsy, 16)])
        b.emit("sync", lambda e: e.dma_start(out=bproj_sb[:], in_=bproj_d[:]), incs=[(dsy, 16)])
        b.emit("gpsimd", lambda e: e.dma_start(out=idn_b[:], in_=idn_d[:]), incs=[(dini, 16)])
        b.emit("gpsimd", lambda e: e.dma_start(
            out=gwT_sb[:].rearrange("p (n f) -> p n f", n=NCH),
            in_=gwT_d[:].rearrange("(n p) f -> p n f", p=128)), incs=[(dini, 16)])
        b.emit("vector", lambda e: e.memset(ones_b[:], 1.0), incs=[(ve, 1)])
        b.emit("vector", lambda e: e.memset(ones128[:], 1.0), incs=[(ve, 1)])
        b.emit("vector", lambda e: e.memset(epsb[:], 1e-5), incs=[(ve, 1)])
        init_ready = [(dsy, b.n(dsy)), (dini, b.n(dini)), (ve, b.n(ve))]

        LNB = (6, 7)

        def ln_sums(x_view, slot, x_ready):
            xbf = FBUF["xbfc"]; xsq = FBUF["xsqc"]
            pe_now = b.n(pe)
            for c in range(NCH):
                xb = xbf[:, c * TC:(c + 1) * TC]
                xq = xsq[:, c * TC:(c + 1) * TC]
                b.emit("scalar", lambda e, c=c, xb=xb: e.copy(xb, x_view(c)),
                       waits=(list(x_ready) + [(pe, pe_now)]) if c == 0 else [], incs=[(ac, 1)])
                b.emit("scalar", lambda e, c=c, xq=xq: e.activation(xq, x_view(c), AF.Square),
                       incs=[(ac, 1)])
            ac_casts = b.n(ac)
            wl = [(ve, b.n(ve)), (ac, ac_casts), bank_w(LNB[0]), bank_w(LNB[1])]
            for c in range(NCH):
                xb = xbf[:, c * TC:(c + 1) * TC]
                xq = xsq[:, c * TC:(c + 1) * TC]
                st, sp = (c == 0), (c == NCH - 1)
                b.emit("tensor", lambda e, st=st, sp=sp, xb=xb: e.matmul(
                    P[LNB[0]][:], ones128[:], xb, start=st, stop=sp),
                    waits=wl if c == 0 else [])
                b.emit("tensor", lambda e, st=st, sp=sp, xq=xq: e.matmul(
                    P[LNB[1]][:], ones128[:], xq, start=st, stop=sp),
                    incs=[(pe, 1)] if sp else [])
            return b.n(pe), ac_casts

        def ln_rows(slot, pe_done):
            bcmr = FBUF["bcmr"]; bcrs = FBUF["bcrs"]
            bm = bcmr[:, slot * TC:(slot + 1) * TC]
            br = bcrs[:, slot * TC:(slot + 1) * TC]
            b.emit("vector", lambda e, bm=bm: e.tensor_scalar(bm, P[LNB[0]][:], 1.0 / C, None, ALU.mult),
                   waits=[(pe, pe_done), (po, b.n(po))], incs=[(ve, 1), bank_rel(LNB[0])])
            g = b.n(ve)
            b.emit("vector", lambda e, bm=bm, br=br: e.tensor_tensor(br, bm, bm, ALU.mult),
                   waits=[(ve, g)], incs=[(ve, 1)])
            g = b.n(ve)
            b.emit("vector", lambda e, br=br: e.scalar_tensor_tensor(
                br, P[LNB[1]][:], 1.0 / C, br, ALU.mult, ALU.subtract),
                waits=[(ve, g)], incs=[(ve, 1), bank_rel(LNB[1])])
            ve_var = b.n(ve)
            b.emit("scalar", lambda e, br=br: e.activation(br, br, AF.Ln, bias=epsb[:, 0:1]),
                   waits=[(ve, ve_var)], incs=[(ac, 1)])
            g = b.n(ac)
            b.emit("scalar", lambda e, br=br: e.activation(br, br, AF.Exp, scale=-0.5),
                   waits=[(ac, g)], incs=[(ac, 1)])
            ac_rstd = b.n(ac)
            b.emit("vector", lambda e, bm=bm, br=br: e.scalar_tensor_tensor(
                bm, bm, -1.0, br, ALU.mult, ALU.mult),
                waits=[(ac, ac_rstd)], incs=[(ve, 1)])

        def ln_stats(x_view, gbase, slot, x_ready):
            ln_rows(slot, ln_sums(x_view, slot, x_ready)[0])

        def ln_apply(x_view, xo_view, slot, x_ready):
            tmpb = FBUF["tmpb"]; tmpb2 = FBUF["tmpb2"]
            bm = FBUF["bcmr"][:, slot * TC:(slot + 1) * TC]
            br = FBUF["bcrs"][:, slot * TC:(slot + 1) * TC]
            ready_v = b.n(ve)
            for c in range(NCH):
                eng = "vector" if c < 4 else "gpsimd"
                tb = tmpb[:, (c % 2) * TC:(c % 2 + 1) * TC] if c < 4 else tmpb2[:, 0:TC]
                sm = ve if c < 4 else po
                wl = [(ve, ready_v)] + list(x_ready) + [(pe, b.n(pe))] if c in (0, 4) else []
                if c >= 4:
                    wl = wl + [(po, b.n(po))]
                b.emit(eng, lambda e, c=c, tb=tb: e.tensor_tensor(tb, x_view(c), br, ALU.mult),
                       waits=wl, incs=[(sm, 1)])
                b.emit(eng, lambda e, c=c, tb=tb: e.tensor_tensor(xo_view(c), tb, bm, ALU.add),
                       waits=[(po, b.n(po))] if c >= 4 else [], incs=[(sm, 1)])
            return b.n(ve), b.n(po)

        def gating_topk(xln_v, half_base):
            scsig = FBUF["scsig"]; tks = FBUF["tks"]; tkm = FBUF["tkm"]
            tkmx = FBUF["tkmx"]; tkden = FBUF["tkden"]
            comb_tm = FBUF["comb_tm"]; comb_fm = FBUF["comb_fm"]
            for j in range(16):
                tcq, tt = j // 4, j % 4
                bk = j % 2
                wl = [bank_w(bk)]
                for c in range(NCH):
                    st, sp = (c == 0), (c == NCH - 1)
                    b.emit("tensor", lambda e, c=c, bk=bk, st=st, sp=sp, tcq=tcq, tt=tt: e.matmul(
                        P[bk][0:128, 0:16],
                        xln_v(c, tcq)[:, tt * 128:(tt + 1) * 128],
                        gwT_sb[:, c * NE:(c + 1) * NE], start=st, stop=sp),
                        waits=wl if c == 0 else [], incs=[(pe, 1)] if sp else [])
                g = b.n(pe)
                b.emit("scalar", lambda e, j=j, bk=bk: e.activation(
                    scsig[:, j * NE:(j + 1) * NE], P[bk][0:128, 0:16], AF.Sigmoid),
                    waits=[(pe, g)], incs=[(ac, 1), bank_rel(bk)])
            ac_sig = b.n(ac)
            v3 = lambda buf: buf.rearrange("p (j e) -> p j e", e=NE)
            mx3 = tkmx.rearrange("p (j e) -> p j e", e=1)
            mx3b = mx3.broadcast_to([128, 16, NE])
            dn3 = tkden.rearrange("p (j e) -> p j e", e=1)
            dn3b = dn3.broadcast_to([128, 16, NE])
            b.emit("vector", lambda e: e.tensor_copy(tks, scsig),
                   waits=[(ac, ac_sig)], incs=[(ve, 1)])
            for _ in range(3):
                b.emit("vector", lambda e: e.tensor_reduce(mx3, v3(tks), AX.X, ALU.max),
                       incs=[(ve, 1)])
                g = b.n(ve)
                b.emit("vector", lambda e: e.tensor_tensor(v3(tkm), v3(tks), mx3b, ALU.is_ge),
                       waits=[(ve, g)], incs=[(ve, 1)])
                b.emit("vector", lambda e: e.scalar_tensor_tensor(
                    tks, tkm, -1e30, tks, ALU.mult, ALU.add), incs=[(ve, 1)])
            b.emit("vector", lambda e: e.tensor_reduce(mx3, v3(tks), AX.X, ALU.max),
                   incs=[(ve, 1)])
            g = b.n(ve)
            b.emit("vector", lambda e: e.tensor_tensor(v3(tkm), v3(scsig), mx3b, ALU.is_ge),
                   waits=[(ve, g)], incs=[(ve, 1)])
            b.emit("vector", lambda e: e.tensor_tensor(comb_tm, scsig, tkm, ALU.mult),
                   incs=[(ve, 1)])
            b.emit("vector", lambda e: e.tensor_reduce(dn3, v3(comb_tm), AX.X, ALU.add),
                   incs=[(ve, 1)])
            g = b.n(ve)
            b.emit("vector", lambda e: e.tensor_scalar(tkden, tkden, 1e-20, None, ALU.add),
                   waits=[(ve, g)], incs=[(ve, 1)])
            g = b.n(ve)
            b.emit("vector", lambda e: e.reciprocal(tkden, tkden),
                   waits=[(ve, g)], incs=[(ve, 1)])
            g = b.n(ve)
            b.emit("vector", lambda e: e.tensor_tensor(v3(comb_tm), v3(comb_tm), dn3b, ALU.mult),
                   waits=[(ve, g)], incs=[(ve, 1)])
            b.emit("vector", lambda e: e.tensor_scalar(comb_tm, comb_tm, 0.5, None, ALU.mult),
                   incs=[(ve, 1)])
            ve_comb = b.n(ve)
            for j in range(16):
                bk = j % 2
                wl = [(ve, ve_comb), bank_w(bk)]
                if j == 0:
                    wl.append((dsc, b.n(dsc)))
                b.emit("tensor", lambda e, j=j, bk=bk: e.transpose(
                    P[bk][0:16, 0:128], comb_tm[:, j * NE:(j + 1) * NE], idn_f[:]),
                    waits=wl, incs=[(pe, 1)])
                g = b.n(pe)
                b.emit("vector", lambda e, j=j, bk=bk: e.tensor_copy(
                    comb_fm[:, j * 128:(j + 1) * 128], P[bk][0:16, 0:128]),
                    waits=[(pe, g)], incs=[(ve, 1), bank_rel(bk)])
            g = b.n(ve)
            b.emit("sync", lambda e, hb=half_base: e.dma_start(
                out=combD_d[:, hb:hb + TH], in_=comb_fm),
                waits=[(ve, g)], incs=[(dsc, 16)])
            return b.n(dsc)

        def expert_block(xln_v, x_v, cfg):
            n_i = cfg["n_i"]
            hbuf = FBUF["hbuf"]; sgb = FBUF["sgb"]; bccb = FBUF["bccb"]
            tmpb = FBUF["tmpb"]
            po_h_l = {}

            def emit_gu(tc):
                hslot = tc % 2
                if cfg["hmod"] == "comb":
                    ring = cfg["bc_state"]["n"] % 2
                    cfg["bc_state"]["n"] += 1
                    b.emit("sync", lambda e, ring=ring, tc=tc: e.dma_start(
                        out=bccb[:, ring * TC:(ring + 1) * TC],
                        in_=combD_d[cfg["e_idx"]:cfg["e_idx"] + 1,
                                    cfg["hb"] + tc * TC:cfg["hb"] + (tc + 1) * TC].broadcast_to([128, TC])),
                        waits=[(dsc, cfg["comb_at"]), (po, b.n(po)), (dsy, b.n(dsy))],
                        incs=[(dbc[ring], 16)])
                    bct = bccb[:, ring * TC:(ring + 1) * TC]
                    bc_ready = (dbc[ring], b.n(dbc[ring]))
                for i in range(n_i):
                    bkG, bkU = (0, 1) if i % 2 == 0 else (2, 3)
                    wl = [bank_w(bkG), cfg["wready"]]
                    for c in range(NCH):
                        st, sp = (c == 0), (c == NCH - 1)
                        b.emit("tensor", lambda e, c=c, i=i, tc=tc, bkG=bkG, st=st, sp=sp: e.matmul(
                            P[bkG][:], cfg["wg"](c, i), xln_v(c, tc), start=st, stop=sp),
                            waits=wl if c == 0 else [], incs=[(pe, 1)] if sp else [])
                    pe_g = b.n(pe)
                    wl2 = [bank_w(bkU)]
                    if "wready2" in cfg and tc == 0 and i == 0:
                        wl2.append(cfg["wready2"])
                    for c in range(NCH):
                        st, sp = (c == 0), (c == NCH - 1)
                        b.emit("tensor", lambda e, c=c, i=i, tc=tc, bkU=bkU, st=st, sp=sp: e.matmul(
                            P[bkU][:], cfg["wu"](c, i), xln_v(c, tc), start=st, stop=sp),
                            waits=wl2 if c == 0 else [], incs=[(pe, 1)] if sp else [])
                    pe_u = b.n(pe)
                    sgt = sgb[:, (i % 2) * TC:(i % 2 + 1) * TC]
                    tmt = tmpb[:, (i % 2) * TC:(i % 2 + 1) * TC]
                    hvt = hbuf[:, (hslot * 8 + i) * TC:(hslot * 8 + i + 1) * TC]
                    b.emit("scalar", lambda e, sgt=sgt, bkG=bkG: e.activation(sgt, P[bkG][:], AF.Silu),
                           waits=[(pe, pe_g), (ve, b.n(ve))], incs=[(ac, 1), bank_rel(bkG)])
                    ac_sg = b.n(ac)
                    b.emit("vector", lambda e, sgt=sgt, tmt=tmt, bkU=bkU: e.tensor_tensor(
                        tmt, sgt, P[bkU][:], ALU.mult),
                        waits=[(pe, pe_u), (ac, ac_sg), (po, b.n(po))],
                        incs=[(ve, 1), bank_rel(bkU)])
                    ve_tm = b.n(ve)
                    if cfg["hmod"] == "comb":
                        b.emit("gpsimd", lambda e, tmt=tmt, hvt=hvt, bct=bct: e.tensor_tensor(
                            hvt, tmt, bct, ALU.mult),
                            waits=[(ve, ve_tm), bc_ready], incs=[(po, 1)])
                    else:
                        b.emit("gpsimd", lambda e, tmt=tmt, hvt=hvt: e.tensor_scalar(
                            hvt, tmt, 0.5, None, ALU.mult),
                            waits=[(ve, ve_tm)], incs=[(po, 1)])
                po_h_l[tc] = b.n(po)

            def emit_down(tc):
                hslot = tc % 2
                for c in range(NCH):
                    bkD = 4 + (c % 2)
                    wl = [bank_w(bkD)]
                    if c == 0:
                        wl.append((po, po_h_l[tc]))
                    for i in range(n_i):
                        st, sp = (i == 0), (i == n_i - 1)
                        hvt = hbuf[:, (hslot * 8 + i) * TC:(hslot * 8 + i + 1) * TC]
                        b.emit("tensor", lambda e, i=i, c=c, bkD=bkD, st=st, sp=sp, hvt=hvt: e.matmul(
                            P[bkD][:], cfg["wd"](i, c), hvt, start=st, stop=sp),
                            waits=wl if i == 0 else [], incs=[(pe, 1)] if sp else [])
                    pe_d = b.n(pe)
                    b.emit("vector", lambda e, c=c, tc=tc, bkD=bkD: e.tensor_tensor(
                        x_v(c, tc), x_v(c, tc), P[bkD][:], ALU.add),
                        waits=[(pe, pe_d)], incs=[(ve, 1), bank_rel(bkD)])
                if cfg.get("on_tc_done"):
                    cfg["on_tc_done"](tc, b.n(ve))

            for tc in range(4):
                emit_gu(tc)
                if tc >= 1:
                    emit_down(tc - 1)
            emit_down(3)

        def emit_expert_loads(e_idx, wbuf, wsem, pe_gate):
            b.emit("gpsimd", lambda e, ei=e_idx, wbuf=wbuf: e.dma_start(
                out=wbuf[:, 0:4096].rearrange("p (n f) -> p n f", n=8),
                in_=weg_d[ei].rearrange("(n p) f -> p n f", p=128)),
                waits=[(pe, pe_gate)], incs=[(wsem, 16)])
            b.emit("gpsimd", lambda e, ei=e_idx, wbuf=wbuf: e.dma_start(
                out=wbuf[:, 4096:8192].rearrange("p (n f) -> p n f", n=8),
                in_=weu_d[ei].rearrange("(n p) f -> p n f", p=128)), incs=[(wsem, 16)])
            b.emit("gpsimd", lambda e, ei=e_idx, wbuf=wbuf: e.dma_start(
                out=wbuf[:, 8192:12288].rearrange("p (n f) -> p n f", n=4),
                in_=wed_d[ei].rearrange("(n p) f -> p n f", p=128)), incs=[(wsem, 16)])
            return b.n(wsem)

        def emit_sink(sink, hb, x_v, outst, ve_done, x_ready=(), sink_done=None):
            if sink == "park":
                for c in range(NCH):
                    b.emit("sync", lambda e, c=c, hb=hb: e.dma_start(
                        out=park_d[c * 128:(c + 1) * 128, hb:hb + TH],
                        in_=x_sb_ref[0][:, c * TH:(c + 1) * TH]),
                        waits=[(ve, ve_done)] if c == 0 else [], incs=[(dsy, 16)])
            else:
                for j in range(16):
                    tcq, tt = j // 4, j % 4
                    dsy_prev = b.n(dsy)
                    for c in range(NCH):
                        bk = c % 2
                        wl = [(ve, ve_done), bank_w(bk)] + list(x_ready)
                        if c == 0 and j >= 1:
                            wl.append((dsy, dsy_prev))
                        b.emit("tensor", lambda e, c=c, tcq=tcq, tt=tt, bk=bk: e.transpose(
                            P[bk][:, 0:128],
                            x_v(c, tcq)[:, tt * 128:(tt + 1) * 128], idn_f[:]),
                            waits=wl, incs=[(pe, 1)])
                        g = b.n(pe)
                        b.emit("vector", lambda e, c=c, bk=bk, outst=outst: e.tensor_copy(
                            outst[:, c * 128:(c + 1) * 128], P[bk][:, 0:128]),
                            waits=[(pe, g)], incs=[(ve, 1), bank_rel(bk)])
                    if sink_done is not None and tt == 3:
                        sink_done[tcq] = (pe, b.n(pe))
                    g = b.n(ve)
                    tglob = hb + j * 128
                    b.emit("sync", lambda e, tglob=tglob, outst=outst: e.dma_start(
                        out=out_d[tglob:tglob + 128, :],
                        in_=outst.rearrange("p (c f) -> p c f", c=8)),
                        waits=[(ve, g)], incs=[(dsy, 16)])

        def emit_sink_tc(hb, x_v, tc, ve_gate, sink_done):
            for c in range(NCH):
                b.emit("sync", lambda e, c=c, tc=tc, hb=hb: e.dma_start(
                    out=park_d[c * 128:(c + 1) * 128, hb + tc * TC: hb + (tc + 1) * TC],
                    in_=x_v(c, tc)),
                    waits=[(ve, ve_gate)] if c == 0 else [], incs=[(dsy, 16)])
            sink_done[tc] = (dsy, b.n(dsy))

        x_sb_ref = [None]

        def ffn_call(src, sink, compute=True):
            with ExitStack() as ps:
                NAMEC[0] += 1
                _uf = str(NAMEC[0])
                x_sb = ps.enter_context(nc.sbuf_tensor("x_ffn" + _uf, [128, NCH * TH], F32))
                xln_sb = ps.enter_context(nc.sbuf_tensor("xln_ffn" + _uf, [128, NCH * TH], BF16))
                hbuf = ps.enter_context(nc.sbuf_tensor("hbuf" + _uf, [128, 2 * 8 * TC], BF16))
                sgb = ps.enter_context(nc.sbuf_tensor("sgb" + _uf, [128, 2 * TC], F32))
                bccb = ps.enter_context(nc.sbuf_tensor("bccb" + _uf, [128, 2 * TC], F32))
                _gsc = hbuf[:, 4096:8192].bitcast(F32)
                scsig = _gsc[:, 0:256]
                tks = _gsc[:, 256:512]
                tkm = _gsc[:, 512:768]
                comb_tm = _gsc[:, 768:1024]
                tkmx = _gsc[:, 1024:1040]
                tkden = _gsc[:, 1040:1056]
                comb_fm = hbuf[:, 0:4096].bitcast(F32)[0:16, :]
                FBUF.update(dict(hbuf=hbuf, sgb=sgb, bccb=bccb, scsig=scsig, tks=tks,
                                 tkm=tkm, tkmx=tkmx, tkden=tkden, comb_tm=comb_tm,
                                 comb_fm=comb_fm))
                NAMEC[0] += 1
                _u = str(NAMEC[0])
                lnbufs = dict(
                    bcmr=ps.enter_context(nc.sbuf_tensor("bcmr" + _u, [128, 2 * TC], F32)),
                    bcrs=ps.enter_context(nc.sbuf_tensor("bcrs" + _u, [128, 2 * TC], F32)),
                    xbfc=ps.enter_context(nc.sbuf_tensor("xbfc" + _u, [128, 8 * TC], BF16)),
                    xsqc=ps.enter_context(nc.sbuf_tensor("xsqc" + _u, [128, 8 * TC], BF16)),
                    tmpb=ps.enter_context(nc.sbuf_tensor("tmpb" + _u, [128, 2 * TC], F32)),
                    tmpb2=ps.enter_context(nc.sbuf_tensor("tmpb2" + _u, [128, TC], F32)),
                )
                FBUF.update(lnbufs)
                outst = hbuf[:, 0:2048].bitcast(F32)
                x_sb_ref[0] = x_sb
                x_v = lambda c, tc: x_sb[:, c * TH + tc * TC: c * TH + (tc + 1) * TC]
                xln_v = lambda c, tc: xln_sb[:, c * TH + tc * TC: c * TH + (tc + 1) * TC]
                src_d = xT_d if src == "input" else park_d

                sink_done = {}
                next_ready = {}

                def emit_half1_loads(tc, eng="sync"):
                    for c in range(NCH):
                        wl = [sink_done[tc]] if c == 0 else []
                        b.emit(eng, lambda e, c=c, tc=tc: e.dma_start(
                            out=x_sb[:, c * TH + tc * TC: c * TH + (tc + 1) * TC],
                            in_=src_d[c * 128:(c + 1) * 128, TH + tc * TC: TH + (tc + 1) * TC]),
                            waits=wl, incs=[(dxl[tc], 16)])
                    next_ready[tc] = (dxl[tc], b.n(dxl[tc]))

                for half in range(2):
                    hb = half * TH
                    x_ready_tc = {}
                    for tc in range(4):
                        if half == 1 and compute:
                            x_ready_tc[tc] = [next_ready[tc]]
                            continue
                        for c in range(NCH):
                            eng = "sync" if c < 4 else "scalar"
                            wl = allsync() + [(dsy, b.n(dsy)), (dxw, b.n(dxw))] + init_ready \
                                if (tc == 0 and c in (0, 4)) else []
                            b.emit(eng, lambda e, c=c, tc=tc, hb=hb: e.dma_start(
                                out=x_sb[:, c * TH + tc * TC: c * TH + (tc + 1) * TC],
                                in_=src_d[c * 128:(c + 1) * 128, hb + tc * TC: hb + (tc + 1) * TC]),
                                waits=wl, incs=[(dxl[tc], 16)])
                        x_ready_tc[tc] = [(dxl[tc], b.n(dxl[tc]))]
                    x_ready = [x_ready_tc[t][0] for t in range(4)]
                    if not compute:
                        emit_sink(sink, hb, x_v, outst, b.n(ve), x_ready)
                        continue
                    wA_at = emit_expert_loads(0, wbufA, dwA, b.n(pe))
                    wB_at = emit_expert_loads(1, wbufB, dwB, b.n(pe))
                    w_at = {0: wA_at, 1: wB_at}
                    for tc in range(4):
                        slot = tc % 2
                        ln_stats(lambda c, tc=tc: x_v(c, tc), hb + tc * TC, slot, x_ready_tc[tc])
                        ln_apply(lambda c, tc=tc: x_v(c, tc),
                                 lambda c, tc=tc: xln_v(c, tc), slot, x_ready_tc[tc])
                    b.emit("tensor", lambda e: e.matmul(
                        P[LNB[0]][0:1, 0:1], ones_b[:], ones_b[:], start=True, stop=True),
                        waits=[(ve, b.n(ve)), (po, b.n(po)), bank_w(LNB[0])], incs=[(pe, 1)])
                    g = b.n(pe)
                    b.emit("vector", lambda e, _tk=tkden: e.tensor_copy(_tk[:, 0:1], P[LNB[0]][0:128, 0:1]),
                           waits=[(pe, g)], incs=[(ve, 1), bank_rel(LNB[0])])
                    comb_at = gating_topk(xln_v, hb)
                    bc_state = {"n": 0}
                    for ei in range(NE):
                        wbuf = wbufA if ei % 2 == 0 else wbufB
                        wsem = dwA if ei % 2 == 0 else dwB
                        expert_block(xln_v, x_v, dict(
                            n_i=4, hmod="comb", e_idx=ei, hb=hb, comb_at=comb_at,
                            bc_state=bc_state, wready=(wsem, w_at[ei]),
                            wg=lambda cc, i, wbuf=wbuf: wbuf[:, cc * 512 + i * 128: cc * 512 + (i + 1) * 128],
                            wu=lambda cc, i, wbuf=wbuf: wbuf[:, 4096 + cc * 512 + i * 128: 4096 + cc * 512 + (i + 1) * 128],
                            wd=lambda i, cc, wbuf=wbuf: wbuf[:, 8192 + i * 1024 + cc * 128: 8192 + i * 1024 + (cc + 1) * 128],
                        ))
                        pe_done = b.n(pe)
                        if ei + 2 < NE:
                            tgt = ei + 2
                            w_at[tgt] = emit_expert_loads(
                                tgt, wbufA if tgt % 2 == 0 else wbufB,
                                dwA if tgt % 2 == 0 else dwB, pe_done)
                        elif ei == NE - 2:
                            b.emit("gpsimd", lambda e: e.dma_start(
                                out=wbufA[:, 0:8192].rearrange("p (n f) -> p n f", n=8),
                                in_=wsg_d[:].rearrange("(n p) f -> p n f", p=128)),
                                waits=[(pe, pe_done)], incs=[(dwA, 16)])
                            b.emit("gpsimd", lambda e: e.dma_start(
                                out=wbufA[:, 8192:12288].rearrange("p (n f) -> p n f", n=4),
                                in_=wsd_d[0:512].rearrange("(n p) f -> p n f", p=128)),
                                incs=[(dwA, 16)])
                            shA_at = b.n(dwA)
                        elif ei == NE - 1:
                            b.emit("gpsimd", lambda e: e.dma_start(
                                out=wbufB[:, 0:8192].rearrange("p (n f) -> p n f", n=8),
                                in_=wsu_d[:].rearrange("(n p) f -> p n f", p=128)),
                                waits=[(pe, pe_done)], incs=[(dwB, 16)])
                            b.emit("gpsimd", lambda e: e.dma_start(
                                out=wbufB[:, 8192:12288].rearrange("p (n f) -> p n f", n=4),
                                in_=wsd_d[512:1024].rearrange("(n p) f -> p n f", p=128)),
                                incs=[(dwB, 16)])
                            shB_at = b.n(dwB)

                    def wsd_sl(i, cc):
                        if i < 4:
                            return wbufA[:, 8192 + i * 1024 + cc * 128: 8192 + i * 1024 + (cc + 1) * 128]
                        return wbufB[:, 8192 + (i - 4) * 1024 + cc * 128: 8192 + (i - 4) * 1024 + (cc + 1) * 128]
                    expert_block(xln_v, x_v, dict(
                        n_i=8, hmod="half", wready=(dwA, shA_at), wready2=(dwB, shB_at),
                        wg=lambda cc, i: wbufA[:, cc * 1024 + i * 128: cc * 1024 + (i + 1) * 128],
                        wu=lambda cc, i: wbufB[:, cc * 1024 + i * 128: cc * 1024 + (i + 1) * 128],
                        wd=wsd_sl,
                        on_tc_done=lambda tc, veg, hb=hb, half=half: (
                            emit_sink_tc(hb, x_v, tc, veg, sink_done),
                            emit_half1_loads(tc) if half == 0 else None)
                        if sink == "park" else None,
                    ))
                    if sink == "out":
                        ve_done = b.n(ve)
                        emit_sink(sink, hb, x_v, outst, ve_done, sink_done=sink_done)
                        if half == 0:
                            for tc in range(4):
                                emit_half1_loads(tc, eng="gpsimd")


        def qkv_phase(K_sb):
            with ExitStack() as ps:
                xpc = ps.enter_context(nc.sbuf_tensor("xpc", [128, 2 * 8 * TC], F32))
                xlnc = ps.enter_context(nc.sbuf_tensor("xlnc", [128, 2 * 8 * TC], BF16))
                qst = ps.enter_context(nc.sbuf_tensor("qst", [128, 2 * TC], BF16))
                vst = ps.enter_context(nc.sbuf_tensor("vst", [128, 2 * 1024], BF16))
                bvbc = ps.enter_context(nc.sbuf_tensor("bvbc", [128, 1024], F32))
                NAMEC[0] += 1
                _u = str(NAMEC[0])
                lnbufs = dict(
                    bcmr=ps.enter_context(nc.sbuf_tensor("bcmr" + _u, [128, 2 * TC], F32)),
                    bcrs=ps.enter_context(nc.sbuf_tensor("bcrs" + _u, [128, 2 * TC], F32)),
                    xbfc=ps.enter_context(nc.sbuf_tensor("xbfc" + _u, [128, 8 * TC], BF16)),
                    xsqc=ps.enter_context(nc.sbuf_tensor("xsqc" + _u, [128, 8 * TC], BF16)),
                    tmpb=ps.enter_context(nc.sbuf_tensor("tmpb" + _u, [128, 2 * TC], F32)),
                    tmpb2=ps.enter_context(nc.sbuf_tensor("tmpb2" + _u, [128, TC], F32)),
                )
                FBUF.update(lnbufs)

                pe_gate = b.n(pe)
                b.emit("gpsimd", lambda e: e.dma_start(
                    out=wbufA[:].rearrange("p (n f) -> p n f", n=4),
                    in_=wat_d[0:512].rearrange("(n p) f -> p n f", p=128)),
                    waits=[(pe, pe_gate)], incs=[(dwA, 16)])
                b.emit("gpsimd", lambda e: e.dma_start(
                    out=wbufB[:].rearrange("p (n f) -> p n f", n=4),
                    in_=wat_d[512:1024].rearrange("(n p) f -> p n f", p=128)),
                    incs=[(dwB, 16)])
                wA_at, wB_at = b.n(dwA), b.n(dwB)
                b.emit("scalar", lambda e: e.dma_start(
                    out=bvbc[:], in_=bav_d[0:1, :].broadcast_to([128, C])), incs=[(dsc, 16)])
                bv_at = b.n(dsc)

                def wat_sl(c, m):
                    buf = wbufA if c < 4 else wbufB
                    cc = c % 4
                    return buf[:, cc * 3072 + m * 128: cc * 3072 + (m + 1) * 128]

                def wat_v(c, blk):
                    buf = wbufA if c < 4 else wbufB
                    cc = c % 4
                    return buf[:, cc * 3072 + 2048 + blk * 512: cc * 3072 + 2048 + (blk + 1) * 512]

                def qkv_load(tcg, wl):
                    ring = tcg % 2
                    gb = tcg * TC
                    b.emit("gpsimd", lambda e, gb=gb, ring=ring: e.dma_start(
                        out=xpc[:, ring * 8 * TC: ring * 8 * TC + 4 * TC].rearrange("p (c f) -> p c f", c=4),
                        in_=park_d[0:512, gb:gb + TC].rearrange("(c p) f -> p c f", p=128)),
                        waits=wl, incs=[(dxc[ring], 16)])
                    b.emit("sync", lambda e, gb=gb, ring=ring: e.dma_start(
                        out=xpc[:, ring * 8 * TC + 4 * TC:(ring + 1) * 8 * TC].rearrange("p (c f) -> p c f", c=4),
                        in_=park_d[512:1024, gb:gb + TC].rearrange("(c p) f -> p c f", p=128)),
                        waits=wl, incs=[(dxc[ring], 16)])
                    return [(dxc[ring], b.n(dxc[ring]))]

                def qkv_body(tcg, x_ready):
                    ring = tcg % 2
                    gb = tcg * TC
                    xv = lambda c, ring=ring: xpc[:, (ring * 8 + c) * TC:(ring * 8 + c + 1) * TC]
                    xlv = lambda c, ring=ring: xlnc[:, (ring * 8 + c) * TC:(ring * 8 + c + 1) * TC]
                    apply_gates[tcg] = ln_apply(xv, xlv, tcg % 2, x_ready)
                    po_ln, ve_ln = b.n(po), b.n(ve)
                    for m in range(16):
                        bk = m % 2
                        wl = [bank_w(bk), (dwA, wA_at), (dwB, wB_at)]
                        if m == 0:
                            wl += [(ve, ve_ln), (po, po_ln)]
                        for c in range(NCH):
                            st, sp = (c == 0), (c == NCH - 1)
                            b.emit("tensor", lambda e, c=c, m=m, bk=bk, st=st, sp=sp, xlv=xlv: e.matmul(
                                P[bk][:], wat_sl(c, m), xlv(c), start=st, stop=sp),
                                waits=wl if c == 0 else [], incs=[(pe, 1)] if sp else [])
                        pe_m = b.n(pe)
                        if m < 8:
                            qslot = m % 2
                            b.emit("scalar", lambda e, m=m, bk=bk, qslot=qslot: e.add(
                                qst[:, qslot * TC:(qslot + 1) * TC], P[bk][:],
                                batqk_sb[:, m:m + 1]),
                                waits=[(pe, pe_m), (dqs[qslot], b.n(dqs[qslot]))],
                                incs=[(ac, 1), bank_rel(bk)])
                            g = b.n(ac)
                            b.emit("sync", lambda e, m=m, gb=gb, qslot=qslot: e.dma_start(
                                out=qsp_d[m * 128:(m + 1) * 128, gb:gb + TC],
                                in_=qst[:, qslot * TC:(qslot + 1) * TC]),
                                waits=[(ac, g)], incs=[(dqs[qslot], 16)])
                        else:
                            mk = m - 8
                            b.emit("scalar", lambda e, mk=mk, bk=bk, gb=gb: e.add(
                                K_sb[:, mk * T + gb: mk * T + gb + TC], P[bk][:],
                                batqk_sb[:, 8 + mk:9 + mk]),
                                waits=[(pe, pe_m)], incs=[(ac, 1), bank_rel(bk)])
                    for tt in range(4):
                        vslot = tt % 2
                        for blk in range(2):
                            bk = 4 + blk
                            wl = [bank_w(bk)]
                            for c in range(NCH):
                                st, sp = (c == 0), (c == NCH - 1)
                                b.emit("tensor", lambda e, c=c, tt=tt, blk=blk, bk=bk, st=st, sp=sp, xlv=xlv: e.matmul(
                                    P[bk][:], xlv(c)[:, tt * 128:(tt + 1) * 128], wat_v(c, blk),
                                    start=st, stop=sp),
                                    waits=wl if c == 0 else [], incs=[(pe, 1)] if sp else [])
                            pe_v = b.n(pe)
                            b.emit("vector", lambda e, blk=blk, bk=bk, vslot=vslot: e.tensor_tensor(
                                vst[:, vslot * 1024 + blk * 512: vslot * 1024 + (blk + 1) * 512],
                                P[bk][:], bvbc[:, blk * 512:(blk + 1) * 512], ALU.add),
                                waits=[(pe, pe_v), (dsc, bv_at), (dvs[vslot], b.n(dvs[vslot]))],
                                incs=[(ve, 1), bank_rel(bk)])
                        g = b.n(ve)
                        trow = gb + tt * 128
                        for pp in range(4):
                            b.emit("sync", lambda e, pp=pp, trow=trow, vslot=vslot: e.dma_start(
                                out=vsp_d[pp, trow:trow + 128, :],
                                in_=vst[:, vslot * 1024 + pp * 256: vslot * 1024 + (pp + 1) * 256]),
                                waits=[(ve, g)] if pp == 0 else [], incs=[(dvs[vslot], 16)])

                xrdy = {}
                apply_gates = {}
                cast_gates = {}
                xrdy[0] = qkv_load(0, [(dsy, b.n(dsy)), (pe, b.n(pe)), (po, b.n(po)),
                                       (ac, b.n(ac)), (ve, b.n(ve))])
                xrdy[1] = qkv_load(1, [(dsy, b.n(dsy)), (pe, b.n(pe)), (po, b.n(po)),
                                       (ac, b.n(ac)), (ve, b.n(ve))])
                for i in range(8):
                    xv = lambda c, i=i: xpc[:, ((i % 2) * 8 + c) * TC:((i % 2) * 8 + c + 1) * TC]
                    pe_done, cast_gates[i] = ln_sums(xv, i % 2, xrdy[i])
                    if i >= 1:
                        qkv_body(i - 1, xrdy[i - 1])
                        if i + 1 <= 7:
                            gv, gp = apply_gates[i - 1]
                            xrdy[i + 1] = qkv_load(i + 1, [
                                (ac, cast_gates[i - 1]), (ve, gv), (po, gp)])
                    ln_rows(i % 2, pe_done)
                qkv_body(7, xrdy[7])

        def attention_phase():
            with ExitStack() as ps:
                K_sb = ps.enter_context(nc.sbuf_tensor("K_sb", [128, NCH * T], BF16))
                qkv_phase(K_sb)
                cmask_sb = ps.enter_context(nc.sbuf_tensor("cmask_sb", [128, 4 * 512], BF16))
                b.emit("gpsimd", lambda e: e.dma_start(
                    out=cmask_sb[:].rearrange("p (a f) -> p a f", a=4),
                    in_=cmask_d[:].rearrange("(a p) f -> p a f", p=128)),
                    waits=[(ve, b.n(ve)), (ac, b.n(ac)), (po, b.n(po))], incs=[(dsy, 16)])
                cmask_at = b.n(dsy)
                ac_qkv = b.n(ac)
                pe_gate = b.n(pe)
                b.emit("gpsimd", lambda e: e.dma_start(
                    out=wbufA[:, 0:8192].rearrange("p (n f) -> p n f", n=8),
                    in_=wproj_d[:].rearrange("(n p) f -> p n f", p=128)),
                    waits=[(pe, pe_gate)], incs=[(dwA, 16)])
                wp_at = b.n(dwA)

                pb_sb = ps.enter_context(nc.sbuf_tensor("pb_sb", [128, 2 * T], BF16))
                pT_sb = ps.enter_context(nc.sbuf_tensor("pT_sb", [128, 32 * 256], BF16))
                av_sb = ps.enter_context(nc.sbuf_tensor("av_sb", [128, 8 * 256], BF16))
                vcol = ps.enter_context(nc.sbuf_tensor("vcol", [128, 6 * 8 * 256], BF16))
                q_sb = ps.enter_context(nc.sbuf_tensor("q_sb", [128, 2 * 8 * 128], BF16))
                xg_sb = ps.enter_context(nc.sbuf_tensor("xg_sb", [128, 8 * 256], F32))
                lsum = ps.enter_context(nc.sbuf_tensor("lsum", [128, 4], F32))
                mrow_t = ps.enter_context(nc.sbuf_tensor("mrow_t", [128, 2], F32))
                mxp = ps.enter_context(nc.sbuf_tensor("mxp", [128, 16], F32))
                lsum2 = ps.enter_context(nc.sbuf_tensor("lsum2", [128, 2], F32))

                qkv_dsy = b.n(dsy)
                qkv_spills = [(dqs[0], b.n(dqs[0])), (dqs[1], b.n(dqs[1])),
                              (dvs[0], b.n(dvs[0])), (dvs[1], b.n(dvs[1]))]

                q_at = {}

                def load_q(t):
                    ring = t % 2
                    b.emit("sync", lambda e, t=t, ring=ring: e.dma_start(
                        out=q_sb[:, ring * 1024:(ring + 1) * 1024].rearrange("p (c f) -> p c f", c=8),
                        in_=qsp_d[:, t * 128:(t + 1) * 128].rearrange("(c p) f -> p c f", p=128)),
                        waits=[(dsy, qkv_dsy), (pe, b.n(pe))] + qkv_spills,
                        incs=[(dqt[ring], 16)])
                    q_at[t] = b.n(dqt[ring])

                av_kk = [0]

                def load_vcol(pair, s0, cnt, sl, gate):
                    b.emit("sync", lambda e, pair=pair, s0=s0, cnt=cnt, sl=sl: e.dma_start(
                        out=vcol[:, sl * 2048: sl * 2048 + cnt * 256].rearrange(
                            "p (j f) -> p j f", f=256),
                        in_=vsp_d[pair, s0 * 128:(s0 + cnt) * 128, :].rearrange("(j p) f -> p j f", p=128)),
                        waits=[(dsy, qkv_dsy), (pe, gate)] + qkv_spills, incs=[(dvc[sl], 16)])
                    return b.n(dvc[sl])

                load_q(0)
                load_q(1)

                for g_i in range(16):
                    smax = 2 * g_i + 2
                    ve_pb_t = {}
                    for ti in range(2):
                        t = 2 * g_i + ti
                        ring = t % 2
                        sring = t % 2
                        nb = t // 4 + 1
                        reg = t % 2
                        nlo = min(nb, 4)
                        bmap = {}
                        for bi in range(nb):
                            tens = reg if bi < 4 else 1 - reg
                            bmap[bi] = tens * 4 + (bi % 4)
                        for bi in range(nb):
                            bk = bmap[bi]
                            wl = [bank_w(bk), (dqt[ring], q_at[t]), (ac, ac_qkv)]
                            for c in range(NCH):
                                st, sp = (c == 0), (c == NCH - 1)
                                b.emit("tensor", lambda e, c=c, bi=bi, bk=bk, st=st, sp=sp, ring=ring: e.matmul(
                                    P[bk],
                                    q_sb[:, ring * 1024 + c * 128: ring * 1024 + (c + 1) * 128],
                                    K_sb[:, c * T + bi * 512: c * T + (bi + 1) * 512],
                                    start=st, stop=sp),
                                    waits=wl if c == 0 else [], incs=[(pe, 1)] if sp else [])
                            pe_blk = b.n(pe)
                            if bi == nb - 1:
                                al = t % 4
                                b.emit("vector", lambda e, bk=bk, al=al: e.tensor_tensor(
                                    P[bk], P[bk], cmask_sb[:, al * 512:(al + 1) * 512], ALU.add),
                                    waits=[(pe, pe_blk), (dsy, cmask_at)], incs=[(ve, 1)])
                            b.emit("vector", lambda e, bk=bk, bi=bi, sring=sring: e.tensor_reduce(
                                mxp[:, sring * 8 + bi: sring * 8 + bi + 1],
                                P[bk], AX.X, ALU.max),
                                waits=[(pe, pe_blk), (ve, b.n(ve))], incs=[(ve, 1)])
                        ve_evac = b.n(ve)
                        b.emit("vector", lambda e, sring=sring, nb=nb: e.tensor_reduce(
                            mrow_t[:, sring:sring + 1], mxp[:, sring * 8: sring * 8 + nb],
                            AX.X, ALU.max),
                            waits=[(ve, ve_evac), (ac, b.n(ac))], incs=[(ve, 1)])
                        g1 = b.n(ve)
                        b.emit("vector", lambda e, sring=sring: e.tensor_scalar(
                            mrow_t[:, sring:sring + 1], mrow_t[:, sring:sring + 1],
                            -1.0 / 32.0, None, ALU.mult), waits=[(ve, g1)], incs=[(ve, 1)])
                        g2 = b.n(ve)
                        rels1 = [bank_rel(bmap[bi]) for bi in range(nlo)]
                        b.emit("scalar", lambda e, sring=sring, nlo=nlo, reg=reg: e.activation(
                            pb_sb[:, sring * T: sring * T + nlo * 512],
                            PFULL[reg][:, 0:nlo * 512], AF.Exp,
                            bias=mrow_t[:, sring:sring + 1], scale=1.0 / 32.0,
                            accum_out=lsum[:, sring:sring + 1]),
                            waits=[(ve, g2), (pe, b.n(pe))], incs=[(ac, 1)] + rels1)
                        if nb > 4:
                            nhi = nb - 4
                            rels2 = [bank_rel(bmap[bi]) for bi in range(4, nb)]
                            b.emit("scalar", lambda e, sring=sring, nhi=nhi, reg=reg: e.activation(
                                pb_sb[:, sring * T + 2048: sring * T + 2048 + nhi * 512],
                                PFULL[1 - reg][:, 0:nhi * 512], AF.Exp,
                                bias=mrow_t[:, sring:sring + 1], scale=1.0 / 32.0,
                                accum_out=lsum2[:, sring:sring + 1]),
                                incs=[(ac, 1)] + rels2)
                            g3b = b.n(ac)
                            b.emit("vector", lambda e, sring=sring: e.tensor_tensor(
                                lsum[:, sring:sring + 1], lsum[:, sring:sring + 1],
                                lsum2[:, sring:sring + 1], ALU.add),
                                waits=[(ac, g3b)], incs=[(ve, 1)])
                        g3 = b.n(ac)
                        g3v = b.n(ve)
                        b.emit("vector", lambda e, sring=sring: e.reciprocal(
                            lsum[:, 2 + sring:3 + sring], lsum[:, sring:sring + 1]),
                            waits=[(ac, g3), (ve, g3v)], incs=[(ve, 1)])
                        g4 = b.n(ve)
                        b.emit("vector", lambda e, sring=sring, nb=nb: e.tensor_scalar(
                            pb_sb[:, sring * T: sring * T + nb * 512],
                            pb_sb[:, sring * T: sring * T + nb * 512],
                            lsum[:, 2 + sring:3 + sring], None, ALU.mult),
                            waits=[(ve, g4)], incs=[(ve, 1)])
                        ve_pb_t[t] = b.n(ve)
                        nfill = smax - (t + 1)
                        if nfill > 0:
                            b.emit("vector", lambda e, t=t, ti=ti, smax=smax: e.memset(
                                pT_sb[:].rearrange("p (j q) -> p j q", q=256)[
                                    :, t + 1:smax, ti * 128:(ti + 1) * 128], 0.0),
                                incs=[(ve, 1)])
                    for ti in range(2):
                        t = 2 * g_i + ti
                        sring = t % 2
                        for j in range(t + 1):
                            bk = 6 + (j % 2)
                            b.emit("tensor", lambda e, j=j, bk=bk, sring=sring: e.transpose(
                                P[bk][:].bitcast(BF16)[:, 0:128],
                                pb_sb[:, sring * T + j * 128: sring * T + (j + 1) * 128],
                                idn_b[:]),
                                waits=[(ve, ve_pb_t[t]), bank_w(bk)], incs=[(pe, 1)])
                            g5 = b.n(pe)
                            b.emit("scalar", lambda e, j=j, bk=bk, ti=ti: e.copy(
                                pT_sb[:, j * 256 + ti * 128: j * 256 + (ti + 1) * 128],
                                P[bk][:].bitcast(BF16)[:, 0:128]),
                                waits=[(pe, g5)], incs=[(ac, 1), bank_rel(bk)])
                        if t + 2 < 32:
                            load_q(t + 2)
                    ac_tr = b.n(ac)
                    ve_fill = b.n(ve)
                    pieces = [(p0, min(8, smax - p0)) for p0 in range(0, smax, 8)]
                    npc = len(pieces)
                    slot_of = {}
                    v_at = {}
                    pe_after_pair = {}

                    def emit_vload(pair, pi, gate):
                        s0, cnt = pieces[pi]
                        sl = av_kk[0] % 6
                        av_kk[0] += 1
                        slot_of[(pair, pi)] = sl
                        v_at[(pair, pi)] = load_vcol(pair, s0, cnt, sl, gate)

                    for pi in range(npc):
                        emit_vload(0, pi, b.n(pe))
                    for pair in range(4):
                        gate0 = b.n(pe)
                        if pair + 1 < 4:
                            for pi in range(min(npc, 6 - npc)):
                                emit_vload(pair + 1, pi, gate0)
                        wl0 = [(ac, ac_tr), (ve, ve_fill)]
                        for pi in range(npc):
                            wl0.append((dvc[slot_of[(pair, pi)]], v_at[(pair, pi)]))
                        for half in range(2):
                            cb = pair * 2 + half
                            bk = 4 + (cb % 2)
                            wl = wl0 + [bank_w(bk)] if half == 0 else [bank_w(bk)]
                            for pi, (s0, cnt) in enumerate(pieces):
                                sl = slot_of[(pair, pi)]
                                for j in range(cnt):
                                    st = (pi == 0 and j == 0)
                                    sp = (pi == npc - 1 and j == cnt - 1)
                                    sj = s0 + j
                                    b.emit("tensor", lambda e, j=j, sj=sj, bk=bk, sl=sl, half=half, st=st, sp=sp: e.matmul(
                                        P[bk][:, 0:256],
                                        vcol[:, sl * 2048 + j * 256 + half * 128: sl * 2048 + j * 256 + half * 128 + 128],
                                        pT_sb[:, sj * 256:(sj + 1) * 256], start=st, stop=sp),
                                        waits=wl if (pi == 0 and j == 0) else [],
                                        incs=[(pe, 1)] if sp else [])
                            pe_av = b.n(pe)
                            b.emit("scalar", lambda e, cb=cb, bk=bk: e.copy(
                                av_sb[:, cb * 256:(cb + 1) * 256], P[bk][:, 0:256]),
                                waits=[(pe, pe_av)], incs=[(ac, 1), bank_rel(bk)])
                        if pair + 1 < 4:
                            for pi in range(min(npc, 6 - npc), npc):
                                emit_vload(pair + 1, pi, b.n(pe))
                    ac_av = b.n(ac)
                    xring = 0
                    b.emit("gpsimd", lambda e, g_i=g_i: e.dma_start(
                        out=xg_sb[:].rearrange("p (c f) -> p c f", c=8),
                        in_=park_d[:, g_i * 256:(g_i + 1) * 256].rearrange("(c p) f -> p c f", p=128)),
                        waits=[(dsy, qkv_dsy), (dxw, b.n(dxw)), (ve, b.n(ve))], incs=[(dxg[0], 16)])
                    x_at = b.n(dxg[0])
                    wp_sl = lambda cc, m: wbufA[:, cc * 1024 + m * 128: cc * 1024 + (m + 1) * 128]
                    for m in range(NCH):
                        bk = 4 + (m % 2)
                        wl = [bank_w(bk), (dwA, wp_at), (ac, ac_av)]
                        if m == 0 and "proj" in ser:
                            wl += allsync()
                        for c in range(NCH):
                            st, sp = (c == 0), (c == NCH - 1)
                            b.emit("tensor", lambda e, c=c, m=m, bk=bk, st=st, sp=sp: e.matmul(
                                P[bk][:, 0:256], wp_sl(c, m), av_sb[:, c * 256:(c + 1) * 256],
                                start=st, stop=sp),
                                waits=wl if c == 0 else [], incs=[(pe, 1)] if sp else [])
                        pe_pj = b.n(pe)
                        b.emit("vector", lambda e, m=m, bk=bk: e.scalar_tensor_tensor(
                            xg_sb[:, m * 256:(m + 1) * 256],
                            P[bk][:, 0:256], bproj_sb[:, m:m + 1],
                            xg_sb[:, m * 256:(m + 1) * 256],
                            ALU.add, ALU.add),
                            waits=[(pe, pe_pj), (dxg[0], x_at)],
                            incs=[(ve, 1), bank_rel(bk)])
                    ve_res = b.n(ve)
                    b.emit("gpsimd", lambda e, g_i=g_i: e.dma_start(
                        out=park_d[:, g_i * 256:(g_i + 1) * 256].rearrange("(c p) f -> p c f", p=128),
                        in_=xg_sb[:].rearrange("p (c f) -> p c f", c=8)),
                        waits=[(ve, ve_res)], incs=[(dxw, 16)])

        if stage == "ffn1":
            ffn_call("input", "out")
        elif stage == "attn":
            ffn_call("input", "park")
            attention_phase()
            ffn_call("park", "out", compute=False)
        else:
            ffn_call("input", "park")
            attention_phase()
            ffn_call("park", "out")

        b.emit("sync", lambda e: None,
               waits=[(dsy, b.n(dsy)), (dsc, b.n(dsc)), (dwA, b.n(dwA)), (dwB, b.n(dwB)),
                      (dxw, b.n(dxw)), (dvs[0], b.n(dvs[0])), (dvs[1], b.n(dvs[1])),
                      (dqs[0], b.n(dqs[0])), (dqs[1], b.n(dqs[1]))])

        block = stack.enter_context(nc.Block())
        b.finish(block)
    return nc


_NC_CACHE = None


def kernel(L0_attn_x, L0_attn_w_attn, L0_attn_b_attn, L0_attn_w_proj, L0_attn_b_proj,
           L0_ffn_gate_w, L0_ffn_expert_gate, L0_ffn_expert_up, L0_ffn_expert_down,
           L0_ffn_shared_gate, L0_ffn_shared_up, L0_ffn_shared_down):
    global _NC_CACHE
    x = np.asarray(L0_attn_x, np.float32)
    B = x.shape[0]
    wat = np.ascontiguousarray(np.asarray(L0_attn_w_attn, np.float32))
    b_attn = np.asarray(L0_attn_b_attn, np.float32)
    batqk = np.ascontiguousarray(b_attn[:2048].reshape(16, 128).T)
    bav = np.ascontiguousarray(b_attn[2048:].reshape(1, 1024))
    wproj = np.ascontiguousarray(np.asarray(L0_attn_w_proj, np.float32))
    bproj = np.ascontiguousarray(np.asarray(L0_attn_b_proj, np.float32).reshape(8, 128).T)
    gwT = np.ascontiguousarray(np.asarray(L0_ffn_gate_w, np.float32).T)
    weg = np.ascontiguousarray(np.asarray(L0_ffn_expert_gate, np.float32))
    weu = np.ascontiguousarray(np.asarray(L0_ffn_expert_up, np.float32))
    wed = np.ascontiguousarray(np.asarray(L0_ffn_expert_down, np.float32))
    wsg = np.ascontiguousarray(np.asarray(L0_ffn_shared_gate, np.float32))
    wsu = np.ascontiguousarray(np.asarray(L0_ffn_shared_up, np.float32))
    wsd = np.ascontiguousarray(np.asarray(L0_ffn_shared_down, np.float32))

    p = np.arange(128)[:, None]
    sl = np.arange(512)[None, :]
    cmask = np.concatenate(
        [np.where(sl <= a * 128 + p, 0.0, -1e10).astype(np.float32) for a in range(4)], axis=0)
    idn = np.eye(128, dtype=np.float32)

    if _NC_CACHE is None:
        _NC_CACHE = build_program()
    nc = _NC_CACHE

    in_maps = []
    for bb in range(B):
        in_maps.append(dict(
            xT=np.ascontiguousarray(x[bb].T),
            wat=wat, batqk=batqk, bav=bav, wproj=wproj, bproj=bproj,
            gwT=gwT, weg=weg, weu=weu, wed=wed, wsg=wsg, wsu=wsu, wsd=wsd,
            cmask=cmask, idn=idn,
        ))
    res = run_bass_kernel_spmd(nc, in_maps, core_ids=list(range(B)))
    out = np.stack([np.asarray(res.results[bb]["out"]) for bb in range(B)], axis=0)
    if os.environ.get("KDUMP", "0") == "1":
        np.save("_dump_qsp.npy", np.asarray(res.results[0]["qsp"]).view(np.uint16))
        np.save("_dump_vsp.npy", np.asarray(res.results[0]["vsp"]).view(np.uint16))
        np.save("_dump_park.npy", np.asarray(res.results[0]["park"]))
    return out.astype(np.float32)
